# version 01
# baseline (speedup 1.0000x reference)
import math
import numpy as np
import ml_dtypes
import concourse.bass as bass
import concourse.mybir as mybir
from concourse.bass_utils import run_bass_kernel_spmd
from concourse.ordered_engine_set import OrderedEngineSet

F32 = mybir.dt.float32
BF16 = mybir.dt.bfloat16
AF = mybir.ActivationFunctionType
ALU = mybir.AluOpType
ET = mybir.EngineType
ts = bass.ts
ds = bass.ds
BF16NP = ml_dtypes.bfloat16
AX = mybir.AxisListType.X

NCORES = 8
D = 1024
NCH = 8
OWN = 2048
HALO = 384
SEGW = 512
PADW = 8
YW = PADW + OWN
NLAYER = 6
DI = 2730
DIP = 2816
NGC = 22
WINB = 384
KWINW = 896
NLAT = 64
NVSLOT = 6
EPS = 1e-5

STAGE = "full"
DEBUG_SNAP = False
ALL_ENG = [ET.PE, ET.DVE, ET.Activation, ET.Pool, ET.SP]

_OUTER_SPEC = [("wq", D * D), ("wk", D * D), ("wv", D * D),
               ("wgq", D * 512), ("wgk", D * 512), ("wgv", D * 512),
               ("r_out_w", 512 * D), ("w_out", D * D)]
_BLOCK_SPEC = [("Wqkv", D * 3 * D), ("Wo", D * D), ("W1a", D * DIP),
               ("W1g", D * DIP), ("W2p", DIP * D)]
WOFF = {}
_o = 0
for _nm, _sz in _OUTER_SPEC:
    WOFF[_nm] = _o
    _o += _sz
OB0 = _o
LOFF = {}
_b = 0
for _nm, _sz in _BLOCK_SPEC:
    LOFF[_nm] = _b
    _b += _sz
LBS = _b
WALLN = OB0 + (NLAYER + 1) * LBS
PE, DVE, A, PL, SP = ET.PE, ET.DVE, ET.Activation, ET.Pool, ET.SP


def _fm(a):
    return np.ascontiguousarray(a.T.reshape(NCH, 128, -1))


def _rope_tables(base):
    pos = np.clip(np.arange(base - HALO, base + OWN), 0, 8191).astype(np.float64)
    half = np.arange(0, 64, 2, dtype=np.float64)
    inv_freq = 1.0 / 10000.0 ** (half / 64.0)
    freqs = pos[:, None] * inv_freq[None, :]
    freqs = np.concatenate([freqs, freqs], axis=1)
    base_scale = (half + 0.4 * 64.0) / (1.4 * 64.0)
    power = (pos - 4096.0) / 128.0
    scale = base_scale[None, :] ** power[:, None]
    scale = np.concatenate([scale, scale], axis=1)
    r8 = math.sqrt(8.0)
    tabs = np.stack([np.cos(freqs) * scale * r8, np.sin(freqs) * scale * r8,
                     np.cos(freqs) / scale * r8, np.sin(freqs) / scale * r8])
    tabs = np.tile(tabs.transpose(0, 2, 1), (1, 2, 1))
    return np.ascontiguousarray(tabs.astype(np.float32))


def _win_masks(outer, first_core):
    r = np.arange(128)[:, None]
    w = np.arange(WINB)[None, :]
    lo = r + 1 if outer else r
    ok = (w >= lo) & (w <= r + 256)
    m = np.zeros((3, 128, WINB), np.float32)
    NEG = np.float32(-3e38)
    m[0] = np.where(ok, 0.0, NEG)
    if first_core:
        m[1] = np.where(ok & (w >= 256), 0.0, NEG)
        m[2] = np.where(ok & (w >= 128), 0.0, NEG)
    else:
        m[1] = m[0]
        m[2] = m[0]
    return m


def _lat_masks(base):
    m = np.zeros((16, 128, NLAT), np.float32)
    NEG = np.float32(-3e38)
    n = np.arange(NLAT)[None, :]
    for s in range(16):
        m[s] = np.where(n <= (base + 128 * s) // 128, 0.0, NEG)
    return m


def _rot_perm():
    P = np.zeros((128, 128), np.float32)
    for g in range(2):
        o = 64 * g
        for dd in range(32):
            P[o + dd + 32, o + dd] = -1.0
            P[o + dd, o + dd + 32] = 1.0
    return P


def _vec_t(v, cols):
    out = np.zeros(128 * cols, np.float64)
    out[:v.shape[0]] = v
    return np.ascontiguousarray(out.reshape(cols, 128).T.astype(np.float32))


def _prep_core_inputs(inputs, c):
    seq, sl = divmod(c, 4)
    base = OWN * sl
    x = np.asarray(inputs["L0_attn_x"][seq], np.float32)
    d = {}
    d["x_own"] = _fm(x[base:base + OWN])
    xh = np.zeros((HALO, D), np.float32)
    if base > 0:
        xh[:] = x[base - HALO:base]
    d["x_halo"] = _fm(xh)
    d["rope"] = _rope_tables(base).astype(BF16NP)
    d["mask_in"] = _win_masks(False, sl == 0).astype(BF16NP)
    d["mask_out"] = _win_masks(True, sl == 0).astype(BF16NP)
    d["mask_lat"] = _lat_masks(base).astype(BF16NP)
    return d


def _prep_shared(inputs):
    f32 = np.float32
    d = {}
    wd = {}
    for k in ("wq", "wk", "wv", "wgq", "wgk", "wgv", "r_out_w", "w_out"):
        wd[k] = np.asarray(inputs["L0_attn_" + k], f32)
    wd["Wqkv"] = np.asarray(inputs["L0_ffn_Wqkv"], f32)
    wd["Wo"] = np.asarray(inputs["L0_ffn_Wo"], f32)
    W1 = np.asarray(inputs["L0_ffn_W1"], f32)
    W1a = np.zeros((NLAYER, D, DIP), f32); W1a[:, :, :DI] = W1[:, :, :DI]
    W1g = np.zeros((NLAYER, D, DIP), f32); W1g[:, :, :DI] = W1[:, :, DI:]
    wd["W1a"], wd["W1g"] = W1a, W1g
    W2 = np.asarray(inputs["L0_ffn_W2"], f32)
    W2p = np.zeros((NLAYER, DIP, D), f32); W2p[:, :DI] = W2
    wd["W2p"] = W2p
    wall = np.zeros((1, WALLN), f32)
    for nm, sz in _OUTER_SPEC:
        wall[0, WOFF[nm]:WOFF[nm] + sz] = wd[nm].reshape(-1)
    for L in range(NLAYER):
        b0 = OB0 + L * LBS
        for nm, sz in _BLOCK_SPEC:
            wall[0, b0 + LOFF[nm]:b0 + LOFF[nm] + sz] = wd[nm][L].reshape(-1)
    d["wall"] = wall
    b1 = np.asarray(inputs["L0_ffn_b1"], f32)
    b1a = np.zeros((NLAYER, DIP), f32); b1a[:, :DI] = b1[:, :DI]
    b1g = np.zeros((NLAYER, DIP), f32); b1g[:, :DI] = b1[:, DI:]
    vecs = np.zeros((NLAYER, 128, 70), f32)
    for i in range(NLAYER):
        vecs[i, :, 0:8] = _vec_t(np.asarray(inputs["L0_ffn_ln_g"][i], f32), 8)
        vecs[i, :, 8:16] = _vec_t(np.asarray(inputs["L0_ffn_ln_b"][i], f32), 8)
        vecs[i, :, 16:24] = _vec_t(np.asarray(inputs["L0_ffn_b2"][i], f32), 8)
        vecs[i, :, 24:46] = _vec_t(b1a[i], NGC)
        vecs[i, :, 46:68] = _vec_t(b1g[i], NGC)
        vecs[i, :, 68] = np.tile(np.asarray(inputs["L0_ffn_q_scale"][i], f32), 2)
        vecs[i, :, 69] = np.tile(np.asarray(inputs["L0_ffn_k_scale"][i], f32), 2)
    d["vecsL"] = np.ascontiguousarray(vecs)
    glw = np.asarray(inputs["L0_attn_gate_local_w"], f32)
    grw = np.asarray(inputs["L0_attn_gate_remote_w"], f32)
    gB = np.zeros((128, NCH, 48), f32)
    for h in range(16):
        fc, half = divmod(h, 2)
        gB[64 * half:64 * half + 64, fc, h] = glw[h]
        gB[64 * half:64 * half + 64, fc, 32 + h] = grw[h]
    d["gateB"] = np.ascontiguousarray(gB.astype(BF16NP))
    d["ident"] = np.eye(128, dtype=f32).astype(BF16NP)
    d["perm"] = _rot_perm().astype(BF16NP)
    sstat = np.zeros((128, NCH, 16), f32)
    for fc in range(NCH):
        sstat[0:64, fc, 2 * fc] = 1.0
        sstat[64:128, fc, 2 * fc + 1] = 1.0
    d["sstat"] = np.ascontiguousarray(sstat.astype(BF16NP))
    e16 = np.zeros((16, NCH, 128), f32)
    for fc in range(NCH):
        e16[2 * fc, fc, 0:64] = 1.0
        e16[2 * fc + 1, fc, 64:128] = 1.0
    d["e16"] = np.ascontiguousarray(e16.astype(BF16NP))
    d["ones2"] = np.ones((128, 2), f32).astype(BF16NP)
    d["onesR"] = np.ones((1, 128), f32).astype(BF16NP)
    return d


class Sem:
    __slots__ = ("name", "h", "c")

    def __init__(self, name, h):
        self.name, self.h, self.c = name, h, 0


class MK:
    def __init__(self, nc, plan=None):
        self.nc = nc
        self.pass2 = plan is not None
        self.plan = plan if plan is not None else {"deltas": {}, "marks": {}}
        self.sems = {}
        self.loopstack = []
        self.scratch = {}
        self.lastwait = {}

    def sem(self, name):
        s = Sem(name, self.nc.alloc_semaphore(name=name))
        self.sems[name] = s
        return s

    def eng(self, e):
        nc = self.nc
        return {PE: nc.tensor, DVE: nc.vector, A: nc.scalar,
                PL: nc.gpsimd, SP: nc.sync}[e]

    def wreg(self, e, idx):
        key = (e, idx)
        if key not in self.scratch:
            self.scratch[key] = self.eng(e).alloc_register(f"wr{idx}")
        return self.scratch[key]

    def inc(self, inst, sem, n=1):
        inst.then_inc(sem.h, n)
        sem.c += n
        return inst

    def wait(self, e, sem, val=None):
        if val is None:
            val = sem.c
        key = (e, sem.name)
        ctx = tuple(lid for lid, _ in self.loopstack)
        prev = self.lastwait.get(key)
        if prev is not None and prev[0] == ctx and prev[1] >= val:
            return
        self.lastwait[key] = (ctx, val)
        eng = self.eng(e)
        terms = []
        for (lid, lv) in self.loopstack:
            dl = self.plan["deltas"].get((lid, sem.name), 0) if self.pass2 else 0
            if dl:
                terms.append((lv, dl))
        if not terms:
            eng.wait_ge(sem.h, max(val, 0))
            return
        r = self.wreg(e, 0)
        eng.reg_mul(r, terms[0][0], terms[0][1])
        for lv, dl in terms[1:]:
            r2 = self.wreg(e, 1)
            eng.reg_mul(r2, lv, dl)
            eng.reg_alu(r, r, r2, ALU.add)
        eng.reg_add(r, r, val)
        if val < 0:
            eng.reg_alu(r, r, 0, ALU.max)
        eng.wait_ge(sem.h, r)

    def loop(self, n, engines, tag):
        return _Loop(self, n, engines, tag)


class _Loop:
    def __init__(self, mk, n, engines, tag):
        self.mk, self.n, self.lid = mk, n, tag
        self.engines = OrderedEngineSet(engines)

    def __enter__(self):
        mk = self.mk
        self._starts = {nm: s.c for nm, s in mk.sems.items()}
        self._fori = mk.nc.Fori(0, self.n, engines=self.engines)
        self.var = self._fori.__enter__()
        mk.loopstack.append((self.lid, self.var))
        return self.var

    def __exit__(self, a_, b_, c_):
        mk = self.mk
        mk.loopstack.pop()
        mk.lastwait.clear()
        self._fori.__exit__(a_, b_, c_)
        for nm, s in mk.sems.items():
            dl = s.c - self._starts[nm]
            key = (self.lid, nm)
            if not mk.pass2:
                mk.plan["deltas"][key] = dl
            else:
                assert mk.plan["deltas"].get(key, 0) == dl, (self.lid, nm)
            s.c = self._starts[nm] + self.n * dl
        return False


class Mem:
    def __init__(self, nc):
        self.nc = nc
        self.base = (nc.sbuf_base + 31) // 32 * 32
        self.top = self.base
        self.offs = {}

    def alloc(self, name, shape, dtype, offset=None):
        esz = mybir.dt.size(dtype)
        freesz = int(np.prod(shape[1:])) * esz
        if offset is None:
            offset = self.top
            self.top = offset + ((freesz + 31) // 32) * 32
        h = self.nc.alloc_sbuf_tensor_at(name, list(shape), dtype, offset=offset)
        self.offs[name] = offset
        return h

    def finish(self):
        self.nc.sbuf_base = self.top

    def at(self, name):
        return self.offs[name]


class B:
    def __init__(self, nc, mk, stage):
        self.nc, self.mk, self.stage = nc, mk, stage
        self.n_layers = {"l2": 2, "l4": 4, "full": NLAYER}.get(stage, NLAYER)
        self._declare_dram()
        self._declare_sbuf()
        s = mk.sem
        self.sG, self.sS, self.sP, self.sA, self.sV, self.sL, self.sC = (
            s("sG"), s("sS"), s("sP"), s("sA"), s("sV"), s("sL"), s("sC"))
        self.sX = s("sX")
        self.sE = s("sE")
        self.sI = nc.alloc_semaphore(name="sI")
        self.t, self.v, self.a, self.g, self.s = (nc.tensor, nc.vector,
                                                  nc.scalar, nc.gpsimd, nc.sync)
        self.r_ipid = self.g.alloc_register("ipid")
        self.r_im4 = self.g.alloc_register("im4")
        self.r_ioff = self.g.alloc_register("ioff")
        self.r_ioff2 = self.g.alloc_register("ioff2")
        self.psF = nc.alloc_psum_tensor("psF", [128, 6, 512], F32)
        self.psB = nc.alloc_psum_tensor("psB", [128, 2, 1024], BF16)

    def _declare_dram(self):
        f = self.nc.dram_tensor
        self.x_own = f("x_own", [NCH, 128, OWN], F32, kind="ExternalInput")
        self.x_halo = f("x_halo", [NCH, 128, HALO], F32, kind="ExternalInput")
        self.rope = f("rope", [4, 128, HALO + OWN], BF16, kind="ExternalInput")
        self.mask_in = f("mask_in", [3, 128, WINB], BF16, kind="ExternalInput")
        self.mask_out = f("mask_out", [3, 128, WINB], BF16, kind="ExternalInput")
        self.mask_lat = f("mask_lat", [16, 128, NLAT], BF16, kind="ExternalInput")
        self.d_wall = f("wall", [1, WALLN], F32, kind="ExternalInput")
        self.d_vecsL = f("vecsL", [NLAYER, 128, 70], F32, kind="ExternalInput")
        for k, shp in (("gateB", [128, NCH, 48]), ("ident", [128, 128]),
                       ("perm", [128, 128]), ("sstat", [128, NCH, 16]),
                       ("e16", [16, NCH, 128]), ("ones2", [128, 2]),
                       ("onesR", [1, 128])):
            setattr(self, "d_" + k, f(k, shp, BF16, kind="ExternalInput"))
        self.y_out = f("y_out", [NCH, 128, OWN], F32, kind="ExternalOutput")
        self.x1_d = f("x1_spill", [NCH, 128, OWN], F32, kind="Internal")
        self.cc_in = f("cc_in", [D, HALO], F32, kind="Internal")
        self.cc_out = f("cc_out", [NCORES * D, HALO], F32, kind="Internal")
        self.pl_in = f("pl_in", [D, 16], BF16, kind="Internal")
        self.pl_out = f("pl_out", [NCORES * D, 16], BF16, kind="Internal")
        self.stg = [f("stg0", [1, LBS], F32, kind="Internal"),
                    f("stg1", [1, LBS], F32, kind="Internal")]
        if DEBUG_SNAP:
            self.d_dbg = f("dbg_y", [NLAYER * D, OWN], F32, kind="Internal")

    def _declare_sbuf(self):
        mem = Mem(self.nc)
        self.mem = mem
        self.y = mem.alloc("y", [128, NCH, YW], F32)
        self.kwin = mem.alloc("kwin", [128, NCH, KWINW], BF16)
        self.vt = mem.alloc("vt", [128, NVSLOT, 1024], BF16)
        self.tabs = mem.alloc("tabs", [128, 4, SEGW], BF16)
        self.qsb = mem.alloc("qsb", [128, NCH, SEGW], BF16)
        self.ao = mem.alloc("ao", [128, NCH, SEGW], BF16)
        self.xop = mem.alloc("xop", [128, NCH, SEGW], BF16)
        self.wsl = mem.alloc("wsl", [128, 2, NCH, 128], BF16)
        self.Wq_sl = mem.alloc("Wq_sl", [128, NCH, D], BF16)
        self.Wk_sl = mem.alloc("Wk_sl", [128, NCH, D], BF16)
        self.Wv_sl = mem.alloc("Wv_sl", [128, NCH, D], BF16)
        self.masks = mem.alloc("masks", [128, 3, WINB], BF16)
        self.masklat = mem.alloc("masklat", [128, 16, NLAT], BF16)
        self.ident = mem.alloc("ident", [128, 128], BF16)
        self.perm = mem.alloc("perm", [128, 128], BF16)
        self.sstat = mem.alloc("sstat", [128, NCH, 16], BF16)
        self.e16 = mem.alloc("e16", [16, NCH, 128], BF16)
        self.ones2 = mem.alloc("ones2", [128, 2], BF16)
        self.onesR = mem.alloc("onesR", [1, 128], BF16)
        self.vecs = mem.alloc("vecs", [128, 70], F32)
        self.csts = mem.alloc("csts", [128, 2], F32)
        self.sq = mem.alloc("sq", [128, 2, SEGW], BF16)
        self.attst = mem.alloc("attst", [128, 2, 8], F32)
        self.pooled = mem.alloc("pooled", [128, NCH, 16], F32)
        self.pooledB = mem.alloc("pooledB", [128, NCH, 16], BF16)
        self.latGK = mem.alloc("latGK", [128, 4, NLAT], BF16)
        self.latGV = mem.alloc("latGV", [128, 4, NLAT], BF16)
        self.latGVT = mem.alloc("latGVT", [64, 8, NLAT], BF16)
        self.T1 = mem.alloc("T1", [128, 2, WINB], BF16)
        self.PT = mem.alloc("PT", [128, 2, WINB], BF16)
        self.latscr = mem.alloc("latscr", [128, 2, NLAT], BF16)
        self.latscrT = mem.alloc("latscrT", [64, 2, 128], BF16)
        self.gateB = mem.alloc("gateB", [128, NCH, 48], BF16)
        self.lrows = mem.alloc("lrows", [1, 2560], F32)
        self.rstdF = mem.alloc("rstdF", [16, SEGW], F32)
        self.rstdH = mem.alloc("rstdH", [16, 2, SEGW], BF16)
        self.rem = mem.alloc("rem", [128, 4, SEGW], BF16)
        lro = mem.at("lrows")
        self.lrhl = mem.alloc("lrhl", [1, 2048], BF16, offset=lro + 6144)
        self.gq = mem.alloc("gq", [128, 4, SEGW], BF16, offset=mem.at("tabs"))
        self.qnF = mem.alloc("qnF", [128, SEGW], F32, offset=mem.at("T1"))
        self.xn = mem.alloc("xn", [128, NCH, OWN], BF16, offset=mem.at("Wq_sl"))
        assert mem.at("Wk_sl") == mem.at("Wq_sl") + 16384
        self.gates = mem.alloc("gates", [48, SEGW], F32, offset=lro)
        self.grbf = mem.alloc("grbf", [16, SEGW], BF16, offset=lro + 2048)
        self.glT = mem.alloc("glT", [128, 4, 16], F32, offset=lro + 3072)
        self.grB_sb = mem.alloc("grB_sb", [128, SEGW], BF16, offset=lro + 3584)
        self.glbf = mem.alloc("glbf", [16, SEGW], BF16, offset=lro + 4608)
        self.t1mix = mem.alloc("t1mix", [128, SEGW], BF16, offset=mem.at("sq"))
        self.latPF = mem.alloc("latPF", [128, NCH, NLAT], BF16, offset=mem.at("rem"))
        self.x1t = mem.alloc("x1t", [128, NCH, SEGW], F32, offset=mem.at("Wv_sl"))
        gsrc = [("Wv_sl", 0), ("Wv_sl", 4096), ("Wv_sl", 8192), ("Wv_sl", 12288),
                ("kwin", 0), ("kwin", 4096), ("kwin", 8192),
                ("vt", 0), ("vt", 4096), ("vt", 8192),
                ("qsb", 0), ("qsb", 4096), ("ao", 0), ("ao", 4096),
                ("xop", 0), ("xop", 4096), ("tabs", 0), ("rem", 0),
                ("T1", 0), ("rstdF", 0), ("lrows", 0), ("lrows", 4096)]
        assert mem.at("PT") == mem.at("T1") + 1536
        assert mem.at("gateB") == mem.at("T1") + 3840
        assert mem.at("rstdH") == mem.at("rstdF") + 2048
        self.gt = [mem.alloc(f"g{i}", [128, 4, SEGW], BF16,
                             offset=mem.at(nm) + off)
                   for i, (nm, off) in enumerate(gsrc)]
        mem.finish()
        assert mem.top <= self.nc.sbuf_top - 1024, (
            f"SBUF over budget: {mem.top} vs {self.nc.sbuf_top}")

    def W(self, e, sem, val=None):
        self.mk.wait(e, sem, val)

    def I(self, inst, sem, n=1):
        return self.mk.inc(inst, sem, n)

    @staticmethod
    def ycols(t):
        return PADW + SEGW * (t - 1)

    @staticmethod
    def hap(handle, off, dims):
        return bass.AP(handle, off, dims)

    def init_loads(self):
        s, g = self.s, self.g
        self.I(s.dma_start(self.y[:, :, PADW:YW],
                           self.x_own.ap().rearrange("c p t -> p c t")), self.sS, 16)
        for src, dst in ((self.d_ident, self.ident), (self.d_perm, self.perm),
                         (self.d_sstat, self.sstat), (self.d_e16, self.e16),
                         (self.d_ones2, self.ones2), (self.d_onesR, self.onesR),
                         (self.d_gateB, self.gateB)):
            self.I(s.dma_start(dst[:], src.ap()), self.sS, 16)
        self.I(s.dma_start(self.masklat[:],
                           self.mask_lat.ap().rearrange("s p n -> p s n")), self.sS, 16)
        self.I(s.dma_start(self.masks[:],
                           self.mask_out.ap().rearrange("s p n -> p s n")), self.sS, 16)
        for w, dst in (("wq", self.Wq_sl), ("wk", self.Wk_sl),
                       ("wv", self.Wv_sl)):
            self.I(g.dma_start(dst[:],
                               self.hap(self.d_wall, WOFF[w],
                                        [[D, 128], [128 * D, NCH], [1, D]])),
                   self.sG, 16)
        self.I(g.memset(self.xop[:], 0.0), self.sL)
        self.I(g.memset(self.y[:, :, 0:PADW], 0.0), self.sL)
        self.I(g.memset(self.csts[:, 0:1], EPS), self.sL)
        self.I(g.memset(self.csts[:, 1:2], 1e-24), self.sL)
        g.reg_load(self.r_ipid, self.nc.partition_id_tensor.ap())
        g.reg_mod(self.r_im4, self.r_ipid, 4)
        g.reg_add(self.r_ioff, self.r_ipid, NCORES - 1)
        g.reg_mod(self.r_ioff, self.r_ioff, NCORES)
        g.reg_mul(self.r_ioff, self.r_ioff, D * HALO)
        g.reg_add(self.r_ioff2, self.r_ioff, 4 * 128 * HALO)
        self.impAP0 = bass.AP(self.cc_out, self.r_ioff,
                              [[HALO, 128], [128 * HALO, 4], [1, HALO]])
        self.impAP1 = bass.AP(self.cc_out, self.r_ioff2,
                              [[HALO, 128], [128 * HALO, 4], [1, HALO]])

    def emit_ln_seg(self, t, mode):
        tt, v, a, g = self.t, self.v, self.a, self.g
        psF = self.psF
        sP, sA, sV, sL, sG, sS = self.sP, self.sA, self.sV, self.sL, self.sG, self.sS
        Wd = SEGW if t > 0 else HALO
        halo = (t == 0)
        c0 = self.ycols(t) if t > 0 else None
        if mode == "ffn":
            rawap = lambda kc: self.xn[:, kc, SEGW * (t - 1):SEGW * (t - 1) + Wd]
        else:
            rawap = lambda kc: self.xop[:, kc, 0:Wd]
        if halo:
            self.W(PL, self.sP)
            self.I(g.dma_start(self.xop[:, :, 0:HALO],
                               self.x_halo.ap().rearrange("c p t -> p c t")),
                   self.sG, 16)
            self.W(PE, sG)
            self.W(PL, sG)
        else:
            self.W(A, sP)
            self.W(A, sV)
            self.W(A, sS)
            self.W(A, sL)
            if mode == "ffn":
                self.I(a.copy(self.xn[:, 0:4, SEGW * (t - 1):SEGW * t],
                              self.y[:, 0:4, c0:c0 + Wd]), sA)
                self.I(a.copy(self.xn[:, 4:8, SEGW * (t - 1):SEGW * t],
                              self.y[:, 4:8, c0 - 1:c0 - 1 + Wd]), sA)
            else:
                self.I(a.copy(self.xop[:, :, 0:Wd], self.y[:, :, c0:c0 + Wd]), sA)
            self.W(PE, sA)
            self.W(PL, sA)
        for kc in range(NCH):
            par = kc & 1
            self.W(PL, sP)
            self.I(g.tensor_tensor(self.sq[:, par, 0:Wd], rawap(kc), rawap(kc),
                                   ALU.mult), sL)
            self.I(tt.matmul(psF[0:1, 0, 0:Wd], self.ones2[:, 0:1], rawap(kc),
                             start=(kc == 0), stop=(kc == NCH - 1)), self.sP)
            self.W(PE, sL)
            self.I(tt.matmul(psF[0:1, 1, 0:Wd], self.ones2[:, 1:2],
                             self.sq[:, par, 0:Wd],
                             start=(kc == 0), stop=(kc == NCH - 1)), self.sP)
        lr = self.lrows
        s0, s1, s2 = lr[0:1, 0:Wd], lr[0:1, 512:512 + Wd], lr[0:1, 1024:1024 + Wd]
        self.W(DVE, sP)
        self.I(v.tensor_scalar_mul(s0, psF[0:1, 0, 0:Wd], 1.0 / D), sV)
        self.W(A, sV)
        self.I(a.activation(s1, s0, AF.Square), sA)
        self.W(DVE, sA)
        self.I(v.scalar_tensor_tensor(s2, psF[0:1, 1, 0:Wd], 1.0 / D, s1,
                                      ALU.mult, ALU.subtract), sV)
        self.W(A, sV)
        self.W(A, self.sL)
        self.I(a.activation(s2, s2, AF.Ln, bias=self.csts[0:1, 0:1]), sA)
        self.W(A, sA)
        self.I(a.activation(s2, s2, AF.Exp, scale=-0.5), sA)
        self.W(DVE, sA)
        self.I(v.scalar_tensor_tensor(s1, s0, -1.0, s2, ALU.mult, ALU.mult), sV)
        hb = self.lrhl
        self.W(A, sV)
        self.I(a.copy(hb[0:1, 0:Wd], s2), sA)
        self.W(DVE, sA)
        self.I(v.tensor_sub(hb[0:1, 512:512 + Wd], s2, hb[0:1, 0:Wd]), sV)
        self.W(A, sV)
        self.I(a.copy(hb[0:1, 1024:1024 + Wd], s1), sA)
        self.W(DVE, sA)
        self.I(v.tensor_sub(hb[0:1, 1536:1536 + Wd], s1, hb[0:1, 1024:1024 + Wd]), sV)
        self.W(PE, sV)
        self.I(tt.matmul(psF[:, 2, 0:Wd], self.onesR[0:1, :], hb[0:1, 0:Wd],
                         start=True, stop=False), self.sP)
        self.I(tt.matmul(psF[:, 2, 0:Wd], self.onesR[0:1, :], hb[0:1, 512:512 + Wd],
                         start=False, stop=True), self.sP)
        self.I(tt.matmul(psF[:, 3, 0:Wd], self.onesR[0:1, :], hb[0:1, 1024:1024 + Wd],
                         start=True, stop=False), self.sP)
        self.I(tt.matmul(psF[:, 3, 0:Wd], self.onesR[0:1, :], hb[0:1, 1536:1536 + Wd],
                         start=False, stop=True), self.sP)
        for kc in range(NCH):
            if mode == "ln0":
                src = self.y[:, kc, c0:c0 + Wd]
                self.W(DVE, sP)
                self.W(DVE, sV)
                self.I(v.scalar_tensor_tensor(self.qnF[:, 0:Wd], src, 1.0,
                                              psF[:, 3, 0:Wd], ALU.mult, ALU.add), sV)
                self.W(DVE, sV)
                self.I(v.tensor_tensor(src, self.qnF[:, 0:Wd], psF[:, 2, 0:Wd],
                                       ALU.mult), sV)
            elif mode == "outer_attn":
                src = rawap(kc) if halo else self.y[:, kc, c0:c0 + Wd]
                self.W(DVE, sP)
                self.W(DVE, sV)
                self.I(v.scalar_tensor_tensor(self.qnF[:, 0:Wd], src, 1.0,
                                              psF[:, 3, 0:Wd], ALU.mult, ALU.add), sV)
                self.W(DVE, sV)
                self.I(v.tensor_tensor(rawap(kc), self.qnF[:, 0:Wd],
                                       psF[:, 2, 0:Wd], ALU.mult), sV)
            else:
                ysrc = (self.y[:, kc, c0:c0 + Wd] if kc < 4
                        else self.y[:, kc, c0 - 1:c0 - 1 + Wd])
                self.W(DVE, sP)
                self.W(DVE, sV)
                self.I(v.scalar_tensor_tensor(self.qnF[:, 0:Wd], ysrc, 1.0,
                                              psF[:, 3, 0:Wd], ALU.mult, ALU.add), sV)
                self.W(DVE, sV)
                self.W(DVE, sS)
                self.I(v.scalar_tensor_tensor(self.qnF[:, 0:Wd], self.qnF[:, 0:Wd],
                                              self.vecs[:, kc:kc + 1],
                                              psF[:, 2, 0:Wd], ALU.mult, ALU.mult), sV)
                self.W(DVE, sV)
                self.I(v.tensor_scalar_add(rawap(kc), self.qnF[:, 0:Wd],
                                           self.vecs[:, 8 + kc:9 + kc]), sV)

    def emit_qk_side(self, t, side):
        tt, v, a, g = self.t, self.v, self.a, self.g
        psF = self.psF
        sP, sA, sV, sL, sG, sS = self.sP, self.sA, self.sV, self.sL, self.sG, self.sS
        Wd = SEGW if t > 0 else HALO
        Wsl = self.Wk_sl if side == "k" else self.Wq_sl
        if side == "k":
            cur0 = 0 if t == 0 else HALO
            dst = lambda fc: self.kwin[:, fc, cur0:cur0 + Wd]
        else:
            dst = lambda fc: self.qsb[:, fc, 0:Wd]
        for fc in range(NCH):
            par = fc & 1
            self.W(PE, sA)
            self.W(PE, sV)
            self.W(PE, sG)
            self.W(PE, sL)
            last = None
            for kc in range(NCH):
                last = tt.matmul(psF[:, par, 0:Wd], Wsl[:, kc, ts(fc, 128)],
                                 self.xop[:, kc, 0:Wd],
                                 start=(kc == 0), stop=(kc == NCH - 1))
            self.I(last, sP)
            self.W(A, sP)
            self.W(A, sL)
            self.I(a.copy(dst(fc), psF[:, par, 0:Wd]), sA)
            self.W(PL, sA)
            self.W(PL, sP)
            self.I(g.tensor_tensor(self.sq[:, par, 0:Wd], dst(fc), dst(fc),
                                   ALU.mult), sL)
            self.W(PE, sL)
            self.I(tt.matmul(psF[0:16, 2, 0:Wd], self.sstat[:, fc, :],
                             self.sq[:, par, 0:Wd], start=(fc == 0),
                             stop=(fc == NCH - 1)), sP)
        self.W(A, sP)
        self.W(A, self.sL)
        self.I(a.activation(self.rstdF[:, 0:Wd], psF[0:16, 2, 0:Wd], AF.Ln,
                            bias=self.csts[0:16, 1:2]), sA)
        self.W(A, sA)
        self.I(a.activation(self.rstdF[:, 0:Wd], self.rstdF[:, 0:Wd], AF.Exp,
                            scale=-0.5), sA)
        self.W(A, sA)
        self.I(a.copy(self.rstdH[:, 0, 0:Wd], self.rstdF[:, 0:Wd]), sA)
        self.W(DVE, sA)
        self.I(v.tensor_sub(self.rstdH[:, 1, 0:Wd], self.rstdF[:, 0:Wd],
                            self.rstdH[:, 0, 0:Wd]), sV)
        scol = 68 if side == "q" else 69
        tb = 0 if side == "q" else 2
        for fc in range(NCH):
            par = fc & 1
            self.W(PE, sV)
            self.W(PE, sA)
            self.I(tt.matmul(psF[:, 5, 0:Wd], self.e16[:, fc, :],
                             self.rstdH[:, 0, 0:Wd], start=True, stop=False), sP)
            self.I(tt.matmul(psF[:, 5, 0:Wd], self.e16[:, fc, :],
                             self.rstdH[:, 1, 0:Wd], start=False, stop=True), sP)
            self.W(DVE, sP)
            self.W(DVE, sL)
            self.W(DVE, sS)
            self.I(v.scalar_tensor_tensor(self.qnF[:, 0:Wd], dst(fc),
                                          self.vecs[:, scol:scol + 1],
                                          psF[:, 5, 0:Wd], ALU.mult, ALU.mult), sV)
            self.W(DVE, sV)
            self.W(DVE, sS)
            self.I(v.tensor_tensor(self.sq[:, 0, 0:Wd], self.qnF[:, 0:Wd],
                                   self.tabs[:, tb + 1, 0:Wd], ALU.mult), sV)
            self.W(PL, sV)
            self.I(g.tensor_tensor(self.sq[:, 1, 0:Wd], self.qnF[:, 0:Wd],
                                   self.tabs[:, tb, 0:Wd], ALU.mult), sL)
            self.W(PE, sV)
            self.W(PE, sL)
            self.W(PE, sA)
            self.I(tt.matmul(psF[:, 3 + par, 0:Wd], self.perm[:],
                             self.sq[:, 0, 0:Wd], start=True, stop=False), sP)
            self.I(tt.matmul(psF[:, 3 + par, 0:Wd], self.ident[:],
                             self.sq[:, 1, 0:Wd], start=False, stop=True), sP)
            self.W(A, sP)
            self.I(a.copy(dst(fc), psF[:, 3 + par, 0:Wd]), sA)

    def emit_qk_outer(self, t, side):
        tt, a = self.t, self.a
        psF = self.psF
        sP, sA, sG, sV = self.sP, self.sA, self.sG, self.sV
        Wd = SEGW if t > 0 else HALO
        Wsl = self.Wk_sl if side == "k" else self.Wq_sl
        if side == "k":
            cur0 = 0 if t == 0 else HALO
            dst = lambda fc: self.kwin[:, fc, cur0:cur0 + Wd]
        else:
            dst = lambda fc: self.qsb[:, fc, 0:Wd]
        for fc in range(NCH):
            par = fc & 1
            self.W(PE, sA)
            self.W(PE, sV)
            self.W(PE, sG)
            self.W(PE, self.sL)
            last = None
            for kc in range(NCH):
                last = tt.matmul(psF[:, par, 0:Wd], Wsl[:, kc, ts(fc, 128)],
                                 self.xop[:, kc, 0:Wd],
                                 start=(kc == 0), stop=(kc == NCH - 1))
            self.I(last, sP)
            self.W(A, sP)
            self.I(a.activation(dst(fc), psF[:, par, 0:Wd], AF.Copy,
                                scale=(0.125 if side == "k" else 1.0)), sA)

    def emit_v(self, t):
        tt, a = self.t, self.a
        psF = self.psF
        sP, sA = self.sP, self.sA
        Wd = SEGW if t > 0 else HALO
        ntc = Wd // 128
        g0 = 0 if t == 0 else 3 + 4 * (t - 1)
        for tc in range(ntc):
            slot = (g0 + tc) % NVSLOT
            for half in range(2):
                par = (tc * 2 + half) & 1
                self.W(PE, sA)
                last = None
                for kc in range(NCH):
                    last = tt.matmul(psF[:, 3 + par, 0:512],
                                     self.xop[:, kc, ts(tc, 128)],
                                     self.Wv_sl[:, kc, ts(half, 512)],
                                     start=(kc == 0), stop=(kc == NCH - 1))
                self.I(last, sP)
                self.W(A, sP)
                self.I(a.copy(self.vt[:, slot, ts(half, 512)],
                              psF[:, 3 + par, 0:512]), sA)

    def emit_attn_subtile(self, t, sl, outer, tag):
        mk = self.mk
        tt, v, a, g = self.t, self.v, self.a, self.g
        psF, psB = self.psF, self.psB
        sP, sA, sV, sL = self.sP, self.sA, self.sV, self.sL
        if t == 0:
            q0c, w0, gsub, msl = 256, 0, None, 0
        else:
            q0c = 128 * sl
            w0 = HALO + 128 * sl - 256
            gsub = 4 * (t - 1) + sl
            msl = 1 if (t == 1 and sl == 0) else (2 if (t == 1 and sl == 1) else 0)
        s_glob = (3 + 4 * (t - 1) + sl) if t > 0 else 2
        cslots = [((s_glob - 2 + i) % NVSLOT) for i in range(3)]
        if True:
            for hv in range(16):
                fcv, half = divmod(hv, 2)
                j, u = divmod(fcv, 2)
                par = hv & 1
                qap = self.qsb[64 * half:64 * half + 64, fcv, q0c:q0c + 128]
                kap = self.kwin[64 * half:64 * half + 64, fcv, w0:w0 + WINB]
                self.W(PE, sV)
                self.W(PE, sA)
                self.I(tt.matmul(psF[:, par, 0:WINB], qap, kap, start=True,
                                 stop=True, skip_group_check=True), sP)
                self.W(DVE, sP)
                self.W(DVE, sA)
                self.I(v.scalar_tensor_tensor(self.T1[:, par, :], psF[:, par, 0:WINB],
                                              1.0, self.masks[:, msl, :],
                                              ALU.mult, ALU.add), sV)
                self.W(DVE, sV)
                self.I(v.reduce_max(self.attst[:, par, 0:1], self.T1[:, par, :],
                                    axis=AX, negate=True), sV)
                self.W(A, sV)
                self.I(a.activation(self.T1[:, par, :], self.T1[:, par, :], AF.Exp,
                                    bias=self.attst[:, par, 0:1],
                                    accum_out=self.attst[:, par, 1:2]), sA)
                self.W(DVE, sA)
                self.I(v.reciprocal(self.attst[:, par, 2:3],
                                    self.attst[:, par, 1:2]), sV)
                self.W(DVE, sV)
                if outer:
                    self.I(v.tensor_scalar(self.T1[:, par, :], self.T1[:, par, :],
                                           self.attst[:, par, 2:3],
                                           self.glT[:, sl, hv:hv + 1],
                                           ALU.mult, ALU.mult), sV)
                else:
                    self.I(v.tensor_scalar_mul(self.T1[:, par, :],
                                               self.T1[:, par, :],
                                               self.attst[:, par, 2:3]), sV)
                self.W(PE, sV)
                self.W(PE, sA)
                for c in range(3):
                    self.I(tt.transpose(psB[:, par, ts(c, 128)],
                                        self.T1[:, par, ts(c, 128)],
                                        self.ident[:]), sP)
                self.W(A, sP)
                self.I(a.copy(self.PT[:, par, :], psB[:, par, 0:WINB]), sA)
                self.W(PE, sA)
                last = None
                for c in range(3):
                    vap = self.vt[:, cslots[c], ts(hv, 64)]
                    last = tt.matmul(psF[0:64, 2 + par, 0:128], vap,
                                     self.PT[:, par, ts(c, 128)],
                                     start=(c == 0), stop=(c == 2),
                                     skip_group_check=True)
                self.I(last, sP)
                self.W(A, sP)
                self.I(a.copy(self.ao[64 * half:64 * half + 64, fcv,
                                      q0c:q0c + 128], psF[0:64, 2 + par, 0:128]), sA)
            if outer and t > 0:
                self.emit_latent_pair(t, sl, gsub)

    def emit_latent_pair(self, t, sl, gsub):
        tt, v, a, g = self.t, self.v, self.a, self.g
        psF, psB = self.psF, self.psB
        sP, sA, sV, sL = self.sP, self.sA, self.sV, self.sL
        q0c = 128 * sl
        for lh in range(8):
            j, w = divmod(lh, 2)
            lr = slice(64 * w, 64 * w + 64)
            self.W(PE, sA)
            self.W(PE, sV)
            self.I(tt.matmul(psF[:, 4, 0:NLAT],
                             self.gq[lr, j, q0c:q0c + 128],
                             self.latGK[lr, j, :],
                             start=True, stop=True, skip_group_check=True), sP)
            self.W(DVE, sP)
            self.W(DVE, sA)
            self.I(v.scalar_tensor_tensor(self.latscr[:, w, :], psF[:, 4, 0:NLAT],
                                          1.0, self.masklat[:, gsub, :],
                                          ALU.mult, ALU.add), sV)
            self.W(DVE, sV)
            self.I(v.reduce_max(self.attst[:, w, 4:5], self.latscr[:, w, :],
                                axis=AX, negate=True), sV)
            self.W(A, sV)
            self.I(a.activation(self.latscr[:, w, :], self.latscr[:, w, :], AF.Exp,
                                bias=self.attst[:, w, 4:5],
                                accum_out=self.attst[:, w, 5:6]), sA)
            self.W(DVE, sA)
            self.I(v.reciprocal(self.attst[:, w, 6:7], self.attst[:, w, 5:6]), sV)
            self.W(DVE, sV)
            self.I(v.tensor_scalar_mul(self.latscr[:, w, :], self.latscr[:, w, :],
                                       self.attst[:, w, 6:7]), sV)
            self.W(PE, sV)
            self.W(PE, sA)
            self.I(tt.transpose(psB[0:64, w, 384:512], self.latscr[:, w, :],
                                self.ident[:]), sP)
            self.W(A, sP)
            self.I(a.copy(self.latscrT[:, w, :], psB[0:64, w, 384:512]), sA)
            self.W(PE, sA)
            self.I(tt.matmul(psF[0:64, 5, 0:128], self.latGVT[:, lh, :],
                             self.latscrT[:, w, :], start=True, stop=True,
                             skip_group_check=True), sP)
            self.W(A, sP)
            self.I(a.copy(self.rem[lr, ds(j, 1), q0c:q0c + 128],
                          psF[0:64, 5, 0:128]), sA)

    def emit_halo_proj(self, Lv):
        tt, v, g = self.t, self.v, self.g
        psF = self.psF
        sP, sA, sV, sG = self.sP, self.sA, self.sV, self.sG
        for fc in range(NCH):
            par = fc & 1
            self.W(PL, sP)
            src = self.hap(self.stg[Lv], LOFF["Wo"] + fc * 128,
                           [[D, 128], [128 * D, NCH], [1, 128]])
            self.I(g.dma_start(self.wsl[:, par, :, :], src), sG, 16)
            self.W(PE, sG)
            self.W(PE, sV)
            self.W(PE, sA)
            last = None
            for kc in range(NCH):
                last = tt.matmul(psF[:, par, 0:1], self.wsl[:, par, kc, :],
                                 self.ao[:, kc, 383:384],
                                 start=(kc == 0), stop=(kc == NCH - 1))
            self.I(last, sP)
            self.W(A, sP)
            self.W(A, sV)
            self.I(self.a.copy(self.pooled[:, fc, 0:1], psF[:, par, 0:1]), sA)

    def emit_halo_add(self):
        self.W(DVE, self.sA)
        self.W(DVE, self.sV)
        self.I(self.v.tensor_tensor(self.y[:, :, PADW - 1:PADW],
                                    self.y[:, :, PADW - 1:PADW],
                                    self.pooled[:, :, 0:1], ALU.add), self.sV)

    def emit_out_proj(self, t, Lv, outer):
        tt, v, a, g = self.t, self.v, self.a, self.g
        psF = self.psF
        sP, sA, sV, sG = self.sP, self.sA, self.sV, self.sG
        Wd = SEGW
        c0 = self.ycols(t)
        if outer:
            for fc in range(NCH):
                par = fc & 1
                self.W(PL, sP)
                self.I(g.dma_start(
                    self.wsl[:, par, 0:4, :],
                    self.hap(self.d_wall, WOFF["r_out_w"] + fc * 128,
                             [[D, 128], [128 * D, 4], [1, 128]])), sG, 16)
                self.W(PE, sG)
                self.W(PE, sV)
                self.W(PE, sA)
                last = None
                for kc in range(4):
                    last = tt.matmul(psF[:, 2 + par, 0:Wd], self.wsl[:, par, kc, :],
                                     self.rem[:, kc, 0:Wd],
                                     start=(kc == 0), stop=(kc == 3))
                self.I(last, sP)
                self.I(tt.matmul(psF[:, 4 + par, 0:Wd], self.e16[:, fc, :],
                                 self.grbf[:, 0:Wd], start=True, stop=True,
                                 skip_group_check=True), sP)
                self.W(A, sP)
                self.I(a.copy(self.grB_sb[:, 0:Wd], psF[:, 4 + par, 0:Wd]), sA)
                self.W(DVE, sA)
                self.I(v.tensor_tensor(self.t1mix[:, 0:Wd], psF[:, 2 + par, 0:Wd],
                                       self.grB_sb[:, 0:Wd], ALU.mult), sV)
                self.W(DVE, sV)
                self.I(v.tensor_tensor(self.ao[:, fc, 0:Wd], self.t1mix[:, 0:Wd],
                                       self.ao[:, fc, 0:Wd], ALU.add), sV)
        for fc in range(NCH):
            par = fc & 1
            self.W(PL, sP)
            if outer:
                src = self.hap(self.d_wall, WOFF["w_out"] + fc * 128,
                               [[D, 128], [128 * D, NCH], [1, 128]])
            else:
                src = self.hap(self.stg[Lv], LOFF["Wo"] + fc * 128,
                               [[D, 128], [128 * D, NCH], [1, 128]])
            self.I(g.dma_start(self.wsl[:, par, :, :], src), sG, 16)
            self.W(PE, sG)
            self.W(PE, sV)
            self.W(PE, sA)
            last = None
            for kc in range(NCH):
                last = tt.matmul(psF[:, par, 0:Wd], self.wsl[:, par, kc, :],
                                 self.ao[:, kc, 0:Wd],
                                 start=(kc == 0), stop=(kc == NCH - 1))
            self.I(last, sP)
            self.W(DVE, sP)
            self.I(v.tensor_tensor(self.y[:, fc, c0:c0 + Wd],
                                   self.y[:, fc, c0:c0 + Wd],
                                   psF[:, par, 0:Wd], ALU.add), sV)

    def emit_gates(self, t):
        tt, v, a = self.t, self.v, self.a
        psF, psB = self.psF, self.psB
        sP, sA, sV = self.sP, self.sA, self.sV
        Wd = SEGW
        for fc in range(NCH):
            self.W(PE, sA)
            self.I(tt.matmul(psF[0:48, 5, 0:Wd], self.gateB[:, fc, :],
                             self.qsb[:, fc, 0:Wd], start=(fc == 0),
                             stop=(fc == NCH - 1)), sP)
        self.W(A, sP)
        self.I(a.activation(self.gates[:, 0:Wd], psF[0:48, 5, 0:Wd], AF.Exp,
                            scale=-1.0), sA)
        self.W(DVE, sA)
        self.I(v.tensor_scalar_add(self.gates[:, 0:Wd], self.gates[:, 0:Wd], 1.0), sV)
        self.W(DVE, sV)
        self.I(v.reciprocal(self.gates[:, 0:Wd], self.gates[:, 0:Wd]), sV)
        self.W(A, sV)
        self.I(a.copy(self.grbf[:, 0:Wd], self.gates[32:48, 0:Wd]), sA)
        self.W(A, sA)
        self.I(a.copy(self.glbf[:, 0:Wd], self.gates[0:16, 0:Wd]), sA)
        for sl in range(4):
            self.W(PE, sA)
            self.I(tt.transpose(psB[0:128, 0, 512:528],
                                self.glbf[0:16, ts(sl, 128)], self.ident[0:16, 0:16]), sP)
            self.W(A, sP)
            self.I(a.copy(self.glT[:, sl, :], psB[0:128, 0, 512:528]), sA)

    def emit_latent_prep(self):
        nc = self.nc
        tt, v, a, g, s = self.t, self.v, self.a, self.g, self.s
        psF, psB = self.psF, self.psB
        sP, sA, sV, sL, sG, sS, sC = (self.sP, self.sA, self.sV, self.sL,
                                      self.sG, self.sS, self.sC)
        for t in range(1, 5):
            self.emit_ln_seg(t, "outer_attn")
            for kc in range(NCH):
                self.W(DVE, sV)
                self.I(v.reduce_sum(
                    self.pooled[:, kc, ts(t - 1, 4)],
                    self.xop[:, kc, 0:SEGW].rearrange("p (l w) -> p l w", w=128),
                    axis=AX), sV)
        self.W(DVE, sV)
        self.I(v.tensor_scalar_mul(self.pooledB[:], self.pooled[:], 1.0 / 128.0), sV)
        self.W(SP, sV)
        self.I(s.dma_start(self.pl_in.ap().rearrange("(c p) l -> p c l", p=128),
                           self.pooledB[:]), self.sE, 16)
        self.W(PL, self.sE)
        self.I(g.collective_compute(
            "AllGather", ALU.bypass, replica_groups=[list(range(NCORES))],
            ins=[self.pl_in.ap().opt()], outs=[self.pl_out.ap().opt()]), self.sC)
        self.W(PL, sC)
        with g.register("pid") as pid, g.register("off") as off, \
                g.register("off2") as off2:
            g.reg_load(pid, nc.partition_id_tensor.ap())
            g.reg_div(off, pid, 4)
            g.reg_mul(off, off, 4 * D * 16)
            for r in range(4):
                g.reg_add(off2, off, r * D * 16)
                src = bass.AP(self.pl_out, off2,
                              [[16, 128], [128 * 16, NCH], [1, 16]])
                dst = bass.AP(self.latPF, r * 16, [[512, 128], [64, NCH], [1, 16]])
                self.I(g.dma_start(dst, src), self.sG, 16)
        for wnm, dstt in (("wgk", self.latGK), ("wgv", self.latGV)):
            for fc in range(4):
                par = fc & 1
                self.W(PL, sP)
                self.I(g.dma_start(
                    self.wsl[:, par, :, :],
                    self.hap(self.d_wall, WOFF[wnm] + fc * 128,
                             [[512, 128], [128 * 512, NCH], [1, 128]])),
                    sG, 16)
                self.W(PE, sG)
                self.W(PE, sA)
                last = None
                for kc in range(NCH):
                    last = tt.matmul(psF[:, 4, 0:NLAT], self.wsl[:, par, kc, :],
                                     self.latPF[:, kc, :],
                                     start=(kc == 0), stop=(kc == NCH - 1))
                self.I(last, sP)
                self.W(A, sP)
                self.I(a.copy(dstt[:, fc, :], psF[:, 4, 0:NLAT]), sA)
        for lh in range(8):
            lfc, lhalf = divmod(lh, 2)
            self.W(PE, sA)
            self.I(tt.transpose(psB[0:64, 0, 512:576],
                                self.latGV[64 * lhalf:64 * lhalf + 64, lfc, :],
                                self.ident[64 * lhalf:64 * lhalf + 64,
                                           64 * lhalf:64 * lhalf + 64]), sP)
            self.W(A, sP)
            self.I(a.copy(self.latGVT[:, lh, :], psB[0:64, 0, 512:576]), sA)

    def emit_gq(self, t):
        tt, g, a = self.t, self.g, self.a
        psF = self.psF
        sP, sA, sG = self.sP, self.sA, self.sG
        Wd = SEGW
        for fc in range(4):
            par = fc & 1
            self.W(PL, sP)
            self.I(g.dma_start(
                self.wsl[:, par, :, :],
                self.hap(self.d_wall, WOFF["wgq"] + fc * 128,
                         [[512, 128], [128 * 512, NCH], [1, 128]])),
                sG, 16)
            self.W(PE, sG)
            self.W(PE, sA)
            last = None
            for kc in range(NCH):
                last = tt.matmul(psF[:, 2 + par, 0:Wd], self.wsl[:, par, kc, :],
                                 self.xop[:, kc, 0:Wd],
                                 start=(kc == 0), stop=(kc == NCH - 1))
            self.I(last, sP)
            self.W(A, sP)
            self.I(a.copy(self.gq[:, fc, 0:Wd], psF[:, 2 + par, 0:Wd]), sA)

    def emit_bridge(self, t):
        if t < 2:
            return
        self.W(PL, self.sP)
        self.I(self.g.tensor_copy(self.kwin[:, :, 0:HALO],
                                  self.kwin[:, :, SEGW:KWINW]), self.sL)

    def emit_outer(self):
        self.emit_latent_prep()
        for t in range(5):
            self.emit_ln_seg(t, "outer_attn")
            self.emit_bridge(t)
            self.emit_qk_outer(t, "k")
            self.emit_v(t)
            if t > 0:
                self.emit_qk_outer(t, "q")
                self.emit_gq(t)
                self.emit_gates(t)
                for sl in range(4):
                    self.emit_attn_subtile(t, sl, True, f"oat{t}_{sl}")
                self.emit_out_proj(t, None, True)

    def emit_layer_body(self, L):
        nc, mk = self.nc, self.mk
        tt, v, a, g, s = self.t, self.v, self.a, self.g, self.s
        sP, sA, sV, sL, sG, sS, sC = (self.sP, self.sA, self.sV, self.sL,
                                      self.sG, self.sS, self.sC)
        Lv = L % 2
        stg = self.stg[Lv]
        self.W(PL, sC)
        self.W(PL, sP)
        self.W(PL, sA)
        with g.If_eq(self.r_im4, 0):
            g.memset(self.xop[:, :, 0:HALO + 1], 0.0)
            g.memset(self.y[:, :, PADW - 1:PADW], 0.0)
        with g.Else():
            g.memset(self.xop[:, 4:8, 0:1], 0.0)
            g.dma_start(self.xop[:, 0:4, 0:HALO],
                        self.impAP0).then_inc(self.sI, 16)
            g.dma_start(self.xop[:, 4:8, 1:HALO + 1],
                        self.impAP1).then_inc(self.sI, 16)
            g.wait_ge(self.sI, 32 * (L + 1))
            g.tensor_copy(self.y[:, 0:4, PADW - 1:PADW],
                          self.xop[:, 0:4, HALO - 1:HALO])
            g.tensor_copy(self.y[:, 4:8, PADW - 1:PADW],
                          self.xop[:, 4:8, HALO:HALO + 1])
        self.I(g.memset(self.attst[0:1, 0, 7:8], 0.0), sL)
        self.W(PL, self.sX)
        for third, dstw in ((0, self.Wq_sl), (1, self.Wk_sl), (2, self.Wv_sl)):
            src = self.hap(stg, LOFF["Wqkv"] + third * D,
                           [[3 * D, 128], [128 * 3 * D, NCH], [1, D]])
            self.I(g.dma_start(dstw[:], src), sG, 16)
        self.W(PL, self.sG)
        self.I(g.dma_start(self.stg[1 - Lv].ap(),
                           self.hap(self.d_wall, OB0 + (L + 1) * LBS,
                                    [[1, 1], [1, LBS]])),
               self.sX, 16)
        self.W(SP, sV)
        self.W(SP, sA)
        self.I(s.dma_start(self.vecs[:],
                           self.hap(self.d_vecsL, L * 128 * 70,
                                    [[70, 128], [1, 70]])),
               sS, 16)
        for t in range(5):
            Wd = SEGW if t > 0 else HALO
            if t > 0:
                c0 = self.ycols(t)
                self.W(A, sP)
                self.W(A, sV)
                self.W(A, sL)
                self.I(a.copy(self.xop[:, 0:4, 0:Wd], self.y[:, 0:4, c0:c0 + Wd]), sA)
                self.I(a.copy(self.xop[:, 4:8, 0:Wd],
                              self.y[:, 4:8, c0 - 1:c0 - 1 + Wd]), sA)
            tcol = 0 if t == 0 else HALO + SEGW * (t - 1)
            self.W(SP, sP)
            self.W(SP, sV)
            self.W(SP, sL)
            self.I(s.dma_start(self.tabs[:, :, 0:Wd],
                               self.rope.ap().rearrange("s p t -> p s t")[:, :, tcol:tcol + Wd]),
                   sS, 16)
            self.emit_bridge(t)
            self.emit_qk_side(t, "k")
            self.emit_v(t)
            self.emit_qk_side(t, "q")
            subs = [2] if t == 0 else [0, 1, 2, 3]
            for sl in subs:
                self.emit_attn_subtile(t, sl, False, f"iat{t}_{sl}")
            if t == 0:
                self.emit_halo_proj(Lv)
            else:
                self.emit_out_proj(t, Lv, False)
        self.emit_halo_add()
        for t in range(1, 5):
            self.emit_ln_seg(t, "ffn")
        self.emit_ffn(Lv)
        if DEBUG_SNAP:
            self.W(PL, sV)
            self.W(PL, sP)
            self.I(g.dma_start(
                bass.AP(self.d_dbg, L * NCH * 128 * OWN,
                        [[NCH * OWN, 128], [OWN, NCH], [1, OWN]]),
                self.y[:, :, PADW:YW]), sG, 16)
        self.W(SP, sV)
        self.W(SP, sC)
        self.I(s.dma_start(self.cc_in.ap().rearrange("(c p) t -> p c t", p=128),
                           self.y[:, :, YW - HALO:YW]), self.sE, 16)
        self.W(PL, self.sE)
        self.I(g.collective_compute(
            "AllGather", ALU.bypass, replica_groups=[list(range(NCORES))],
            ins=[self.cc_in.ap().opt()], outs=[self.cc_out.ap().opt()]), sC)

    def emit_ffn(self, Lv):
        mk = self.mk
        tt, v, a, g = self.t, self.v, self.a, self.g
        psF = self.psF
        sP, sA, sV, sG = self.sP, self.sA, self.sV, self.sG
        for gc in range(NGC):
            for ab in range(2):
                wo = LOFF["W1a"] if ab == 0 else LOFF["W1g"]
                self.W(PL, sP)
                self.I(g.dma_start(
                    self.wsl[:, ab, :, :],
                    self.hap(self.stg[Lv], wo + gc * 128,
                             [[DIP, 128], [128 * DIP, NCH], [1, 128]])), sG, 16)
            for t in range(1, 5):
                for ab in range(2):
                    self.W(PE, sG)
                    self.W(PE, sA if ab else sV)
                    last = None
                    for kc in range(NCH):
                        last = tt.matmul(psF[:, ab * 2 + (t & 1), 0:SEGW],
                                         self.wsl[:, ab, kc, :],
                                         self.xn[:, kc, ts(t - 1, SEGW)],
                                         start=(kc == 0), stop=(kc == NCH - 1),
                                         skip_group_check=True)
                    self.I(last, sP)
                self.W(A, sP)
                self.W(A, sV)
                self.W(A, self.sS)
                self.I(a.activation(self.sq[:, t & 1, :], psF[:, 2 + (t & 1), 0:SEGW],
                                    AF.Gelu, bias=self.vecs[:, 46 + gc:47 + gc]), sA)
                self.W(DVE, sA)
                self.W(DVE, self.sS)
                self.I(v.scalar_tensor_tensor(self.gt[gc][:, t - 1, :],
                                              psF[:, (t & 1), 0:SEGW],
                                              self.vecs[:, 24 + gc:25 + gc],
                                              self.sq[:, t & 1, :],
                                              ALU.add, ALU.mult), sV)
        for fc in range(NCH):
            for grp in range(3):
                kcs = 8 if grp < 2 else 6
                self.W(PL, sP)
                self.I(g.dma_start(
                    self.wsl[:, grp & 1, 0:kcs, :],
                    self.hap(self.stg[Lv],
                             LOFF["W2p"] + grp * 8 * 128 * D + fc * 128,
                             [[D, 128], [128 * D, kcs], [1, 128]])), sG, 16)
                for t in range(1, 5):
                    self.W(PE, sG)
                    if grp == 0:
                        self.W(PE, sV)
                    last = None
                    for kc in range(kcs):
                        last = tt.matmul(psF[:, 2 + (t - 1), 0:SEGW],
                                         self.wsl[:, grp & 1, kc, :],
                                         self.gt[grp * 8 + kc][:, t - 1, :],
                                         start=(grp == 0 and kc == 0),
                                         stop=(grp == 2 and kc == kcs - 1),
                                         skip_group_check=True)
                    self.I(last, sP)
            for t in range(1, 5):
                c0 = self.ycols(t)
                yap = self.y[:, fc, c0:c0 + SEGW]
                self.W(DVE, sP)
                self.W(DVE, self.sS)
                self.I(v.scalar_tensor_tensor(yap, psF[:, 2 + (t - 1), 0:SEGW],
                                              self.vecs[:, 16 + fc:17 + fc],
                                              yap, ALU.add, ALU.add), sV)

    def emit_final(self):
        v, s = self.v, self.s
        self.W(DVE, self.sE)
        if DEBUG_SNAP:
            self.W(DVE, self.sG)
        for t in range(1, 5):
            c0 = self.ycols(t)
            self.W(SP, self.sV)
            self.W(SP, self.sP)
            self.I(s.dma_start(
                self.x1t[:, :, 0:SEGW],
                self.x1_d.ap().rearrange("c p t -> p c t")[:, :, ts(t - 1, SEGW)]),
                self.sS, 16)
            self.W(DVE, self.sS)
            for kc in range(NCH):
                self.I(v.tensor_tensor(self.y[:, kc, c0:c0 + SEGW],
                                       self.y[:, kc, c0:c0 + SEGW],
                                       self.x1t[:, kc, 0:SEGW], ALU.add), self.sV)

    def emit_output(self):
        for sem in (self.sV, self.sP, self.sA, self.sL):
            self.W(SP, sem)
        self.I(self.s.dma_start(self.y_out.ap().rearrange("c p t -> p c t"),
                                self.y[:, :, PADW:YW]), self.sS, 16)
        self.W(SP, self.sS)

    def emit(self):
        mk = self.mk
        self.init_loads()
        if self.stage == "m1":
            self.emit_output()
            self.nc.all_engine_barrier()
            return
        self.emit_outer()
        if self.stage == "outer":
            self.emit_output()
            self.nc.all_engine_barrier()
            return
        self.W(SP, self.sV)
        self.W(SP, self.sP)
        self.I(self.s.dma_start(self.x1_d.ap().rearrange("c p t -> p c t"),
                                self.y[:, :, PADW:YW]), self.sS, 16)
        for t in range(1, 5):
            self.emit_ln_seg(t, "ln0")
        self.W(SP, self.sV)
        self.W(SP, self.sP)
        self.I(self.s.dma_start(self.cc_in.ap().rearrange("(c p) t -> p c t", p=128),
                                self.y[:, :, YW - HALO:YW]), self.sE, 16)
        self.W(PL, self.sE)
        self.I(self.g.collective_compute(
            "AllGather", ALU.bypass, replica_groups=[list(range(NCORES))],
            ins=[self.cc_in.ap().opt()], outs=[self.cc_out.ap().opt()]), self.sC)
        self.W(SP, self.sA)
        self.I(self.s.dma_start(self.masks[:],
                                self.mask_in.ap().rearrange("s p n -> p s n")),
               self.sS, 16)
        if self.stage == "ln0":
            self.emit_output()
            self.nc.all_engine_barrier()
            return
        self.I(self.g.dma_start(self.stg[0].ap(),
                                self.hap(self.d_wall, OB0, [[1, 1], [1, LBS]])),
               self.sX, 16)
        for L in range(self.n_layers):
            self.emit_layer_body(L)
        if self.stage == "full":
            self.emit_final()
        self.emit_output()
        self.nc.all_engine_barrier()


def build_nc(stage=STAGE, plan=None):
    nc = bass.Bass("TRN2", target_bir_lowering=False)
    mk = MK(nc, plan)
    b = B(nc, mk, stage)
    b.emit()
    return nc, mk


def _in_maps(inputs):
    shared = _prep_shared(inputs)
    skeys = ("wall", "vecsL", "gateB", "ident", "perm",
             "sstat", "e16", "ones2", "onesR")
    maps = []
    for c in range(NCORES):
        d = _prep_core_inputs(inputs, c)
        m = {k: d[k] for k in ("x_own", "x_halo", "rope", "mask_in",
                               "mask_out", "mask_lat")}
        for k in skeys:
            m[k] = shared[k]
        maps.append(m)
    return maps


def kernel(**inputs):
    maps = _in_maps(inputs)
    nc, mk = build_nc(STAGE, None)
    res = run_bass_kernel_spmd(nc, maps, core_ids=list(range(NCORES)))
    out = np.zeros((2, 8192, D), np.float32)
    for c in range(NCORES):
        seq, sl = divmod(c, 4)
        r = np.asarray(res.results[c]["y_out"])
        out[seq, OWN * sl:OWN * (sl + 1)] = r.reshape(D, OWN).T
    return out


# version 02
# speedup vs baseline: 1.0020x; 1.0020x over previous
import math
import numpy as np
import ml_dtypes
import concourse.bass as bass
import concourse.mybir as mybir
from concourse.bass_utils import run_bass_kernel_spmd
from concourse.ordered_engine_set import OrderedEngineSet

F32 = mybir.dt.float32
BF16 = mybir.dt.bfloat16
AF = mybir.ActivationFunctionType
ALU = mybir.AluOpType
ET = mybir.EngineType
ts = bass.ts
ds = bass.ds
BF16NP = ml_dtypes.bfloat16
AX = mybir.AxisListType.X

NCORES = 8
D = 1024
NCH = 8
OWN = 2048
HALO = 384
SEGW = 512
PADW = 8
YW = PADW + OWN
NLAYER = 6
DI = 2730
DIP = 2816
NGC = 22
WINB = 384
KWINW = 896
NLAT = 64
NVSLOT = 6
EPS = 1e-5

STAGE = "full"
DEBUG_SNAP = False
ALL_ENG = [ET.PE, ET.DVE, ET.Activation, ET.Pool, ET.SP]

_OUTER_SPEC = [("wq", D * D), ("wk", D * D), ("wv", D * D),
               ("wgq", D * 512), ("wgk", D * 512), ("wgv", D * 512),
               ("r_out_w", 512 * D), ("w_out", D * D)]
_BLOCK_SPEC = [("Wqkv", D * 3 * D), ("Wo", D * D), ("W1a", D * DIP),
               ("W1g", D * DIP), ("W2p", DIP * D)]
WOFF = {}
_o = 0
for _nm, _sz in _OUTER_SPEC:
    WOFF[_nm] = _o
    _o += _sz
OB0 = _o
LOFF = {}
_b = 0
for _nm, _sz in _BLOCK_SPEC:
    LOFF[_nm] = _b
    _b += _sz
LBS = _b
WALLN = OB0 + (NLAYER + 1) * LBS
PE, DVE, A, PL, SP = ET.PE, ET.DVE, ET.Activation, ET.Pool, ET.SP


def _fm(a):
    return np.ascontiguousarray(a.T.reshape(NCH, 128, -1))


def _rope_tables(base):
    pos = np.clip(np.arange(base - HALO, base + OWN), 0, 8191).astype(np.float64)
    half = np.arange(0, 64, 2, dtype=np.float64)
    inv_freq = 1.0 / 10000.0 ** (half / 64.0)
    freqs = pos[:, None] * inv_freq[None, :]
    freqs = np.concatenate([freqs, freqs], axis=1)
    base_scale = (half + 0.4 * 64.0) / (1.4 * 64.0)
    power = (pos - 4096.0) / 128.0
    scale = base_scale[None, :] ** power[:, None]
    scale = np.concatenate([scale, scale], axis=1)
    r8 = math.sqrt(8.0)
    tabs = np.stack([np.cos(freqs) * scale * r8, np.sin(freqs) * scale * r8,
                     np.cos(freqs) / scale * r8, np.sin(freqs) / scale * r8])
    tabs = np.tile(tabs.transpose(0, 2, 1), (1, 2, 1))
    return np.ascontiguousarray(tabs.astype(np.float32))


def _win_masks(outer, first_core):
    r = np.arange(128)[:, None]
    w = np.arange(WINB)[None, :]
    lo = r + 1 if outer else r
    ok = (w >= lo) & (w <= r + 256)
    m = np.zeros((3, 128, WINB), np.float32)
    NEG = np.float32(-3e38)
    m[0] = np.where(ok, 0.0, NEG)
    if first_core:
        m[1] = np.where(ok & (w >= 256), 0.0, NEG)
        m[2] = np.where(ok & (w >= 128), 0.0, NEG)
    else:
        m[1] = m[0]
        m[2] = m[0]
    return m


def _lat_masks(base):
    m = np.zeros((16, 128, NLAT), np.float32)
    NEG = np.float32(-3e38)
    n = np.arange(NLAT)[None, :]
    for s in range(16):
        m[s] = np.where(n <= (base + 128 * s) // 128, 0.0, NEG)
    return m


def _rot_perm():
    P = np.zeros((128, 128), np.float32)
    for g in range(2):
        o = 64 * g
        for dd in range(32):
            P[o + dd + 32, o + dd] = -1.0
            P[o + dd, o + dd + 32] = 1.0
    return P


def _vec_t(v, cols):
    out = np.zeros(128 * cols, np.float64)
    out[:v.shape[0]] = v
    return np.ascontiguousarray(out.reshape(cols, 128).T.astype(np.float32))


def _prep_core_inputs(inputs, c):
    seq, sl = divmod(c, 4)
    base = OWN * sl
    x = np.asarray(inputs["L0_attn_x"][seq], np.float32)
    d = {}
    d["x_own"] = _fm(x[base:base + OWN])
    xh = np.zeros((HALO, D), np.float32)
    if base > 0:
        xh[:] = x[base - HALO:base]
    d["x_halo"] = _fm(xh)
    d["rope"] = _rope_tables(base).astype(BF16NP)
    d["mask_in"] = _win_masks(False, sl == 0).astype(BF16NP)
    d["mask_out"] = _win_masks(True, sl == 0).astype(BF16NP)
    d["mask_lat"] = _lat_masks(base).astype(BF16NP)
    return d


def _prep_shared(inputs):
    f32 = np.float32
    d = {}
    wd = {}
    for k in ("wq", "wk", "wv", "wgq", "wgk", "wgv", "r_out_w", "w_out"):
        wd[k] = np.asarray(inputs["L0_attn_" + k], f32)
    wd["Wqkv"] = np.asarray(inputs["L0_ffn_Wqkv"], f32)
    wd["Wo"] = np.asarray(inputs["L0_ffn_Wo"], f32)
    W1 = np.asarray(inputs["L0_ffn_W1"], f32)
    W1a = np.zeros((NLAYER, D, DIP), f32); W1a[:, :, :DI] = W1[:, :, :DI]
    W1g = np.zeros((NLAYER, D, DIP), f32); W1g[:, :, :DI] = W1[:, :, DI:]
    wd["W1a"], wd["W1g"] = W1a, W1g
    W2 = np.asarray(inputs["L0_ffn_W2"], f32)
    W2p = np.zeros((NLAYER, DIP, D), f32); W2p[:, :DI] = W2
    wd["W2p"] = W2p
    wall = np.zeros((1, WALLN), f32)
    for nm, sz in _OUTER_SPEC:
        wall[0, WOFF[nm]:WOFF[nm] + sz] = wd[nm].reshape(-1)
    for L in range(NLAYER):
        b0 = OB0 + L * LBS
        for nm, sz in _BLOCK_SPEC:
            wall[0, b0 + LOFF[nm]:b0 + LOFF[nm] + sz] = wd[nm][L].reshape(-1)
    d["wall"] = wall
    b1 = np.asarray(inputs["L0_ffn_b1"], f32)
    b1a = np.zeros((NLAYER, DIP), f32); b1a[:, :DI] = b1[:, :DI]
    b1g = np.zeros((NLAYER, DIP), f32); b1g[:, :DI] = b1[:, DI:]
    vecs = np.zeros((NLAYER, 128, 70), f32)
    for i in range(NLAYER):
        vecs[i, :, 0:8] = _vec_t(np.asarray(inputs["L0_ffn_ln_g"][i], f32), 8)
        vecs[i, :, 8:16] = _vec_t(np.asarray(inputs["L0_ffn_ln_b"][i], f32), 8)
        vecs[i, :, 16:24] = _vec_t(np.asarray(inputs["L0_ffn_b2"][i], f32), 8)
        vecs[i, :, 24:46] = _vec_t(b1a[i], NGC)
        vecs[i, :, 46:68] = _vec_t(b1g[i], NGC)
        vecs[i, :, 68] = np.tile(np.asarray(inputs["L0_ffn_q_scale"][i], f32), 2)
        vecs[i, :, 69] = np.tile(np.asarray(inputs["L0_ffn_k_scale"][i], f32), 2)
    d["vecsL"] = np.ascontiguousarray(vecs)
    glw = np.asarray(inputs["L0_attn_gate_local_w"], f32)
    grw = np.asarray(inputs["L0_attn_gate_remote_w"], f32)
    gB = np.zeros((128, NCH, 48), f32)
    for h in range(16):
        fc, half = divmod(h, 2)
        gB[64 * half:64 * half + 64, fc, h] = glw[h]
        gB[64 * half:64 * half + 64, fc, 32 + h] = grw[h]
    d["gateB"] = np.ascontiguousarray(gB.astype(BF16NP))
    d["ident"] = np.eye(128, dtype=f32).astype(BF16NP)
    d["perm"] = _rot_perm().astype(BF16NP)
    sstat = np.zeros((128, NCH, 16), f32)
    for fc in range(NCH):
        sstat[0:64, fc, 2 * fc] = 1.0
        sstat[64:128, fc, 2 * fc + 1] = 1.0
    d["sstat"] = np.ascontiguousarray(sstat.astype(BF16NP))
    e16 = np.zeros((16, NCH, 128), f32)
    for fc in range(NCH):
        e16[2 * fc, fc, 0:64] = 1.0
        e16[2 * fc + 1, fc, 64:128] = 1.0
    d["e16"] = np.ascontiguousarray(e16.astype(BF16NP))
    d["ones2"] = np.ones((128, 2), f32).astype(BF16NP)
    d["onesR"] = np.ones((1, 128), f32).astype(BF16NP)
    return d


class Sem:
    __slots__ = ("name", "h", "c")

    def __init__(self, name, h):
        self.name, self.h, self.c = name, h, 0


class MK:
    def __init__(self, nc, plan=None):
        self.nc = nc
        self.pass2 = plan is not None
        self.plan = plan if plan is not None else {"deltas": {}, "marks": {}}
        self.sems = {}
        self.loopstack = []
        self.scratch = {}
        self.lastwait = {}

    def sem(self, name):
        s = Sem(name, self.nc.alloc_semaphore(name=name))
        self.sems[name] = s
        return s

    def eng(self, e):
        nc = self.nc
        return {PE: nc.tensor, DVE: nc.vector, A: nc.scalar,
                PL: nc.gpsimd, SP: nc.sync}[e]

    def wreg(self, e, idx):
        key = (e, idx)
        if key not in self.scratch:
            self.scratch[key] = self.eng(e).alloc_register(f"wr{idx}")
        return self.scratch[key]

    def inc(self, inst, sem, n=1):
        inst.then_inc(sem.h, n)
        sem.c += n
        return inst

    def wait(self, e, sem, val=None):
        if val is None:
            val = sem.c
        key = (e, sem.name)
        ctx = tuple(lid for lid, _ in self.loopstack)
        prev = self.lastwait.get(key)
        if prev is not None and prev[0] == ctx and prev[1] >= val:
            return
        self.lastwait[key] = (ctx, val)
        eng = self.eng(e)
        terms = []
        for (lid, lv) in self.loopstack:
            dl = self.plan["deltas"].get((lid, sem.name), 0) if self.pass2 else 0
            if dl:
                terms.append((lv, dl))
        if not terms:
            eng.wait_ge(sem.h, max(val, 0))
            return
        r = self.wreg(e, 0)
        eng.reg_mul(r, terms[0][0], terms[0][1])
        for lv, dl in terms[1:]:
            r2 = self.wreg(e, 1)
            eng.reg_mul(r2, lv, dl)
            eng.reg_alu(r, r, r2, ALU.add)
        eng.reg_add(r, r, val)
        if val < 0:
            eng.reg_alu(r, r, 0, ALU.max)
        eng.wait_ge(sem.h, r)

    def loop(self, n, engines, tag):
        return _Loop(self, n, engines, tag)


class _Loop:
    def __init__(self, mk, n, engines, tag):
        self.mk, self.n, self.lid = mk, n, tag
        self.engines = OrderedEngineSet(engines)

    def __enter__(self):
        mk = self.mk
        self._starts = {nm: s.c for nm, s in mk.sems.items()}
        self._fori = mk.nc.Fori(0, self.n, engines=self.engines)
        self.var = self._fori.__enter__()
        mk.loopstack.append((self.lid, self.var))
        return self.var

    def __exit__(self, a_, b_, c_):
        mk = self.mk
        mk.loopstack.pop()
        mk.lastwait.clear()
        self._fori.__exit__(a_, b_, c_)
        for nm, s in mk.sems.items():
            dl = s.c - self._starts[nm]
            key = (self.lid, nm)
            if not mk.pass2:
                mk.plan["deltas"][key] = dl
            else:
                assert mk.plan["deltas"].get(key, 0) == dl, (self.lid, nm)
            s.c = self._starts[nm] + self.n * dl
        return False


class Mem:
    def __init__(self, nc):
        self.nc = nc
        self.base = (nc.sbuf_base + 31) // 32 * 32
        self.top = self.base
        self.offs = {}

    def alloc(self, name, shape, dtype, offset=None):
        esz = mybir.dt.size(dtype)
        freesz = int(np.prod(shape[1:])) * esz
        if offset is None:
            offset = self.top
            self.top = offset + ((freesz + 31) // 32) * 32
        h = self.nc.alloc_sbuf_tensor_at(name, list(shape), dtype, offset=offset)
        self.offs[name] = offset
        return h

    def finish(self):
        self.nc.sbuf_base = self.top

    def at(self, name):
        return self.offs[name]


class B:
    def __init__(self, nc, mk, stage):
        self.nc, self.mk, self.stage = nc, mk, stage
        self.n_layers = {"l2": 2, "l4": 4, "full": NLAYER}.get(stage, NLAYER)
        self._declare_dram()
        self._declare_sbuf()
        s = mk.sem
        self.sG, self.sS, self.sP, self.sA, self.sV, self.sL, self.sC = (
            s("sG"), s("sS"), s("sP"), s("sA"), s("sV"), s("sL"), s("sC"))
        self.sX = s("sX")
        self.sE = s("sE")
        self.sI = nc.alloc_semaphore(name="sI")
        self.t, self.v, self.a, self.g, self.s = (nc.tensor, nc.vector,
                                                  nc.scalar, nc.gpsimd, nc.sync)
        self.r_ipid = self.g.alloc_register("ipid")
        self.r_im4 = self.g.alloc_register("im4")
        self.r_ioff = self.g.alloc_register("ioff")
        self.r_ioff2 = self.g.alloc_register("ioff2")
        self.psF = nc.alloc_psum_tensor("psF", [128, 6, 512], F32)
        self.psB = nc.alloc_psum_tensor("psB", [128, 2, 1024], BF16)

    def _declare_dram(self):
        f = self.nc.dram_tensor
        self.x_own = f("x_own", [NCH, 128, OWN], F32, kind="ExternalInput")
        self.x_halo = f("x_halo", [NCH, 128, HALO], F32, kind="ExternalInput")
        self.rope = f("rope", [4, 128, HALO + OWN], BF16, kind="ExternalInput")
        self.mask_in = f("mask_in", [3, 128, WINB], BF16, kind="ExternalInput")
        self.mask_out = f("mask_out", [3, 128, WINB], BF16, kind="ExternalInput")
        self.mask_lat = f("mask_lat", [16, 128, NLAT], BF16, kind="ExternalInput")
        self.d_wall = f("wall", [1, WALLN], F32, kind="ExternalInput")
        self.d_vecsL = f("vecsL", [NLAYER, 128, 70], F32, kind="ExternalInput")
        for k, shp in (("gateB", [128, NCH, 48]), ("ident", [128, 128]),
                       ("perm", [128, 128]), ("sstat", [128, NCH, 16]),
                       ("e16", [16, NCH, 128]), ("ones2", [128, 2]),
                       ("onesR", [1, 128])):
            setattr(self, "d_" + k, f(k, shp, BF16, kind="ExternalInput"))
        self.y_out = f("y_out", [NCH, 128, OWN], F32, kind="ExternalOutput")
        self.x1_d = f("x1_spill", [NCH, 128, OWN], F32, kind="Internal")
        self.cc_in = f("cc_in", [D, HALO], F32, kind="Internal")
        self.cc_out = f("cc_out", [NCORES * D, HALO], F32, kind="Internal")
        self.pl_in = f("pl_in", [D, 16], BF16, kind="Internal")
        self.pl_out = f("pl_out", [NCORES * D, 16], BF16, kind="Internal")
        self.stg = [f("stg0", [1, LBS], F32, kind="Internal"),
                    f("stg1", [1, LBS], F32, kind="Internal")]
        if DEBUG_SNAP:
            self.d_dbg = f("dbg_y", [NLAYER * D, OWN], F32, kind="Internal")

    def _declare_sbuf(self):
        mem = Mem(self.nc)
        self.mem = mem
        self.y = mem.alloc("y", [128, NCH, YW], F32)
        self.kwin = mem.alloc("kwin", [128, NCH, KWINW], BF16)
        self.vt = mem.alloc("vt", [128, NVSLOT, 1024], BF16)
        self.tabs = mem.alloc("tabs", [128, 4, SEGW], BF16)
        self.qsb = mem.alloc("qsb", [128, NCH, SEGW], BF16)
        self.ao = mem.alloc("ao", [128, NCH, SEGW], BF16)
        self.xop = mem.alloc("xop", [128, NCH, SEGW], BF16)
        self.wsl = mem.alloc("wsl", [128, 2, NCH, 128], BF16)
        self.Wq_sl = mem.alloc("Wq_sl", [128, NCH, D], BF16)
        self.Wk_sl = mem.alloc("Wk_sl", [128, NCH, D], BF16)
        self.Wv_sl = mem.alloc("Wv_sl", [128, NCH, D], BF16)
        self.masks = mem.alloc("masks", [128, 3, WINB], BF16)
        self.masklat = mem.alloc("masklat", [128, 16, NLAT], BF16)
        self.ident = mem.alloc("ident", [128, 128], BF16)
        self.perm = mem.alloc("perm", [128, 128], BF16)
        self.sstat = mem.alloc("sstat", [128, NCH, 16], BF16)
        self.e16 = mem.alloc("e16", [16, NCH, 128], BF16)
        self.ones2 = mem.alloc("ones2", [128, 2], BF16)
        self.onesR = mem.alloc("onesR", [1, 128], BF16)
        self.vecs = mem.alloc("vecs", [128, 70], F32)
        self.csts = mem.alloc("csts", [128, 2], F32)
        self.sq = mem.alloc("sq", [128, 2, SEGW], BF16)
        self.attst = mem.alloc("attst", [128, 2, 8], F32)
        self.pooled = mem.alloc("pooled", [128, NCH, 16], F32)
        self.pooledB = mem.alloc("pooledB", [128, NCH, 16], BF16)
        self.latGK = mem.alloc("latGK", [128, 4, NLAT], BF16)
        self.latGV = mem.alloc("latGV", [128, 4, NLAT], BF16)
        self.latGVT = mem.alloc("latGVT", [64, 8, NLAT], BF16)
        self.T1 = mem.alloc("T1", [128, 2, WINB], BF16)
        self.PT = mem.alloc("PT", [128, 2, WINB], BF16)
        self.latscr = mem.alloc("latscr", [128, 2, NLAT], BF16)
        self.latscrT = mem.alloc("latscrT", [64, 2, 128], BF16)
        self.gateB = mem.alloc("gateB", [128, NCH, 48], BF16)
        self.lrows = mem.alloc("lrows", [1, 2560], F32)
        self.rstdF = mem.alloc("rstdF", [16, SEGW], F32)
        self.rstdH = mem.alloc("rstdH", [16, 2, SEGW], BF16)
        self.rem = mem.alloc("rem", [128, 4, SEGW], BF16)
        lro = mem.at("lrows")
        self.lrhl = mem.alloc("lrhl", [1, 2048], BF16, offset=lro + 6144)
        self.gq = mem.alloc("gq", [128, 4, SEGW], BF16, offset=mem.at("tabs"))
        self.qnF = mem.alloc("qnF", [128, SEGW], F32, offset=mem.at("T1"))
        self.xn = mem.alloc("xn", [128, NCH, OWN], BF16, offset=mem.at("Wq_sl"))
        assert mem.at("Wk_sl") == mem.at("Wq_sl") + 16384
        self.gates = mem.alloc("gates", [48, SEGW], F32, offset=lro)
        self.grbf = mem.alloc("grbf", [16, SEGW], BF16, offset=lro + 2048)
        self.glT = mem.alloc("glT", [128, 4, 16], F32, offset=lro + 3072)
        self.grB_sb = mem.alloc("grB_sb", [128, SEGW], BF16, offset=lro + 3584)
        self.glbf = mem.alloc("glbf", [16, SEGW], BF16, offset=lro + 4608)
        self.t1mix = mem.alloc("t1mix", [128, SEGW], BF16, offset=mem.at("sq"))
        self.latPF = mem.alloc("latPF", [128, NCH, NLAT], BF16, offset=mem.at("rem"))
        self.x1t = mem.alloc("x1t", [128, NCH, SEGW], F32, offset=mem.at("Wv_sl"))
        gsrc = [("Wv_sl", 0), ("Wv_sl", 4096), ("Wv_sl", 8192), ("Wv_sl", 12288),
                ("kwin", 0), ("kwin", 4096), ("kwin", 8192),
                ("vt", 0), ("vt", 4096), ("vt", 8192),
                ("qsb", 0), ("qsb", 4096), ("ao", 0), ("ao", 4096),
                ("xop", 0), ("xop", 4096), ("tabs", 0), ("rem", 0),
                ("T1", 0), ("rstdF", 0), ("lrows", 0), ("lrows", 4096)]
        assert mem.at("PT") == mem.at("T1") + 1536
        assert mem.at("gateB") == mem.at("T1") + 3840
        assert mem.at("rstdH") == mem.at("rstdF") + 2048
        self.gt = [mem.alloc(f"g{i}", [128, 4, SEGW], BF16,
                             offset=mem.at(nm) + off)
                   for i, (nm, off) in enumerate(gsrc)]
        mem.finish()
        assert mem.top <= self.nc.sbuf_top - 1024, (
            f"SBUF over budget: {mem.top} vs {self.nc.sbuf_top}")

    def W(self, e, sem, val=None):
        self.mk.wait(e, sem, val)

    def I(self, inst, sem, n=1):
        return self.mk.inc(inst, sem, n)

    @staticmethod
    def ycols(t):
        return PADW + SEGW * (t - 1)

    @staticmethod
    def hap(handle, off, dims):
        return bass.AP(handle, off, dims)

    def init_loads(self):
        s, g = self.s, self.g
        self.I(s.dma_start(self.y[:, :, PADW:YW],
                           self.x_own.ap().rearrange("c p t -> p c t")), self.sS, 16)
        for src, dst in ((self.d_ident, self.ident), (self.d_perm, self.perm),
                         (self.d_sstat, self.sstat), (self.d_e16, self.e16),
                         (self.d_ones2, self.ones2), (self.d_onesR, self.onesR),
                         (self.d_gateB, self.gateB)):
            self.I(s.dma_start(dst[:], src.ap()), self.sS, 16)
        self.I(s.dma_start(self.masklat[:],
                           self.mask_lat.ap().rearrange("s p n -> p s n")), self.sS, 16)
        self.I(s.dma_start(self.masks[:],
                           self.mask_out.ap().rearrange("s p n -> p s n")), self.sS, 16)
        for w, dst in (("wq", self.Wq_sl), ("wk", self.Wk_sl),
                       ("wv", self.Wv_sl)):
            self.I(g.dma_start(dst[:],
                               self.hap(self.d_wall, WOFF[w],
                                        [[D, 128], [128 * D, NCH], [1, D]])),
                   self.sG, 16)
        self.I(g.memset(self.xop[:], 0.0), self.sL)
        self.I(g.memset(self.y[:, :, 0:PADW], 0.0), self.sL)
        self.I(g.memset(self.csts[:, 0:1], EPS), self.sL)
        self.I(g.memset(self.csts[:, 1:2], 1e-24), self.sL)
        g.reg_load(self.r_ipid, self.nc.partition_id_tensor.ap())
        g.reg_mod(self.r_im4, self.r_ipid, 4)
        g.reg_add(self.r_ioff, self.r_ipid, NCORES - 1)
        g.reg_mod(self.r_ioff, self.r_ioff, NCORES)
        g.reg_mul(self.r_ioff, self.r_ioff, D * HALO)
        g.reg_add(self.r_ioff2, self.r_ioff, 4 * 128 * HALO)
        self.impAP0 = bass.AP(self.cc_out, self.r_ioff,
                              [[HALO, 128], [128 * HALO, 4], [1, HALO]])
        self.impAP1 = bass.AP(self.cc_out, self.r_ioff2,
                              [[HALO, 128], [128 * HALO, 4], [1, HALO]])

    def emit_ln_seg(self, t, mode):
        tt, v, a, g = self.t, self.v, self.a, self.g
        psF = self.psF
        sP, sA, sV, sL, sG, sS = self.sP, self.sA, self.sV, self.sL, self.sG, self.sS
        Wd = SEGW if t > 0 else HALO
        halo = (t == 0)
        c0 = self.ycols(t) if t > 0 else None
        if mode == "ffn":
            rawap = lambda kc: self.xn[:, kc, SEGW * (t - 1):SEGW * (t - 1) + Wd]
        else:
            rawap = lambda kc: self.xop[:, kc, 0:Wd]
        if halo:
            self.W(PL, self.sP)
            self.I(g.dma_start(self.xop[:, :, 0:HALO],
                               self.x_halo.ap().rearrange("c p t -> p c t")),
                   self.sG, 16)
            self.W(PE, sG)
            self.W(PL, sG)
        else:
            self.W(A, sP)
            self.W(A, sV)
            self.W(A, sS)
            self.W(A, sL)
            if mode == "ffn":
                self.I(a.copy(self.xn[:, 0:4, SEGW * (t - 1):SEGW * t],
                              self.y[:, 0:4, c0:c0 + Wd]), sA)
                self.I(a.copy(self.xn[:, 4:8, SEGW * (t - 1):SEGW * t],
                              self.y[:, 4:8, c0 - 1:c0 - 1 + Wd]), sA)
            else:
                self.I(a.copy(self.xop[:, :, 0:Wd], self.y[:, :, c0:c0 + Wd]), sA)
            self.W(PE, sA)
            self.W(PL, sA)
        for kc in range(NCH):
            par = kc & 1
            self.W(PL, sP)
            self.I(g.tensor_tensor(self.sq[:, par, 0:Wd], rawap(kc), rawap(kc),
                                   ALU.mult), sL)
            self.I(tt.matmul(psF[0:1, 0, 0:Wd], self.ones2[:, 0:1], rawap(kc),
                             start=(kc == 0), stop=(kc == NCH - 1)), self.sP)
            self.W(PE, sL)
            self.I(tt.matmul(psF[0:1, 1, 0:Wd], self.ones2[:, 1:2],
                             self.sq[:, par, 0:Wd],
                             start=(kc == 0), stop=(kc == NCH - 1)), self.sP)
        lr = self.lrows
        s0, s1, s2 = lr[0:1, 0:Wd], lr[0:1, 512:512 + Wd], lr[0:1, 1024:1024 + Wd]
        self.W(DVE, sP)
        self.I(v.tensor_scalar_mul(s0, psF[0:1, 0, 0:Wd], 1.0 / D), sV)
        self.W(A, sV)
        self.I(a.activation(s1, s0, AF.Square), sA)
        self.W(DVE, sA)
        self.I(v.scalar_tensor_tensor(s2, psF[0:1, 1, 0:Wd], 1.0 / D, s1,
                                      ALU.mult, ALU.subtract), sV)
        self.W(A, sV)
        self.W(A, self.sL)
        self.I(a.activation(s2, s2, AF.Ln, bias=self.csts[0:1, 0:1]), sA)
        self.W(A, sA)
        self.I(a.activation(s2, s2, AF.Exp, scale=-0.5), sA)
        self.W(DVE, sA)
        self.I(v.scalar_tensor_tensor(s1, s0, -1.0, s2, ALU.mult, ALU.mult), sV)
        hb = self.lrhl
        self.W(A, sV)
        self.I(a.copy(hb[0:1, 0:Wd], s2), sA)
        self.W(DVE, sA)
        self.I(v.tensor_sub(hb[0:1, 512:512 + Wd], s2, hb[0:1, 0:Wd]), sV)
        self.W(A, sV)
        self.I(a.copy(hb[0:1, 1024:1024 + Wd], s1), sA)
        self.W(DVE, sA)
        self.I(v.tensor_sub(hb[0:1, 1536:1536 + Wd], s1, hb[0:1, 1024:1024 + Wd]), sV)
        self.W(PE, sV)
        self.I(tt.matmul(psF[:, 2, 0:Wd], self.onesR[0:1, :], hb[0:1, 0:Wd],
                         start=True, stop=False), self.sP)
        self.I(tt.matmul(psF[:, 2, 0:Wd], self.onesR[0:1, :], hb[0:1, 512:512 + Wd],
                         start=False, stop=True), self.sP)
        self.I(tt.matmul(psF[:, 3, 0:Wd], self.onesR[0:1, :], hb[0:1, 1024:1024 + Wd],
                         start=True, stop=False), self.sP)
        self.I(tt.matmul(psF[:, 3, 0:Wd], self.onesR[0:1, :], hb[0:1, 1536:1536 + Wd],
                         start=False, stop=True), self.sP)
        for kc in range(NCH):
            if mode == "ln0":
                src = self.y[:, kc, c0:c0 + Wd]
                self.W(DVE, sP)
                self.W(DVE, sV)
                self.I(v.scalar_tensor_tensor(self.qnF[:, 0:Wd], src, 1.0,
                                              psF[:, 3, 0:Wd], ALU.mult, ALU.add), sV)
                self.W(DVE, sV)
                self.I(v.tensor_tensor(src, self.qnF[:, 0:Wd], psF[:, 2, 0:Wd],
                                       ALU.mult), sV)
            elif mode == "outer_attn":
                src = rawap(kc) if halo else self.y[:, kc, c0:c0 + Wd]
                self.W(DVE, sP)
                self.W(DVE, sV)
                self.I(v.scalar_tensor_tensor(self.qnF[:, 0:Wd], src, 1.0,
                                              psF[:, 3, 0:Wd], ALU.mult, ALU.add), sV)
                self.W(DVE, sV)
                self.I(v.tensor_tensor(rawap(kc), self.qnF[:, 0:Wd],
                                       psF[:, 2, 0:Wd], ALU.mult), sV)
            else:
                ysrc = (self.y[:, kc, c0:c0 + Wd] if kc < 4
                        else self.y[:, kc, c0 - 1:c0 - 1 + Wd])
                self.W(DVE, sP)
                self.W(DVE, sV)
                self.I(v.scalar_tensor_tensor(self.qnF[:, 0:Wd], ysrc, 1.0,
                                              psF[:, 3, 0:Wd], ALU.mult, ALU.add), sV)
                self.W(DVE, sV)
                self.W(DVE, sS)
                self.I(v.scalar_tensor_tensor(self.qnF[:, 0:Wd], self.qnF[:, 0:Wd],
                                              self.vecs[:, kc:kc + 1],
                                              psF[:, 2, 0:Wd], ALU.mult, ALU.mult), sV)
                self.W(DVE, sV)
                self.I(v.tensor_scalar_add(rawap(kc), self.qnF[:, 0:Wd],
                                           self.vecs[:, 8 + kc:9 + kc]), sV)

    def emit_qk_side(self, t, side):
        tt, v, a, g = self.t, self.v, self.a, self.g
        psF = self.psF
        sP, sA, sV, sL, sG, sS = self.sP, self.sA, self.sV, self.sL, self.sG, self.sS
        Wd = SEGW if t > 0 else HALO
        Wsl = self.Wk_sl if side == "k" else self.Wq_sl
        if side == "k":
            cur0 = 0 if t == 0 else HALO
            dst = lambda fc: self.kwin[:, fc, cur0:cur0 + Wd]
        else:
            dst = lambda fc: self.qsb[:, fc, 0:Wd]
        for fc in range(NCH):
            par = fc & 1
            self.W(PE, sA)
            self.W(PE, sV)
            self.W(PE, sG)
            self.W(PE, sL)
            last = None
            for kc in range(NCH):
                last = tt.matmul(psF[:, par, 0:Wd], Wsl[:, kc, ts(fc, 128)],
                                 self.xop[:, kc, 0:Wd],
                                 start=(kc == 0), stop=(kc == NCH - 1))
            self.I(last, sP)
            self.W(A, sP)
            self.W(A, sL)
            self.I(a.copy(dst(fc), psF[:, par, 0:Wd]), sA)
            self.W(PL, sA)
            self.W(PL, sP)
            self.I(g.tensor_tensor(self.sq[:, par, 0:Wd], dst(fc), dst(fc),
                                   ALU.mult), sL)
            self.W(PE, sL)
            self.I(tt.matmul(psF[0:16, 2, 0:Wd], self.sstat[:, fc, :],
                             self.sq[:, par, 0:Wd], start=(fc == 0),
                             stop=(fc == NCH - 1)), sP)
        self.W(A, sP)
        self.W(A, self.sL)
        self.I(a.activation(self.rstdF[:, 0:Wd], psF[0:16, 2, 0:Wd], AF.Ln,
                            bias=self.csts[0:16, 1:2]), sA)
        self.W(A, sA)
        self.I(a.activation(self.rstdF[:, 0:Wd], self.rstdF[:, 0:Wd], AF.Exp,
                            scale=-0.5), sA)
        self.W(A, sA)
        self.I(a.copy(self.rstdH[:, 0, 0:Wd], self.rstdF[:, 0:Wd]), sA)
        self.W(DVE, sA)
        self.I(v.tensor_sub(self.rstdH[:, 1, 0:Wd], self.rstdF[:, 0:Wd],
                            self.rstdH[:, 0, 0:Wd]), sV)
        scol = 68 if side == "q" else 69
        tb = 0 if side == "q" else 2
        for fc in range(NCH):
            par = fc & 1
            self.W(PE, sV)
            self.W(PE, sA)
            self.I(tt.matmul(psF[:, 5, 0:Wd], self.e16[:, fc, :],
                             self.rstdH[:, 0, 0:Wd], start=True, stop=False), sP)
            self.I(tt.matmul(psF[:, 5, 0:Wd], self.e16[:, fc, :],
                             self.rstdH[:, 1, 0:Wd], start=False, stop=True), sP)
            self.W(DVE, sP)
            self.W(DVE, sL)
            self.W(DVE, sS)
            self.I(v.scalar_tensor_tensor(self.qnF[:, 0:Wd], dst(fc),
                                          self.vecs[:, scol:scol + 1],
                                          psF[:, 5, 0:Wd], ALU.mult, ALU.mult), sV)
            self.W(DVE, sV)
            self.W(DVE, sS)
            self.I(v.tensor_tensor(self.sq[:, 0, 0:Wd], self.qnF[:, 0:Wd],
                                   self.tabs[:, tb + 1, 0:Wd], ALU.mult), sV)
            self.W(PL, sV)
            self.I(g.tensor_tensor(self.sq[:, 1, 0:Wd], self.qnF[:, 0:Wd],
                                   self.tabs[:, tb, 0:Wd], ALU.mult), sL)
            self.W(PE, sV)
            self.W(PE, sL)
            self.W(PE, sA)
            self.I(tt.matmul(psF[:, 3 + par, 0:Wd], self.perm[:],
                             self.sq[:, 0, 0:Wd], start=True, stop=False), sP)
            self.I(tt.matmul(psF[:, 3 + par, 0:Wd], self.ident[:],
                             self.sq[:, 1, 0:Wd], start=False, stop=True), sP)
            self.W(A, sP)
            self.I(a.copy(dst(fc), psF[:, 3 + par, 0:Wd]), sA)

    def emit_qk_outer(self, t, side):
        tt, a = self.t, self.a
        psF = self.psF
        sP, sA, sG, sV = self.sP, self.sA, self.sG, self.sV
        Wd = SEGW if t > 0 else HALO
        Wsl = self.Wk_sl if side == "k" else self.Wq_sl
        if side == "k":
            cur0 = 0 if t == 0 else HALO
            dst = lambda fc: self.kwin[:, fc, cur0:cur0 + Wd]
        else:
            dst = lambda fc: self.qsb[:, fc, 0:Wd]
        for fc in range(NCH):
            par = fc & 1
            self.W(PE, sA)
            self.W(PE, sV)
            self.W(PE, sG)
            self.W(PE, self.sL)
            last = None
            for kc in range(NCH):
                last = tt.matmul(psF[:, par, 0:Wd], Wsl[:, kc, ts(fc, 128)],
                                 self.xop[:, kc, 0:Wd],
                                 start=(kc == 0), stop=(kc == NCH - 1))
            self.I(last, sP)
            self.W(A, sP)
            self.I(a.activation(dst(fc), psF[:, par, 0:Wd], AF.Copy,
                                scale=(0.125 if side == "k" else 1.0)), sA)

    def emit_v(self, t):
        tt, a = self.t, self.a
        psF = self.psF
        sP, sA = self.sP, self.sA
        Wd = SEGW if t > 0 else HALO
        ntc = Wd // 128
        g0 = 0 if t == 0 else 3 + 4 * (t - 1)
        for tc in range(ntc):
            slot = (g0 + tc) % NVSLOT
            for half in range(2):
                par = (tc * 2 + half) & 1
                self.W(PE, sA)
                last = None
                for kc in range(NCH):
                    last = tt.matmul(psF[:, 3 + par, 0:512],
                                     self.xop[:, kc, ts(tc, 128)],
                                     self.Wv_sl[:, kc, ts(half, 512)],
                                     start=(kc == 0), stop=(kc == NCH - 1))
                self.I(last, sP)
                self.W(A, sP)
                self.I(a.copy(self.vt[:, slot, ts(half, 512)],
                              psF[:, 3 + par, 0:512]), sA)

    def emit_attn_subtile(self, t, sl, outer, tag):
        mk = self.mk
        tt, v, a, g = self.t, self.v, self.a, self.g
        psF, psB = self.psF, self.psB
        sP, sA, sV, sL = self.sP, self.sA, self.sV, self.sL
        if t == 0:
            q0c, w0, gsub, msl = 256, 0, None, 0
        else:
            q0c = 128 * sl
            w0 = HALO + 128 * sl - 256
            gsub = 4 * (t - 1) + sl
            msl = 1 if (t == 1 and sl == 0) else (2 if (t == 1 and sl == 1) else 0)
        s_glob = (3 + 4 * (t - 1) + sl) if t > 0 else 2
        cslots = [((s_glob - 2 + i) % NVSLOT) for i in range(3)]
        sA0, sV0, sP0 = sA.c, sV.c, sP.c
        v_stt, a_exp, a_pt, p_S, p_tr = {}, {}, {}, {}, {}

        def emit_S(hv):
            fcv, half = divmod(hv, 2)
            par = hv & 1
            qap = self.qsb[64 * half:64 * half + 64, fcv, q0c:q0c + 128]
            kap = self.kwin[64 * half:64 * half + 64, fcv, w0:w0 + WINB]
            self.W(PE, sV, v_stt.get(hv - 2, sV0))
            self.W(PE, sA, sA0)
            self.I(tt.matmul(psF[:, par, 0:WINB], qap, kap, start=True,
                             stop=True, skip_group_check=True), sP)
            p_S[hv] = sP.c

        emit_S(0)
        for hv in range(16):
            if hv + 1 < 16:
                emit_S(hv + 1)
            fcv, half = divmod(hv, 2)
            par = hv & 1
            self.W(DVE, sP, max(p_S[hv], p_tr.get(hv - 2, sP0)))
            self.W(DVE, sA, a_exp.get(hv - 2, sA0))
            self.I(v.scalar_tensor_tensor(self.T1[:, par, :], psF[:, par, 0:WINB],
                                          1.0, self.masks[:, msl, :],
                                          ALU.mult, ALU.add), sV)
            v_stt[hv] = sV.c
            self.W(DVE, sV)
            self.I(v.reduce_max(self.attst[:, par, 0:1], self.T1[:, par, :],
                                axis=AX, negate=True), sV)
            self.W(A, sV)
            self.I(a.activation(self.T1[:, par, :], self.T1[:, par, :], AF.Exp,
                                bias=self.attst[:, par, 0:1],
                                accum_out=self.attst[:, par, 1:2]), sA)
            a_exp[hv] = sA.c
            self.W(DVE, sA)
            self.I(v.reciprocal(self.attst[:, par, 2:3],
                                self.attst[:, par, 1:2]), sV)
            self.W(DVE, sV)
            if outer:
                self.I(v.tensor_scalar(self.T1[:, par, :], self.T1[:, par, :],
                                       self.attst[:, par, 2:3],
                                       self.glT[:, sl, hv:hv + 1],
                                       ALU.mult, ALU.mult), sV)
            else:
                self.I(v.tensor_scalar_mul(self.T1[:, par, :],
                                           self.T1[:, par, :],
                                           self.attst[:, par, 2:3]), sV)
            v_scale = sV.c
            self.W(PE, sV, v_scale)
            self.W(PE, sA, a_pt.get(hv - 2, sA0))
            for c in range(3):
                self.I(tt.transpose(psB[:, par, ts(c, 128)],
                                    self.T1[:, par, ts(c, 128)],
                                    self.ident[:]), sP)
            p_tr[hv] = sP.c
            self.W(A, sP, p_tr[hv])
            self.I(a.copy(self.PT[:, par, :], psB[:, par, 0:WINB]), sA)
            a_pt[hv] = sA.c
            self.W(PE, sA, a_pt[hv])
            last = None
            for c in range(3):
                vap = self.vt[:, cslots[c], ts(hv, 64)]
                last = tt.matmul(psF[0:64, 2 + par, 0:128], vap,
                                 self.PT[:, par, ts(c, 128)],
                                 start=(c == 0), stop=(c == 2),
                                 skip_group_check=True)
            self.I(last, sP)
            self.W(A, sP)
            self.I(a.copy(self.ao[64 * half:64 * half + 64, fcv,
                                  q0c:q0c + 128], psF[0:64, 2 + par, 0:128]), sA)
        if outer and t > 0:
            self.emit_latent_pair(t, sl, gsub)

    def emit_latent_pair(self, t, sl, gsub):
        tt, v, a, g = self.t, self.v, self.a, self.g
        psF, psB = self.psF, self.psB
        sP, sA, sV, sL = self.sP, self.sA, self.sV, self.sL
        q0c = 128 * sl
        for lh in range(8):
            j, w = divmod(lh, 2)
            lr = slice(64 * w, 64 * w + 64)
            self.W(PE, sA)
            self.W(PE, sV)
            self.I(tt.matmul(psF[:, 4, 0:NLAT],
                             self.gq[lr, j, q0c:q0c + 128],
                             self.latGK[lr, j, :],
                             start=True, stop=True, skip_group_check=True), sP)
            self.W(DVE, sP)
            self.W(DVE, sA)
            self.I(v.scalar_tensor_tensor(self.latscr[:, w, :], psF[:, 4, 0:NLAT],
                                          1.0, self.masklat[:, gsub, :],
                                          ALU.mult, ALU.add), sV)
            self.W(DVE, sV)
            self.I(v.reduce_max(self.attst[:, w, 4:5], self.latscr[:, w, :],
                                axis=AX, negate=True), sV)
            self.W(A, sV)
            self.I(a.activation(self.latscr[:, w, :], self.latscr[:, w, :], AF.Exp,
                                bias=self.attst[:, w, 4:5],
                                accum_out=self.attst[:, w, 5:6]), sA)
            self.W(DVE, sA)
            self.I(v.reciprocal(self.attst[:, w, 6:7], self.attst[:, w, 5:6]), sV)
            self.W(DVE, sV)
            self.I(v.tensor_scalar_mul(self.latscr[:, w, :], self.latscr[:, w, :],
                                       self.attst[:, w, 6:7]), sV)
            self.W(PE, sV)
            self.W(PE, sA)
            self.I(tt.transpose(psB[0:64, w, 384:512], self.latscr[:, w, :],
                                self.ident[:]), sP)
            self.W(A, sP)
            self.I(a.copy(self.latscrT[:, w, :], psB[0:64, w, 384:512]), sA)
            self.W(PE, sA)
            self.I(tt.matmul(psF[0:64, 5, 0:128], self.latGVT[:, lh, :],
                             self.latscrT[:, w, :], start=True, stop=True,
                             skip_group_check=True), sP)
            self.W(A, sP)
            self.I(a.copy(self.rem[lr, ds(j, 1), q0c:q0c + 128],
                          psF[0:64, 5, 0:128]), sA)

    def emit_halo_proj(self, Lv):
        tt, v, g = self.t, self.v, self.g
        psF = self.psF
        sP, sA, sV, sG = self.sP, self.sA, self.sV, self.sG
        for fc in range(NCH):
            par = fc & 1
            self.W(PL, sP)
            src = self.hap(self.stg[Lv], LOFF["Wo"] + fc * 128,
                           [[D, 128], [128 * D, NCH], [1, 128]])
            self.I(g.dma_start(self.wsl[:, par, :, :], src), sG, 16)
            self.W(PE, sG)
            self.W(PE, sV)
            self.W(PE, sA)
            last = None
            for kc in range(NCH):
                last = tt.matmul(psF[:, par, 0:1], self.wsl[:, par, kc, :],
                                 self.ao[:, kc, 383:384],
                                 start=(kc == 0), stop=(kc == NCH - 1))
            self.I(last, sP)
            self.W(A, sP)
            self.W(A, sV)
            self.I(self.a.copy(self.pooled[:, fc, 0:1], psF[:, par, 0:1]), sA)

    def emit_halo_add(self):
        self.W(DVE, self.sA)
        self.W(DVE, self.sV)
        self.I(self.v.tensor_tensor(self.y[:, :, PADW - 1:PADW],
                                    self.y[:, :, PADW - 1:PADW],
                                    self.pooled[:, :, 0:1], ALU.add), self.sV)

    def emit_out_proj(self, t, Lv, outer):
        tt, v, a, g = self.t, self.v, self.a, self.g
        psF = self.psF
        sP, sA, sV, sG = self.sP, self.sA, self.sV, self.sG
        Wd = SEGW
        c0 = self.ycols(t)
        if outer:
            for fc in range(NCH):
                par = fc & 1
                self.W(PL, sP)
                self.I(g.dma_start(
                    self.wsl[:, par, 0:4, :],
                    self.hap(self.d_wall, WOFF["r_out_w"] + fc * 128,
                             [[D, 128], [128 * D, 4], [1, 128]])), sG, 16)
                self.W(PE, sG)
                self.W(PE, sV)
                self.W(PE, sA)
                last = None
                for kc in range(4):
                    last = tt.matmul(psF[:, 2 + par, 0:Wd], self.wsl[:, par, kc, :],
                                     self.rem[:, kc, 0:Wd],
                                     start=(kc == 0), stop=(kc == 3))
                self.I(last, sP)
                self.I(tt.matmul(psF[:, 4 + par, 0:Wd], self.e16[:, fc, :],
                                 self.grbf[:, 0:Wd], start=True, stop=True,
                                 skip_group_check=True), sP)
                self.W(A, sP)
                self.I(a.copy(self.grB_sb[:, 0:Wd], psF[:, 4 + par, 0:Wd]), sA)
                self.W(DVE, sA)
                self.I(v.tensor_tensor(self.t1mix[:, 0:Wd], psF[:, 2 + par, 0:Wd],
                                       self.grB_sb[:, 0:Wd], ALU.mult), sV)
                self.W(DVE, sV)
                self.I(v.tensor_tensor(self.ao[:, fc, 0:Wd], self.t1mix[:, 0:Wd],
                                       self.ao[:, fc, 0:Wd], ALU.add), sV)
        for fc in range(NCH):
            par = fc & 1
            self.W(PL, sP)
            if outer:
                src = self.hap(self.d_wall, WOFF["w_out"] + fc * 128,
                               [[D, 128], [128 * D, NCH], [1, 128]])
            else:
                src = self.hap(self.stg[Lv], LOFF["Wo"] + fc * 128,
                               [[D, 128], [128 * D, NCH], [1, 128]])
            self.I(g.dma_start(self.wsl[:, par, :, :], src), sG, 16)
            self.W(PE, sG)
            self.W(PE, sV)
            self.W(PE, sA)
            last = None
            for kc in range(NCH):
                last = tt.matmul(psF[:, par, 0:Wd], self.wsl[:, par, kc, :],
                                 self.ao[:, kc, 0:Wd],
                                 start=(kc == 0), stop=(kc == NCH - 1))
            self.I(last, sP)
            self.W(DVE, sP)
            self.I(v.tensor_tensor(self.y[:, fc, c0:c0 + Wd],
                                   self.y[:, fc, c0:c0 + Wd],
                                   psF[:, par, 0:Wd], ALU.add), sV)

    def emit_gates(self, t):
        tt, v, a = self.t, self.v, self.a
        psF, psB = self.psF, self.psB
        sP, sA, sV = self.sP, self.sA, self.sV
        Wd = SEGW
        for fc in range(NCH):
            self.W(PE, sA)
            self.I(tt.matmul(psF[0:48, 5, 0:Wd], self.gateB[:, fc, :],
                             self.qsb[:, fc, 0:Wd], start=(fc == 0),
                             stop=(fc == NCH - 1)), sP)
        self.W(A, sP)
        self.I(a.activation(self.gates[:, 0:Wd], psF[0:48, 5, 0:Wd], AF.Exp,
                            scale=-1.0), sA)
        self.W(DVE, sA)
        self.I(v.tensor_scalar_add(self.gates[:, 0:Wd], self.gates[:, 0:Wd], 1.0), sV)
        self.W(DVE, sV)
        self.I(v.reciprocal(self.gates[:, 0:Wd], self.gates[:, 0:Wd]), sV)
        self.W(A, sV)
        self.I(a.copy(self.grbf[:, 0:Wd], self.gates[32:48, 0:Wd]), sA)
        self.W(A, sA)
        self.I(a.copy(self.glbf[:, 0:Wd], self.gates[0:16, 0:Wd]), sA)
        for sl in range(4):
            self.W(PE, sA)
            self.I(tt.transpose(psB[0:128, 0, 512:528],
                                self.glbf[0:16, ts(sl, 128)], self.ident[0:16, 0:16]), sP)
            self.W(A, sP)
            self.I(a.copy(self.glT[:, sl, :], psB[0:128, 0, 512:528]), sA)

    def emit_latent_prep(self):
        nc = self.nc
        tt, v, a, g, s = self.t, self.v, self.a, self.g, self.s
        psF, psB = self.psF, self.psB
        sP, sA, sV, sL, sG, sS, sC = (self.sP, self.sA, self.sV, self.sL,
                                      self.sG, self.sS, self.sC)
        for t in range(1, 5):
            self.emit_ln_seg(t, "outer_attn")
            for kc in range(NCH):
                self.W(DVE, sV)
                self.I(v.reduce_sum(
                    self.pooled[:, kc, ts(t - 1, 4)],
                    self.xop[:, kc, 0:SEGW].rearrange("p (l w) -> p l w", w=128),
                    axis=AX), sV)
        self.W(DVE, sV)
        self.I(v.tensor_scalar_mul(self.pooledB[:], self.pooled[:], 1.0 / 128.0), sV)
        self.W(SP, sV)
        self.I(s.dma_start(self.pl_in.ap().rearrange("(c p) l -> p c l", p=128),
                           self.pooledB[:]), self.sE, 16)
        self.W(PL, self.sE)
        self.I(g.collective_compute(
            "AllGather", ALU.bypass, replica_groups=[list(range(NCORES))],
            ins=[self.pl_in.ap().opt()], outs=[self.pl_out.ap().opt()]), self.sC)
        self.W(PL, sC)
        with g.register("pid") as pid, g.register("off") as off, \
                g.register("off2") as off2:
            g.reg_load(pid, nc.partition_id_tensor.ap())
            g.reg_div(off, pid, 4)
            g.reg_mul(off, off, 4 * D * 16)
            for r in range(4):
                g.reg_add(off2, off, r * D * 16)
                src = bass.AP(self.pl_out, off2,
                              [[16, 128], [128 * 16, NCH], [1, 16]])
                dst = bass.AP(self.latPF, r * 16, [[512, 128], [64, NCH], [1, 16]])
                self.I(g.dma_start(dst, src), self.sG, 16)
        for wnm, dstt in (("wgk", self.latGK), ("wgv", self.latGV)):
            for fc in range(4):
                par = fc & 1
                self.W(PL, sP)
                self.I(g.dma_start(
                    self.wsl[:, par, :, :],
                    self.hap(self.d_wall, WOFF[wnm] + fc * 128,
                             [[512, 128], [128 * 512, NCH], [1, 128]])),
                    sG, 16)
                self.W(PE, sG)
                self.W(PE, sA)
                last = None
                for kc in range(NCH):
                    last = tt.matmul(psF[:, 4, 0:NLAT], self.wsl[:, par, kc, :],
                                     self.latPF[:, kc, :],
                                     start=(kc == 0), stop=(kc == NCH - 1))
                self.I(last, sP)
                self.W(A, sP)
                self.I(a.copy(dstt[:, fc, :], psF[:, 4, 0:NLAT]), sA)
        for lh in range(8):
            lfc, lhalf = divmod(lh, 2)
            self.W(PE, sA)
            self.I(tt.transpose(psB[0:64, 0, 512:576],
                                self.latGV[64 * lhalf:64 * lhalf + 64, lfc, :],
                                self.ident[64 * lhalf:64 * lhalf + 64,
                                           64 * lhalf:64 * lhalf + 64]), sP)
            self.W(A, sP)
            self.I(a.copy(self.latGVT[:, lh, :], psB[0:64, 0, 512:576]), sA)

    def emit_gq(self, t):
        tt, g, a = self.t, self.g, self.a
        psF = self.psF
        sP, sA, sG = self.sP, self.sA, self.sG
        Wd = SEGW
        for fc in range(4):
            par = fc & 1
            self.W(PL, sP)
            self.I(g.dma_start(
                self.wsl[:, par, :, :],
                self.hap(self.d_wall, WOFF["wgq"] + fc * 128,
                         [[512, 128], [128 * 512, NCH], [1, 128]])),
                sG, 16)
            self.W(PE, sG)
            self.W(PE, sA)
            last = None
            for kc in range(NCH):
                last = tt.matmul(psF[:, 2 + par, 0:Wd], self.wsl[:, par, kc, :],
                                 self.xop[:, kc, 0:Wd],
                                 start=(kc == 0), stop=(kc == NCH - 1))
            self.I(last, sP)
            self.W(A, sP)
            self.I(a.copy(self.gq[:, fc, 0:Wd], psF[:, 2 + par, 0:Wd]), sA)

    def emit_bridge(self, t):
        if t < 2:
            return
        self.W(PL, self.sP)
        self.I(self.g.tensor_copy(self.kwin[:, :, 0:HALO],
                                  self.kwin[:, :, SEGW:KWINW]), self.sL)

    def emit_outer(self):
        self.emit_latent_prep()
        for t in range(5):
            self.emit_ln_seg(t, "outer_attn")
            self.emit_bridge(t)
            self.emit_qk_outer(t, "k")
            self.emit_v(t)
            if t > 0:
                self.emit_qk_outer(t, "q")
                self.emit_gq(t)
                self.emit_gates(t)
                for sl in range(4):
                    self.emit_attn_subtile(t, sl, True, f"oat{t}_{sl}")
                self.emit_out_proj(t, None, True)

    def emit_layer_body(self, L):
        nc, mk = self.nc, self.mk
        tt, v, a, g, s = self.t, self.v, self.a, self.g, self.s
        sP, sA, sV, sL, sG, sS, sC = (self.sP, self.sA, self.sV, self.sL,
                                      self.sG, self.sS, self.sC)
        Lv = L % 2
        stg = self.stg[Lv]
        self.W(PL, sC)
        self.W(PL, sP)
        self.W(PL, sA)
        with g.If_eq(self.r_im4, 0):
            g.memset(self.xop[:, :, 0:HALO + 1], 0.0)
            g.memset(self.y[:, :, PADW - 1:PADW], 0.0)
        with g.Else():
            g.memset(self.xop[:, 4:8, 0:1], 0.0)
            g.dma_start(self.xop[:, 0:4, 0:HALO],
                        self.impAP0).then_inc(self.sI, 16)
            g.dma_start(self.xop[:, 4:8, 1:HALO + 1],
                        self.impAP1).then_inc(self.sI, 16)
            g.wait_ge(self.sI, 32 * (L + 1))
            g.tensor_copy(self.y[:, 0:4, PADW - 1:PADW],
                          self.xop[:, 0:4, HALO - 1:HALO])
            g.tensor_copy(self.y[:, 4:8, PADW - 1:PADW],
                          self.xop[:, 4:8, HALO:HALO + 1])
        self.I(g.memset(self.attst[0:1, 0, 7:8], 0.0), sL)
        self.W(PL, self.sX)
        for third, dstw in ((0, self.Wq_sl), (1, self.Wk_sl), (2, self.Wv_sl)):
            src = self.hap(stg, LOFF["Wqkv"] + third * D,
                           [[3 * D, 128], [128 * 3 * D, NCH], [1, D]])
            self.I(g.dma_start(dstw[:], src), sG, 16)
        self.W(PL, self.sG)
        self.I(g.dma_start(self.stg[1 - Lv].ap(),
                           self.hap(self.d_wall, OB0 + (L + 1) * LBS,
                                    [[1, 1], [1, LBS]])),
               self.sX, 16)
        self.W(SP, sV)
        self.W(SP, sA)
        self.I(s.dma_start(self.vecs[:],
                           self.hap(self.d_vecsL, L * 128 * 70,
                                    [[70, 128], [1, 70]])),
               sS, 16)
        for t in range(5):
            Wd = SEGW if t > 0 else HALO
            if t > 0:
                c0 = self.ycols(t)
                self.W(A, sP)
                self.W(A, sV)
                self.W(A, sL)
                self.I(a.copy(self.xop[:, 0:4, 0:Wd], self.y[:, 0:4, c0:c0 + Wd]), sA)
                self.I(a.copy(self.xop[:, 4:8, 0:Wd],
                              self.y[:, 4:8, c0 - 1:c0 - 1 + Wd]), sA)
            tcol = 0 if t == 0 else HALO + SEGW * (t - 1)
            self.W(SP, sP)
            self.W(SP, sV)
            self.W(SP, sL)
            self.I(s.dma_start(self.tabs[:, :, 0:Wd],
                               self.rope.ap().rearrange("s p t -> p s t")[:, :, tcol:tcol + Wd]),
                   sS, 16)
            self.emit_bridge(t)
            self.emit_qk_side(t, "k")
            self.emit_v(t)
            self.emit_qk_side(t, "q")
            subs = [2] if t == 0 else [0, 1, 2, 3]
            for sl in subs:
                self.emit_attn_subtile(t, sl, False, f"iat{t}_{sl}")
            if t == 0:
                self.emit_halo_proj(Lv)
            else:
                self.emit_out_proj(t, Lv, False)
        self.emit_halo_add()
        for t in range(1, 5):
            self.emit_ln_seg(t, "ffn")
        self.emit_ffn(Lv)
        if DEBUG_SNAP:
            self.W(PL, sV)
            self.W(PL, sP)
            self.I(g.dma_start(
                bass.AP(self.d_dbg, L * NCH * 128 * OWN,
                        [[NCH * OWN, 128], [OWN, NCH], [1, OWN]]),
                self.y[:, :, PADW:YW]), sG, 16)
        self.W(SP, sV)
        self.W(SP, sC)
        self.I(s.dma_start(self.cc_in.ap().rearrange("(c p) t -> p c t", p=128),
                           self.y[:, :, YW - HALO:YW]), self.sE, 16)
        self.W(PL, self.sE)
        self.I(g.collective_compute(
            "AllGather", ALU.bypass, replica_groups=[list(range(NCORES))],
            ins=[self.cc_in.ap().opt()], outs=[self.cc_out.ap().opt()]), sC)

    def emit_ffn(self, Lv):
        mk = self.mk
        tt, v, a, g = self.t, self.v, self.a, self.g
        psF = self.psF
        sP, sA, sV, sG = self.sP, self.sA, self.sV, self.sG
        for gc in range(NGC):
            for ab in range(2):
                wo = LOFF["W1a"] if ab == 0 else LOFF["W1g"]
                self.W(PL, sP)
                self.I(g.dma_start(
                    self.wsl[:, ab, :, :],
                    self.hap(self.stg[Lv], wo + gc * 128,
                             [[DIP, 128], [128 * DIP, NCH], [1, 128]])), sG, 16)
            for t in range(1, 5):
                for ab in range(2):
                    self.W(PE, sG)
                    self.W(PE, sA if ab else sV)
                    last = None
                    for kc in range(NCH):
                        last = tt.matmul(psF[:, ab * 2 + (t & 1), 0:SEGW],
                                         self.wsl[:, ab, kc, :],
                                         self.xn[:, kc, ts(t - 1, SEGW)],
                                         start=(kc == 0), stop=(kc == NCH - 1),
                                         skip_group_check=True)
                    self.I(last, sP)
                self.W(A, sP)
                self.W(A, sV)
                self.W(A, self.sS)
                self.I(a.activation(self.sq[:, t & 1, :], psF[:, 2 + (t & 1), 0:SEGW],
                                    AF.Gelu, bias=self.vecs[:, 46 + gc:47 + gc]), sA)
                self.W(DVE, sA)
                self.W(DVE, self.sS)
                self.I(v.scalar_tensor_tensor(self.gt[gc][:, t - 1, :],
                                              psF[:, (t & 1), 0:SEGW],
                                              self.vecs[:, 24 + gc:25 + gc],
                                              self.sq[:, t & 1, :],
                                              ALU.add, ALU.mult), sV)
        for fc in range(NCH):
            for grp in range(3):
                kcs = 8 if grp < 2 else 6
                self.W(PL, sP)
                self.I(g.dma_start(
                    self.wsl[:, grp & 1, 0:kcs, :],
                    self.hap(self.stg[Lv],
                             LOFF["W2p"] + grp * 8 * 128 * D + fc * 128,
                             [[D, 128], [128 * D, kcs], [1, 128]])), sG, 16)
                for t in range(1, 5):
                    self.W(PE, sG)
                    if grp == 0:
                        self.W(PE, sV)
                    last = None
                    for kc in range(kcs):
                        last = tt.matmul(psF[:, 2 + (t - 1), 0:SEGW],
                                         self.wsl[:, grp & 1, kc, :],
                                         self.gt[grp * 8 + kc][:, t - 1, :],
                                         start=(grp == 0 and kc == 0),
                                         stop=(grp == 2 and kc == kcs - 1),
                                         skip_group_check=True)
                    self.I(last, sP)
            for t in range(1, 5):
                c0 = self.ycols(t)
                yap = self.y[:, fc, c0:c0 + SEGW]
                self.W(DVE, sP)
                self.W(DVE, self.sS)
                self.I(v.scalar_tensor_tensor(yap, psF[:, 2 + (t - 1), 0:SEGW],
                                              self.vecs[:, 16 + fc:17 + fc],
                                              yap, ALU.add, ALU.add), sV)

    def emit_final(self):
        v, s = self.v, self.s
        self.W(DVE, self.sE)
        if DEBUG_SNAP:
            self.W(DVE, self.sG)
        for t in range(1, 5):
            c0 = self.ycols(t)
            self.W(SP, self.sV)
            self.W(SP, self.sP)
            self.I(s.dma_start(
                self.x1t[:, :, 0:SEGW],
                self.x1_d.ap().rearrange("c p t -> p c t")[:, :, ts(t - 1, SEGW)]),
                self.sS, 16)
            self.W(DVE, self.sS)
            for kc in range(NCH):
                self.I(v.tensor_tensor(self.y[:, kc, c0:c0 + SEGW],
                                       self.y[:, kc, c0:c0 + SEGW],
                                       self.x1t[:, kc, 0:SEGW], ALU.add), self.sV)

    def emit_output(self):
        for sem in (self.sV, self.sP, self.sA, self.sL):
            self.W(SP, sem)
        self.I(self.s.dma_start(self.y_out.ap().rearrange("c p t -> p c t"),
                                self.y[:, :, PADW:YW]), self.sS, 16)
        self.W(SP, self.sS)

    def emit(self):
        mk = self.mk
        self.init_loads()
        if self.stage == "m1":
            self.emit_output()
            self.nc.all_engine_barrier()
            return
        self.emit_outer()
        if self.stage == "outer":
            self.emit_output()
            self.nc.all_engine_barrier()
            return
        self.W(SP, self.sV)
        self.W(SP, self.sP)
        self.I(self.s.dma_start(self.x1_d.ap().rearrange("c p t -> p c t"),
                                self.y[:, :, PADW:YW]), self.sS, 16)
        for t in range(1, 5):
            self.emit_ln_seg(t, "ln0")
        self.W(SP, self.sV)
        self.W(SP, self.sP)
        self.I(self.s.dma_start(self.cc_in.ap().rearrange("(c p) t -> p c t", p=128),
                                self.y[:, :, YW - HALO:YW]), self.sE, 16)
        self.W(PL, self.sE)
        self.I(self.g.collective_compute(
            "AllGather", ALU.bypass, replica_groups=[list(range(NCORES))],
            ins=[self.cc_in.ap().opt()], outs=[self.cc_out.ap().opt()]), self.sC)
        self.W(SP, self.sA)
        self.I(self.s.dma_start(self.masks[:],
                                self.mask_in.ap().rearrange("s p n -> p s n")),
               self.sS, 16)
        if self.stage == "ln0":
            self.emit_output()
            self.nc.all_engine_barrier()
            return
        self.I(self.g.dma_start(self.stg[0].ap(),
                                self.hap(self.d_wall, OB0, [[1, 1], [1, LBS]])),
               self.sX, 16)
        for L in range(self.n_layers):
            self.emit_layer_body(L)
        if self.stage == "full":
            self.emit_final()
        self.emit_output()
        self.nc.all_engine_barrier()


def build_nc(stage=STAGE, plan=None):
    nc = bass.Bass("TRN2", target_bir_lowering=False)
    mk = MK(nc, plan)
    b = B(nc, mk, stage)
    b.emit()
    return nc, mk


def _in_maps(inputs):
    shared = _prep_shared(inputs)
    skeys = ("wall", "vecsL", "gateB", "ident", "perm",
             "sstat", "e16", "ones2", "onesR")
    maps = []
    for c in range(NCORES):
        d = _prep_core_inputs(inputs, c)
        m = {k: d[k] for k in ("x_own", "x_halo", "rope", "mask_in",
                               "mask_out", "mask_lat")}
        for k in skeys:
            m[k] = shared[k]
        maps.append(m)
    return maps


def kernel(**inputs):
    maps = _in_maps(inputs)
    nc, mk = build_nc(STAGE, None)
    res = run_bass_kernel_spmd(nc, maps, core_ids=list(range(NCORES)))
    out = np.zeros((2, 8192, D), np.float32)
    for c in range(NCORES):
        seq, sl = divmod(c, 4)
        r = np.asarray(res.results[c]["y_out"])
        out[seq, OWN * sl:OWN * (sl + 1)] = r.reshape(D, OWN).T
    return out


# version 03
# speedup vs baseline: 1.2895x; 1.2869x over previous
import math
import numpy as np
import ml_dtypes
import concourse.bass as bass
import concourse.mybir as mybir
from concourse.bass_utils import run_bass_kernel_spmd
from concourse.ordered_engine_set import OrderedEngineSet

F32 = mybir.dt.float32
BF16 = mybir.dt.bfloat16
AF = mybir.ActivationFunctionType
ALU = mybir.AluOpType
ET = mybir.EngineType
ts = bass.ts
ds = bass.ds
BF16NP = ml_dtypes.bfloat16
AX = mybir.AxisListType.X

NCORES = 8
D = 1024
NCH = 8
OWN = 2048
HALO = 384
SEGW = 512
PADW = 8
YW = PADW + OWN
NLAYER = 6
DI = 2730
DIP = 2816
NGC = 22
WINB = 384
KWINW = 896
NLAT = 64
NVSLOT = 6
EPS = 1e-5

STAGE = "full"
DEBUG_SNAP = False
ALL_ENG = [ET.PE, ET.DVE, ET.Activation, ET.Pool, ET.SP]

_OUTER_SPEC = [("wq", D * D), ("wk", D * D), ("wv", D * D),
               ("wgq", D * 512), ("wgk", D * 512), ("wgv", D * 512),
               ("r_out_w", 512 * D), ("w_out", D * D)]
_BLOCK_SPEC = [("Wqkv", D * 3 * D), ("Wo", D * D), ("W1a", D * DIP),
               ("W1g", D * DIP), ("W2p", DIP * D)]
WOFF = {}
_o = 0
for _nm, _sz in _OUTER_SPEC:
    WOFF[_nm] = _o
    _o += _sz
OB0 = _o
LOFF = {}
_b = 0
for _nm, _sz in _BLOCK_SPEC:
    LOFF[_nm] = _b
    _b += _sz
LBS = _b
WALLN = OB0 + (NLAYER + 1) * LBS
PE, DVE, A, PL, SP = ET.PE, ET.DVE, ET.Activation, ET.Pool, ET.SP


def _fm(a):
    return np.ascontiguousarray(a.T.reshape(NCH, 128, -1))


def _rope_tables(base):
    pos = np.clip(np.arange(base - HALO, base + OWN), 0, 8191).astype(np.float64)
    half = np.arange(0, 64, 2, dtype=np.float64)
    inv_freq = 1.0 / 10000.0 ** (half / 64.0)
    freqs = pos[:, None] * inv_freq[None, :]
    freqs = np.concatenate([freqs, freqs], axis=1)
    base_scale = (half + 0.4 * 64.0) / (1.4 * 64.0)
    power = (pos - 4096.0) / 128.0
    scale = base_scale[None, :] ** power[:, None]
    scale = np.concatenate([scale, scale], axis=1)
    r8 = math.sqrt(8.0)
    tabs = np.stack([np.cos(freqs) * scale * r8, np.sin(freqs) * scale * r8,
                     np.cos(freqs) / scale * r8, np.sin(freqs) / scale * r8])
    tabs = np.tile(tabs.transpose(0, 2, 1), (1, 2, 1))
    return np.ascontiguousarray(tabs.astype(np.float32))


def _win_masks(outer, first_core):
    r = np.arange(128)[:, None]
    w = np.arange(WINB)[None, :]
    lo = r + 1 if outer else r
    ok = (w >= lo) & (w <= r + 256)
    m = np.zeros((3, 128, WINB), np.float32)
    NEG = np.float32(-3e38)
    m[0] = np.where(ok, 0.0, NEG)
    if first_core:
        m[1] = np.where(ok & (w >= 256), 0.0, NEG)
        m[2] = np.where(ok & (w >= 128), 0.0, NEG)
    else:
        m[1] = m[0]
        m[2] = m[0]
    return m


def _lat_masks(base):
    m = np.zeros((16, 128, NLAT), np.float32)
    NEG = np.float32(-3e38)
    n = np.arange(NLAT)[None, :]
    for s in range(16):
        m[s] = np.where(n <= (base + 128 * s) // 128, 0.0, NEG)
    return m


def _rot_perm():
    P = np.zeros((128, 128), np.float32)
    for g in range(2):
        o = 64 * g
        for dd in range(32):
            P[o + dd + 32, o + dd] = -1.0
            P[o + dd, o + dd + 32] = 1.0
    return P


def _vec_t(v, cols):
    out = np.zeros(128 * cols, np.float64)
    out[:v.shape[0]] = v
    return np.ascontiguousarray(out.reshape(cols, 128).T.astype(np.float32))


def _prep_core_inputs(inputs, c):
    seq, sl = divmod(c, 4)
    base = OWN * sl
    x = np.asarray(inputs["L0_attn_x"][seq], np.float32)
    d = {}
    d["x_own"] = _fm(x[base:base + OWN])
    xh = np.zeros((HALO, D), np.float32)
    if base > 0:
        xh[:] = x[base - HALO:base]
    d["x_halo"] = _fm(xh)
    d["rope"] = _rope_tables(base).astype(BF16NP)
    d["mask_in"] = _win_masks(False, sl == 0).astype(BF16NP)
    d["mask_out"] = _win_masks(True, sl == 0).astype(BF16NP)
    d["mask_lat"] = _lat_masks(base).astype(BF16NP)
    return d


def _prep_shared(inputs):
    f32 = np.float32
    d = {}
    wd = {}
    for k in ("wq", "wk", "wv", "wgq", "wgk", "wgv", "r_out_w", "w_out"):
        wd[k] = np.asarray(inputs["L0_attn_" + k], f32)
    wd["Wqkv"] = np.asarray(inputs["L0_ffn_Wqkv"], f32)
    wd["Wo"] = np.asarray(inputs["L0_ffn_Wo"], f32)
    W1 = np.asarray(inputs["L0_ffn_W1"], f32)
    W1a = np.zeros((NLAYER, D, DIP), f32); W1a[:, :, :DI] = W1[:, :, :DI]
    W1g = np.zeros((NLAYER, D, DIP), f32); W1g[:, :, :DI] = W1[:, :, DI:]
    wd["W1a"], wd["W1g"] = W1a, W1g
    W2 = np.asarray(inputs["L0_ffn_W2"], f32)
    W2p = np.zeros((NLAYER, DIP, D), f32); W2p[:, :DI] = W2
    wd["W2p"] = W2p
    wall = np.zeros((1, WALLN), f32)
    for nm, sz in _OUTER_SPEC:
        wall[0, WOFF[nm]:WOFF[nm] + sz] = wd[nm].reshape(-1)
    for L in range(NLAYER):
        b0 = OB0 + L * LBS
        for nm, sz in _BLOCK_SPEC:
            wall[0, b0 + LOFF[nm]:b0 + LOFF[nm] + sz] = wd[nm][L].reshape(-1)
    d["wall"] = wall
    b1 = np.asarray(inputs["L0_ffn_b1"], f32)
    b1a = np.zeros((NLAYER, DIP), f32); b1a[:, :DI] = b1[:, :DI]
    b1g = np.zeros((NLAYER, DIP), f32); b1g[:, :DI] = b1[:, DI:]
    vecs = np.zeros((NLAYER, 128, 70), f32)
    for i in range(NLAYER):
        vecs[i, :, 0:8] = _vec_t(np.asarray(inputs["L0_ffn_ln_g"][i], f32), 8)
        vecs[i, :, 8:16] = _vec_t(np.asarray(inputs["L0_ffn_ln_b"][i], f32), 8)
        vecs[i, :, 16:24] = _vec_t(np.asarray(inputs["L0_ffn_b2"][i], f32), 8)
        vecs[i, :, 24:46] = _vec_t(b1a[i], NGC)
        vecs[i, :, 46:68] = _vec_t(b1g[i], NGC)
        vecs[i, :, 68] = np.tile(np.asarray(inputs["L0_ffn_q_scale"][i], f32), 2)
        vecs[i, :, 69] = np.tile(np.asarray(inputs["L0_ffn_k_scale"][i], f32), 2)
    d["vecsL"] = np.ascontiguousarray(vecs)
    glw = np.asarray(inputs["L0_attn_gate_local_w"], f32)
    grw = np.asarray(inputs["L0_attn_gate_remote_w"], f32)
    gB = np.zeros((128, NCH, 48), f32)
    for h in range(16):
        fc, half = divmod(h, 2)
        gB[64 * half:64 * half + 64, fc, h] = glw[h]
        gB[64 * half:64 * half + 64, fc, 32 + h] = grw[h]
    d["gateB"] = np.ascontiguousarray(gB.astype(BF16NP))
    d["ident"] = np.eye(128, dtype=f32).astype(BF16NP)
    d["perm"] = _rot_perm().astype(BF16NP)
    sstat = np.zeros((128, NCH, 16), f32)
    for fc in range(NCH):
        sstat[0:64, fc, 2 * fc] = 1.0
        sstat[64:128, fc, 2 * fc + 1] = 1.0
    d["sstat"] = np.ascontiguousarray(sstat.astype(BF16NP))
    e16 = np.zeros((16, NCH, 128), f32)
    for fc in range(NCH):
        e16[2 * fc, fc, 0:64] = 1.0
        e16[2 * fc + 1, fc, 64:128] = 1.0
    d["e16"] = np.ascontiguousarray(e16.astype(BF16NP))
    d["ones2"] = np.ones((128, 2), f32).astype(BF16NP)
    d["onesR"] = np.ones((1, 128), f32).astype(BF16NP)
    return d


class Sem:
    __slots__ = ("name", "h", "c")

    def __init__(self, name, h):
        self.name, self.h, self.c = name, h, 0


class MK:
    def __init__(self, nc, plan=None):
        self.nc = nc
        self.pass2 = plan is not None
        self.plan = plan if plan is not None else {"deltas": {}, "marks": {}}
        self.sems = {}
        self.loopstack = []
        self.scratch = {}
        self.lastwait = {}

    def sem(self, name):
        s = Sem(name, self.nc.alloc_semaphore(name=name))
        self.sems[name] = s
        return s

    def eng(self, e):
        nc = self.nc
        return {PE: nc.tensor, DVE: nc.vector, A: nc.scalar,
                PL: nc.gpsimd, SP: nc.sync}[e]

    def wreg(self, e, idx):
        key = (e, idx)
        if key not in self.scratch:
            self.scratch[key] = self.eng(e).alloc_register(f"wr{idx}")
        return self.scratch[key]

    def inc(self, inst, sem, n=1):
        inst.then_inc(sem.h, n)
        sem.c += n
        return inst

    def wait(self, e, sem, val=None):
        if val is None:
            val = sem.c
        key = (e, sem.name)
        ctx = tuple(lid for lid, _ in self.loopstack)
        prev = self.lastwait.get(key)
        if prev is not None and prev[0] == ctx and prev[1] >= val:
            return
        self.lastwait[key] = (ctx, val)
        eng = self.eng(e)
        terms = []
        for (lid, lv) in self.loopstack:
            dl = self.plan["deltas"].get((lid, sem.name), 0) if self.pass2 else 0
            if dl:
                terms.append((lv, dl))
        if not terms:
            eng.wait_ge(sem.h, max(val, 0))
            return
        r = self.wreg(e, 0)
        eng.reg_mul(r, terms[0][0], terms[0][1])
        for lv, dl in terms[1:]:
            r2 = self.wreg(e, 1)
            eng.reg_mul(r2, lv, dl)
            eng.reg_alu(r, r, r2, ALU.add)
        eng.reg_add(r, r, val)
        if val < 0:
            eng.reg_alu(r, r, 0, ALU.max)
        eng.wait_ge(sem.h, r)

    def loop(self, n, engines, tag):
        return _Loop(self, n, engines, tag)


class _Loop:
    def __init__(self, mk, n, engines, tag):
        self.mk, self.n, self.lid = mk, n, tag
        self.engines = OrderedEngineSet(engines)

    def __enter__(self):
        mk = self.mk
        self._starts = {nm: s.c for nm, s in mk.sems.items()}
        self._fori = mk.nc.Fori(0, self.n, engines=self.engines)
        self.var = self._fori.__enter__()
        mk.loopstack.append((self.lid, self.var))
        return self.var

    def __exit__(self, a_, b_, c_):
        mk = self.mk
        mk.loopstack.pop()
        mk.lastwait.clear()
        self._fori.__exit__(a_, b_, c_)
        for nm, s in mk.sems.items():
            dl = s.c - self._starts[nm]
            key = (self.lid, nm)
            if not mk.pass2:
                mk.plan["deltas"][key] = dl
            else:
                assert mk.plan["deltas"].get(key, 0) == dl, (self.lid, nm)
            s.c = self._starts[nm] + self.n * dl
        return False


class Mem:
    def __init__(self, nc):
        self.nc = nc
        self.base = (nc.sbuf_base + 31) // 32 * 32
        self.top = self.base
        self.offs = {}

    def alloc(self, name, shape, dtype, offset=None):
        esz = mybir.dt.size(dtype)
        freesz = int(np.prod(shape[1:])) * esz
        if offset is None:
            offset = self.top
            self.top = offset + ((freesz + 31) // 32) * 32
        h = self.nc.alloc_sbuf_tensor_at(name, list(shape), dtype, offset=offset)
        self.offs[name] = offset
        return h

    def finish(self):
        self.nc.sbuf_base = self.top

    def at(self, name):
        return self.offs[name]


class B:
    def __init__(self, nc, mk, stage):
        self.nc, self.mk, self.stage = nc, mk, stage
        self.n_layers = {"l2": 2, "l4": 4, "full": NLAYER}.get(stage, NLAYER)
        self._declare_dram()
        self._declare_sbuf()
        s = mk.sem
        self.sG, self.sS, self.sP, self.sA, self.sV, self.sL, self.sC = (
            s("sG"), s("sS"), s("sP"), s("sA"), s("sV"), s("sL"), s("sC"))
        self.sX = s("sX")
        self.sE = s("sE")
        self.sI = nc.alloc_semaphore(name="sI")
        self.t, self.v, self.a, self.g, self.s = (nc.tensor, nc.vector,
                                                  nc.scalar, nc.gpsimd, nc.sync)
        self.r_ipid = self.g.alloc_register("ipid")
        self.r_im4 = self.g.alloc_register("im4")
        self.r_ioff = self.g.alloc_register("ioff")
        self.r_ioff2 = self.g.alloc_register("ioff2")
        self.psF = nc.alloc_psum_tensor("psF", [128, 6, 512], F32)
        self.psB = nc.alloc_psum_tensor("psB", [128, 2, 1024], BF16)

    def _declare_dram(self):
        f = self.nc.dram_tensor
        self.x_own = f("x_own", [NCH, 128, OWN], F32, kind="ExternalInput")
        self.x_halo = f("x_halo", [NCH, 128, HALO], F32, kind="ExternalInput")
        self.rope = f("rope", [4, 128, HALO + OWN], BF16, kind="ExternalInput")
        self.mask_in = f("mask_in", [3, 128, WINB], BF16, kind="ExternalInput")
        self.mask_out = f("mask_out", [3, 128, WINB], BF16, kind="ExternalInput")
        self.mask_lat = f("mask_lat", [16, 128, NLAT], BF16, kind="ExternalInput")
        self.d_wall = f("wall", [1, WALLN], F32, kind="ExternalInput")
        self.d_vecsL = f("vecsL", [NLAYER, 128, 70], F32, kind="ExternalInput")
        for k, shp in (("gateB", [128, NCH, 48]), ("ident", [128, 128]),
                       ("perm", [128, 128]), ("sstat", [128, NCH, 16]),
                       ("e16", [16, NCH, 128]), ("ones2", [128, 2]),
                       ("onesR", [1, 128])):
            setattr(self, "d_" + k, f(k, shp, BF16, kind="ExternalInput"))
        self.y_out = f("y_out", [NCH, 128, OWN], F32, kind="ExternalOutput")
        self.x1_d = f("x1_spill", [NCH, 128, OWN], F32, kind="Internal")
        self.cc_in = f("cc_in", [D, HALO], F32, kind="Internal")
        self.cc_out = f("cc_out", [NCORES * D, HALO], F32, kind="Internal")
        self.pl_in = f("pl_in", [D, 16], BF16, kind="Internal")
        self.pl_out = f("pl_out", [NCORES * D, 16], BF16, kind="Internal")
        self.stg = [f("stg0", [1, LBS], F32, kind="Internal"),
                    f("stg1", [1, LBS], F32, kind="Internal")]
        if DEBUG_SNAP:
            self.d_dbg = f("dbg_y", [NLAYER * D, OWN], F32, kind="Internal")

    def _declare_sbuf(self):
        mem = Mem(self.nc)
        self.mem = mem
        self.y = mem.alloc("y", [128, NCH, YW], F32)
        self.kwin = mem.alloc("kwin", [128, NCH, KWINW], BF16)
        self.vt = mem.alloc("vt", [128, NVSLOT, 1024], BF16)
        self.tabs = mem.alloc("tabs", [128, 4, SEGW], BF16)
        self.qsb = mem.alloc("qsb", [128, NCH, SEGW], BF16)
        self.ao = mem.alloc("ao", [128, NCH, SEGW], BF16)
        self.xop = mem.alloc("xop", [128, NCH, SEGW], BF16)
        self.wsl = mem.alloc("wsl", [128, 2, NCH, 128], BF16)
        self.Wq_sl = mem.alloc("Wq_sl", [128, NCH, D], BF16)
        self.Wk_sl = mem.alloc("Wk_sl", [128, NCH, D], BF16)
        self.Wv_sl = mem.alloc("Wv_sl", [128, NCH, D], BF16)
        self.masks = mem.alloc("masks", [128, 3, WINB], BF16)
        self.masklat = mem.alloc("masklat", [128, 16, NLAT], BF16)
        self.ident = mem.alloc("ident", [128, 128], BF16)
        self.perm = mem.alloc("perm", [128, 128], BF16)
        self.sstat = mem.alloc("sstat", [128, NCH, 16], BF16)
        self.e16 = mem.alloc("e16", [16, NCH, 128], BF16)
        self.ones2 = mem.alloc("ones2", [128, 2], BF16)
        self.onesR = mem.alloc("onesR", [1, 128], BF16)
        self.vecs = mem.alloc("vecs", [128, 70], F32)
        self.csts = mem.alloc("csts", [128, 2], F32)
        self.sq = mem.alloc("sq", [128, 2, SEGW], BF16)
        self.attst = mem.alloc("attst", [128, 2, 8], F32)
        self.pooled = mem.alloc("pooled", [128, NCH, 16], F32)
        self.pooledB = mem.alloc("pooledB", [128, NCH, 16], BF16)
        self.latGK = mem.alloc("latGK", [128, 4, NLAT], BF16)
        self.latGV = mem.alloc("latGV", [128, 4, NLAT], BF16)
        self.latGVT = mem.alloc("latGVT", [64, 8, NLAT], BF16)
        self.T1 = mem.alloc("T1", [128, 2, WINB], BF16)
        self.PT = mem.alloc("PT", [128, 2, WINB], BF16)
        self.latscr = mem.alloc("latscr", [128, 2, NLAT], BF16)
        self.latscrT = mem.alloc("latscrT", [64, 2, 128], BF16)
        self.gateB = mem.alloc("gateB", [128, NCH, 48], BF16)
        self.lrows = mem.alloc("lrows", [1, 2560], F32)
        self.rstdF = mem.alloc("rstdF", [16, SEGW], F32)
        self.rstdH = mem.alloc("rstdH", [16, 2, SEGW], BF16)
        self.rem = mem.alloc("rem", [128, 4, SEGW], BF16)
        lro = mem.at("lrows")
        self.lrhl = mem.alloc("lrhl", [1, 2048], BF16, offset=lro + 6144)
        self.gq = mem.alloc("gq", [128, 4, SEGW], BF16, offset=mem.at("tabs"))
        self.qnF = mem.alloc("qnF", [128, SEGW], F32, offset=mem.at("T1"))
        self.xn = mem.alloc("xn", [128, NCH, OWN], BF16, offset=mem.at("Wq_sl"))
        assert mem.at("Wk_sl") == mem.at("Wq_sl") + 16384
        self.gates = mem.alloc("gates", [48, SEGW], F32, offset=lro)
        self.grbf = mem.alloc("grbf", [16, SEGW], BF16, offset=lro + 2048)
        self.glT = mem.alloc("glT", [128, 4, 16], F32, offset=lro + 3072)
        self.grB_sb = mem.alloc("grB_sb", [128, SEGW], BF16, offset=lro + 3584)
        self.glbf = mem.alloc("glbf", [16, SEGW], BF16, offset=lro + 4608)
        self.t1mix = mem.alloc("t1mix", [128, SEGW], BF16, offset=mem.at("sq"))
        self.latPF = mem.alloc("latPF", [128, NCH, NLAT], BF16, offset=mem.at("rem"))
        self.x1t = mem.alloc("x1t", [128, NCH, SEGW], F32, offset=mem.at("Wv_sl"))
        gsrc = [("Wv_sl", 0), ("Wv_sl", 4096), ("Wv_sl", 8192), ("Wv_sl", 12288),
                ("kwin", 0), ("kwin", 4096), ("kwin", 8192),
                ("vt", 0), ("vt", 4096), ("vt", 8192),
                ("qsb", 0), ("qsb", 4096), ("ao", 0), ("ao", 4096),
                ("xop", 0), ("xop", 4096), ("tabs", 0), ("rem", 0),
                ("T1", 0), ("rstdF", 0), ("lrows", 0), ("lrows", 4096)]
        assert mem.at("PT") == mem.at("T1") + 1536
        assert mem.at("gateB") == mem.at("T1") + 3840
        assert mem.at("rstdH") == mem.at("rstdF") + 2048
        self.gt = [mem.alloc(f"g{i}", [128, 4, SEGW], BF16,
                             offset=mem.at(nm) + off)
                   for i, (nm, off) in enumerate(gsrc)]
        mem.finish()
        assert mem.top <= self.nc.sbuf_top - 1024, (
            f"SBUF over budget: {mem.top} vs {self.nc.sbuf_top}")

    def W(self, e, sem, val=None):
        self.mk.wait(e, sem, val)

    def I(self, inst, sem, n=1):
        return self.mk.inc(inst, sem, n)

    @staticmethod
    def ycols(t):
        return PADW + SEGW * (t - 1)

    @staticmethod
    def hap(handle, off, dims):
        return bass.AP(handle, off, dims)

    def init_loads(self):
        s, g = self.s, self.g
        self.I(s.dma_start(self.y[:, :, PADW:YW],
                           self.x_own.ap().rearrange("c p t -> p c t")), self.sS, 16)
        for src, dst in ((self.d_ident, self.ident), (self.d_perm, self.perm),
                         (self.d_sstat, self.sstat), (self.d_e16, self.e16),
                         (self.d_ones2, self.ones2), (self.d_onesR, self.onesR),
                         (self.d_gateB, self.gateB)):
            self.I(s.dma_start(dst[:], src.ap()), self.sS, 16)
        self.I(s.dma_start(self.masklat[:],
                           self.mask_lat.ap().rearrange("s p n -> p s n")), self.sS, 16)
        self.I(s.dma_start(self.masks[:],
                           self.mask_out.ap().rearrange("s p n -> p s n")), self.sS, 16)
        for w, dst in (("wq", self.Wq_sl), ("wk", self.Wk_sl),
                       ("wv", self.Wv_sl)):
            self.I(g.dma_start(dst[:],
                               self.hap(self.d_wall, WOFF[w],
                                        [[D, 128], [128 * D, NCH], [1, D]])),
                   self.sG, 16)
        self.I(g.memset(self.xop[:], 0.0), self.sL)
        self.I(g.memset(self.y[:, :, 0:PADW], 0.0), self.sL)
        self.I(g.memset(self.csts[:, 0:1], EPS), self.sL)
        self.I(g.memset(self.csts[:, 1:2], 1e-24), self.sL)
        g.reg_load(self.r_ipid, self.nc.partition_id_tensor.ap())
        g.reg_mod(self.r_im4, self.r_ipid, 4)
        g.reg_add(self.r_ioff, self.r_ipid, NCORES - 1)
        g.reg_mod(self.r_ioff, self.r_ioff, NCORES)
        g.reg_mul(self.r_ioff, self.r_ioff, D * HALO)
        g.reg_add(self.r_ioff2, self.r_ioff, 4 * 128 * HALO)
        self.impAP0 = bass.AP(self.cc_out, self.r_ioff,
                              [[HALO, 128], [128 * HALO, 4], [1, HALO]])
        self.impAP1 = bass.AP(self.cc_out, self.r_ioff2,
                              [[HALO, 128], [128 * HALO, 4], [1, HALO]])

    def emit_ln_seg(self, t, mode):
        tt, v, a, g = self.t, self.v, self.a, self.g
        psF = self.psF
        sP, sA, sV, sL, sG, sS = self.sP, self.sA, self.sV, self.sL, self.sG, self.sS
        Wd = SEGW if t > 0 else HALO
        halo = (t == 0)
        c0 = self.ycols(t) if t > 0 else None
        if mode == "ffn":
            rawap = lambda kc: self.xn[:, kc, SEGW * (t - 1):SEGW * (t - 1) + Wd]
        else:
            rawap = lambda kc: self.xop[:, kc, 0:Wd]
        if halo:
            self.W(PL, self.sP)
            self.I(g.dma_start(self.xop[:, :, 0:HALO],
                               self.x_halo.ap().rearrange("c p t -> p c t")),
                   self.sG, 16)
            self.W(PE, sG)
            self.W(PL, sG)
        else:
            self.W(A, sP)
            self.W(A, sV)
            self.W(A, sS)
            self.W(A, sL)
            if mode == "ffn":
                self.I(a.copy(self.xn[:, 0:4, SEGW * (t - 1):SEGW * t],
                              self.y[:, 0:4, c0:c0 + Wd]), sA)
                self.I(a.copy(self.xn[:, 4:8, SEGW * (t - 1):SEGW * t],
                              self.y[:, 4:8, c0 - 1:c0 - 1 + Wd]), sA)
            else:
                self.I(a.copy(self.xop[:, :, 0:Wd], self.y[:, :, c0:c0 + Wd]), sA)
            self.W(PE, sA)
            self.W(PL, sA)
        for kc in range(NCH):
            par = kc & 1
            self.W(PL, sP)
            self.I(g.tensor_tensor(self.sq[:, par, 0:Wd], rawap(kc), rawap(kc),
                                   ALU.mult), sL)
            self.I(tt.matmul(psF[0:1, 0, 0:Wd], self.ones2[:, 0:1], rawap(kc),
                             start=(kc == 0), stop=(kc == NCH - 1)), self.sP)
            self.W(PE, sL)
            self.I(tt.matmul(psF[0:1, 1, 0:Wd], self.ones2[:, 1:2],
                             self.sq[:, par, 0:Wd],
                             start=(kc == 0), stop=(kc == NCH - 1)), self.sP)
        lr = self.lrows
        s0, s1, s2 = lr[0:1, 0:Wd], lr[0:1, 512:512 + Wd], lr[0:1, 1024:1024 + Wd]
        self.W(DVE, sP)
        self.I(v.tensor_scalar_mul(s0, psF[0:1, 0, 0:Wd], 1.0 / D), sV)
        self.W(A, sV)
        self.I(a.activation(s1, s0, AF.Square), sA)
        self.W(DVE, sA)
        self.I(v.scalar_tensor_tensor(s2, psF[0:1, 1, 0:Wd], 1.0 / D, s1,
                                      ALU.mult, ALU.subtract), sV)
        self.W(A, sV)
        self.W(A, self.sL)
        self.I(a.activation(s2, s2, AF.Ln, bias=self.csts[0:1, 0:1]), sA)
        self.W(A, sA)
        self.I(a.activation(s2, s2, AF.Exp, scale=-0.5), sA)
        self.W(DVE, sA)
        self.I(v.scalar_tensor_tensor(s1, s0, -1.0, s2, ALU.mult, ALU.mult), sV)
        hb = self.lrhl
        self.W(A, sV)
        self.I(a.copy(hb[0:1, 0:Wd], s2), sA)
        self.W(DVE, sA)
        self.I(v.tensor_sub(hb[0:1, 512:512 + Wd], s2, hb[0:1, 0:Wd]), sV)
        self.W(A, sV)
        self.I(a.copy(hb[0:1, 1024:1024 + Wd], s1), sA)
        self.W(DVE, sA)
        self.I(v.tensor_sub(hb[0:1, 1536:1536 + Wd], s1, hb[0:1, 1024:1024 + Wd]), sV)
        self.W(PE, sV)
        self.I(tt.matmul(psF[:, 2, 0:Wd], self.onesR[0:1, :], hb[0:1, 0:Wd],
                         start=True, stop=False), self.sP)
        self.I(tt.matmul(psF[:, 2, 0:Wd], self.onesR[0:1, :], hb[0:1, 512:512 + Wd],
                         start=False, stop=True), self.sP)
        self.I(tt.matmul(psF[:, 3, 0:Wd], self.onesR[0:1, :], hb[0:1, 1024:1024 + Wd],
                         start=True, stop=False), self.sP)
        self.I(tt.matmul(psF[:, 3, 0:Wd], self.onesR[0:1, :], hb[0:1, 1536:1536 + Wd],
                         start=False, stop=True), self.sP)
        for kc in range(NCH):
            if mode == "ln0":
                src = self.y[:, kc, c0:c0 + Wd]
                self.W(DVE, sP)
                self.W(DVE, sV)
                self.I(v.scalar_tensor_tensor(self.qnF[:, 0:Wd], src, 1.0,
                                              psF[:, 3, 0:Wd], ALU.mult, ALU.add), sV)
                self.W(DVE, sV)
                self.I(v.tensor_tensor(src, self.qnF[:, 0:Wd], psF[:, 2, 0:Wd],
                                       ALU.mult), sV)
            elif mode == "outer_attn":
                src = rawap(kc) if halo else self.y[:, kc, c0:c0 + Wd]
                self.W(DVE, sP)
                self.W(DVE, sV)
                self.I(v.scalar_tensor_tensor(self.qnF[:, 0:Wd], src, 1.0,
                                              psF[:, 3, 0:Wd], ALU.mult, ALU.add), sV)
                self.W(DVE, sV)
                self.I(v.tensor_tensor(rawap(kc), self.qnF[:, 0:Wd],
                                       psF[:, 2, 0:Wd], ALU.mult), sV)
            else:
                ysrc = (self.y[:, kc, c0:c0 + Wd] if kc < 4
                        else self.y[:, kc, c0 - 1:c0 - 1 + Wd])
                self.W(DVE, sP)
                self.W(DVE, sV)
                self.I(v.scalar_tensor_tensor(self.qnF[:, 0:Wd], ysrc, 1.0,
                                              psF[:, 3, 0:Wd], ALU.mult, ALU.add), sV)
                self.W(DVE, sV)
                self.W(DVE, sS)
                self.I(v.scalar_tensor_tensor(self.qnF[:, 0:Wd], self.qnF[:, 0:Wd],
                                              self.vecs[:, kc:kc + 1],
                                              psF[:, 2, 0:Wd], ALU.mult, ALU.mult), sV)
                self.W(DVE, sV)
                self.I(v.tensor_scalar_add(rawap(kc), self.qnF[:, 0:Wd],
                                           self.vecs[:, 8 + kc:9 + kc]), sV)

    def emit_qk_side(self, t, side):
        tt, v, a, g = self.t, self.v, self.a, self.g
        psF = self.psF
        sP, sA, sV, sL, sG, sS = self.sP, self.sA, self.sV, self.sL, self.sG, self.sS
        Wd = SEGW if t > 0 else HALO
        Wsl = self.Wk_sl if side == "k" else self.Wq_sl
        if side == "k":
            cur0 = 0 if t == 0 else HALO
            dst = lambda fc: self.kwin[:, fc, cur0:cur0 + Wd]
        else:
            dst = lambda fc: self.qsb[:, fc, 0:Wd]
        sA_e, sV_e, sL_e, sP_e, sG_e = sA.c, sV.c, sL.c, sP.c, sG.c
        p_mm, a_cp, l_sq, p_st = {}, {}, {}, {}

        def emit_mm8(fc):
            par = fc & 1
            self.W(PE, sA, max(sA_e, a_cp.get(fc - 2, 0)))
            self.W(PE, sV, sV_e)
            self.W(PE, sG, sG_e)
            self.W(PE, sL, sL_e)
            last = None
            for kc in range(NCH):
                last = tt.matmul(psF[:, par, 0:Wd], Wsl[:, kc, ts(fc, 128)],
                                 self.xop[:, kc, 0:Wd],
                                 start=(kc == 0), stop=(kc == NCH - 1))
            self.I(last, sP)
            p_mm[fc] = sP.c

        emit_mm8(0)
        for fc in range(NCH):
            if fc + 1 < NCH:
                emit_mm8(fc + 1)
            par = fc & 1
            self.W(A, sP, p_mm[fc])
            self.W(A, sL, sL_e)
            self.I(a.copy(dst(fc), psF[:, par, 0:Wd]), sA)
            a_cp[fc] = sA.c
            self.W(PL, sA, a_cp[fc])
            self.W(PL, sP, p_st.get(fc - 2, sP_e))
            self.I(g.tensor_tensor(self.sq[:, par, 0:Wd], dst(fc), dst(fc),
                                   ALU.mult), sL)
            l_sq[fc] = sL.c
            self.W(PE, sL, l_sq[fc])
            self.I(tt.matmul(psF[0:16, 2, 0:Wd], self.sstat[:, fc, :],
                             self.sq[:, par, 0:Wd], start=(fc == 0),
                             stop=(fc == NCH - 1)), sP)
            p_st[fc] = sP.c
        self.W(A, sP)
        self.W(A, self.sL)
        self.I(a.activation(self.rstdF[:, 0:Wd], psF[0:16, 2, 0:Wd], AF.Ln,
                            bias=self.csts[0:16, 1:2]), sA)
        self.W(A, sA)
        self.I(a.activation(self.rstdF[:, 0:Wd], self.rstdF[:, 0:Wd], AF.Exp,
                            scale=-0.5), sA)
        self.W(A, sA)
        self.I(a.copy(self.rstdH[:, 0, 0:Wd], self.rstdF[:, 0:Wd]), sA)
        self.W(DVE, sA)
        self.I(v.tensor_sub(self.rstdH[:, 1, 0:Wd], self.rstdF[:, 0:Wd],
                            self.rstdH[:, 0, 0:Wd]), sV)
        scol = 68 if side == "q" else 69
        tb = 0 if side == "q" else 2
        a_r2, v_r2, l_e2, s_e2 = sA.c, sV.c, sL.c, self.sS.c
        p_bc, v_stt2, v_sq0, l_sq1, p_rot, a_cp2 = {}, {}, {}, {}, {}, {}

        def emit_bc(fc):
            bb = 2 + 3 * (fc & 1)
            self.W(PE, sA, max(a_r2, a_cp2.get(fc - 2, 0)))
            self.W(PE, sV, max(v_r2, v_stt2.get(fc - 2, 0)))
            self.I(tt.matmul(psF[:, bb, 0:Wd], self.e16[:, fc, :],
                             self.rstdH[:, 0, 0:Wd], start=True, stop=False), sP)
            self.I(tt.matmul(psF[:, bb, 0:Wd], self.e16[:, fc, :],
                             self.rstdH[:, 1, 0:Wd], start=False, stop=True), sP)
            p_bc[fc] = sP.c

        emit_bc(0)
        for fc in range(NCH):
            if fc + 1 < NCH:
                emit_bc(fc + 1)
            par = fc & 1
            bb = 2 + 3 * (fc & 1)
            self.W(DVE, sP, max(p_bc[fc], p_rot.get(fc - 1, sP_e)))
            self.W(DVE, sL, l_sq1.get(fc - 1, l_e2))
            self.W(DVE, sS, s_e2)
            self.W(DVE, sA, a_cp[fc])
            self.I(v.scalar_tensor_tensor(self.qnF[:, 0:Wd], dst(fc),
                                          self.vecs[:, scol:scol + 1],
                                          psF[:, bb, 0:Wd], ALU.mult, ALU.mult), sV)
            v_stt2[fc] = sV.c
            self.W(DVE, sV)
            self.I(v.tensor_tensor(self.sq[:, 0, 0:Wd], self.qnF[:, 0:Wd],
                                   self.tabs[:, tb + 1, 0:Wd], ALU.mult), sV)
            v_sq0[fc] = sV.c
            self.W(PL, sV, v_stt2[fc])
            self.W(PL, sP, p_rot.get(fc - 1, sP_e))
            self.I(g.tensor_tensor(self.sq[:, 1, 0:Wd], self.qnF[:, 0:Wd],
                                   self.tabs[:, tb, 0:Wd], ALU.mult), sL)
            l_sq1[fc] = sL.c
            self.W(PE, sV, v_sq0[fc])
            self.W(PE, sL, l_sq1[fc])
            self.W(PE, sA, a_cp2.get(fc - 2, a_r2))
            self.I(tt.matmul(psF[:, 3 + par, 0:Wd], self.perm[:],
                             self.sq[:, 0, 0:Wd], start=True, stop=False), sP)
            self.I(tt.matmul(psF[:, 3 + par, 0:Wd], self.ident[:],
                             self.sq[:, 1, 0:Wd], start=False, stop=True), sP)
            p_rot[fc] = sP.c
            self.W(A, sP, p_rot[fc])
            self.I(a.copy(dst(fc), psF[:, 3 + par, 0:Wd]), sA)
            a_cp2[fc] = sA.c

    def emit_qk_outer(self, t, side):
        tt, a = self.t, self.a
        psF = self.psF
        sP, sA, sG, sV = self.sP, self.sA, self.sG, self.sV
        Wd = SEGW if t > 0 else HALO
        Wsl = self.Wk_sl if side == "k" else self.Wq_sl
        if side == "k":
            cur0 = 0 if t == 0 else HALO
            dst = lambda fc: self.kwin[:, fc, cur0:cur0 + Wd]
        else:
            dst = lambda fc: self.qsb[:, fc, 0:Wd]
        for fc in range(NCH):
            par = fc & 1
            self.W(PE, sA)
            self.W(PE, sV)
            self.W(PE, sG)
            self.W(PE, self.sL)
            last = None
            for kc in range(NCH):
                last = tt.matmul(psF[:, par, 0:Wd], Wsl[:, kc, ts(fc, 128)],
                                 self.xop[:, kc, 0:Wd],
                                 start=(kc == 0), stop=(kc == NCH - 1))
            self.I(last, sP)
            self.W(A, sP)
            self.I(a.activation(dst(fc), psF[:, par, 0:Wd], AF.Copy,
                                scale=(0.125 if side == "k" else 1.0)), sA)

    def emit_v(self, t):
        tt, a = self.t, self.a
        psF = self.psF
        sP, sA = self.sP, self.sA
        Wd = SEGW if t > 0 else HALO
        ntc = Wd // 128
        g0 = 0 if t == 0 else 3 + 4 * (t - 1)
        sA_ev = sA.c
        a_ev, p_v = {}, {}
        for tc in range(ntc):
            slot = (g0 + tc) % NVSLOT
            for half in range(2):
                i = tc * 2 + half
                par = i & 1
                self.W(PE, sA, max(sA_ev, a_ev.get(i - 2, 0)))
                last = None
                for kc in range(NCH):
                    last = tt.matmul(psF[:, 3 + par, 0:512],
                                     self.xop[:, kc, ts(tc, 128)],
                                     self.Wv_sl[:, kc, ts(half, 512)],
                                     start=(kc == 0), stop=(kc == NCH - 1))
                self.I(last, sP)
                p_v[i] = sP.c
                self.W(A, sP, p_v[i])
                self.I(a.copy(self.vt[:, slot, ts(half, 512)],
                              psF[:, 3 + par, 0:512]), sA)
                a_ev[i] = sA.c

    def emit_attn_subtile(self, t, sl, outer, tag):
        mk = self.mk
        tt, v, a, g = self.t, self.v, self.a, self.g
        psF, psB = self.psF, self.psB
        sP, sA, sV, sL = self.sP, self.sA, self.sV, self.sL
        if t == 0:
            q0c, w0, gsub, msl = 256, 0, None, 0
        else:
            q0c = 128 * sl
            w0 = HALO + 128 * sl - 256
            gsub = 4 * (t - 1) + sl
            msl = 1 if (t == 1 and sl == 0) else (2 if (t == 1 and sl == 1) else 0)
        s_glob = (3 + 4 * (t - 1) + sl) if t > 0 else 2
        cslots = [((s_glob - 2 + i) % NVSLOT) for i in range(3)]
        sA0, sV0, sP0 = sA.c, sV.c, sP.c
        v_stt, a_exp, a_pt, p_S, p_tr = {}, {}, {}, {}, {}

        def emit_S(hv):
            fcv, half = divmod(hv, 2)
            par = hv & 1
            qap = self.qsb[64 * half:64 * half + 64, fcv, q0c:q0c + 128]
            kap = self.kwin[64 * half:64 * half + 64, fcv, w0:w0 + WINB]
            self.W(PE, sV, v_stt.get(hv - 2, sV0))
            self.W(PE, sA, sA0)
            self.I(tt.matmul(psF[:, par, 0:WINB], qap, kap, start=True,
                             stop=True, skip_group_check=True), sP)
            p_S[hv] = sP.c

        emit_S(0)
        for hv in range(16):
            if hv + 1 < 16:
                emit_S(hv + 1)
            fcv, half = divmod(hv, 2)
            par = hv & 1
            self.W(DVE, sP, max(p_S[hv], p_tr.get(hv - 2, sP0)))
            self.W(DVE, sA, a_exp.get(hv - 2, sA0))
            self.I(v.scalar_tensor_tensor(self.T1[:, par, :], psF[:, par, 0:WINB],
                                          1.0, self.masks[:, msl, :],
                                          ALU.mult, ALU.add), sV)
            v_stt[hv] = sV.c
            self.W(DVE, sV)
            self.I(v.reduce_max(self.attst[:, par, 0:1], self.T1[:, par, :],
                                axis=AX, negate=True), sV)
            self.W(A, sV)
            self.I(a.activation(self.T1[:, par, :], self.T1[:, par, :], AF.Exp,
                                bias=self.attst[:, par, 0:1],
                                accum_out=self.attst[:, par, 1:2]), sA)
            a_exp[hv] = sA.c
            self.W(DVE, sA)
            self.I(v.reciprocal(self.attst[:, par, 2:3],
                                self.attst[:, par, 1:2]), sV)
            self.W(DVE, sV)
            if outer:
                self.I(v.tensor_scalar(self.T1[:, par, :], self.T1[:, par, :],
                                       self.attst[:, par, 2:3],
                                       self.glT[:, sl, hv:hv + 1],
                                       ALU.mult, ALU.mult), sV)
            else:
                self.I(v.tensor_scalar_mul(self.T1[:, par, :],
                                           self.T1[:, par, :],
                                           self.attst[:, par, 2:3]), sV)
            v_scale = sV.c
            self.W(PE, sV, v_scale)
            self.W(PE, sA, a_pt.get(hv - 2, sA0))
            for c in range(3):
                self.I(tt.transpose(psB[:, par, ts(c, 128)],
                                    self.T1[:, par, ts(c, 128)],
                                    self.ident[:]), sP)
            p_tr[hv] = sP.c
            self.W(A, sP, p_tr[hv])
            self.I(a.copy(self.PT[:, par, :], psB[:, par, 0:WINB]), sA)
            a_pt[hv] = sA.c
            self.W(PE, sA, a_pt[hv])
            last = None
            for c in range(3):
                vap = self.vt[:, cslots[c], ts(hv, 64)]
                last = tt.matmul(psF[0:64, 2 + par, 0:128], vap,
                                 self.PT[:, par, ts(c, 128)],
                                 start=(c == 0), stop=(c == 2),
                                 skip_group_check=True)
            self.I(last, sP)
            self.W(A, sP)
            self.I(a.copy(self.ao[64 * half:64 * half + 64, fcv,
                                  q0c:q0c + 128], psF[0:64, 2 + par, 0:128]), sA)
        if outer and t > 0:
            self.emit_latent_pair(t, sl, gsub)

    def emit_latent_pair(self, t, sl, gsub):
        tt, v, a, g = self.t, self.v, self.a, self.g
        psF, psB = self.psF, self.psB
        sP, sA, sV, sL = self.sP, self.sA, self.sV, self.sL
        q0c = 128 * sl
        for lh in range(8):
            j, w = divmod(lh, 2)
            lr = slice(64 * w, 64 * w + 64)
            self.W(PE, sA)
            self.W(PE, sV)
            self.I(tt.matmul(psF[:, 4, 0:NLAT],
                             self.gq[lr, j, q0c:q0c + 128],
                             self.latGK[lr, j, :],
                             start=True, stop=True, skip_group_check=True), sP)
            self.W(DVE, sP)
            self.W(DVE, sA)
            self.I(v.scalar_tensor_tensor(self.latscr[:, w, :], psF[:, 4, 0:NLAT],
                                          1.0, self.masklat[:, gsub, :],
                                          ALU.mult, ALU.add), sV)
            self.W(DVE, sV)
            self.I(v.reduce_max(self.attst[:, w, 4:5], self.latscr[:, w, :],
                                axis=AX, negate=True), sV)
            self.W(A, sV)
            self.I(a.activation(self.latscr[:, w, :], self.latscr[:, w, :], AF.Exp,
                                bias=self.attst[:, w, 4:5],
                                accum_out=self.attst[:, w, 5:6]), sA)
            self.W(DVE, sA)
            self.I(v.reciprocal(self.attst[:, w, 6:7], self.attst[:, w, 5:6]), sV)
            self.W(DVE, sV)
            self.I(v.tensor_scalar_mul(self.latscr[:, w, :], self.latscr[:, w, :],
                                       self.attst[:, w, 6:7]), sV)
            self.W(PE, sV)
            self.W(PE, sA)
            self.I(tt.transpose(psB[0:64, w, 384:512], self.latscr[:, w, :],
                                self.ident[:]), sP)
            self.W(A, sP)
            self.I(a.copy(self.latscrT[:, w, :], psB[0:64, w, 384:512]), sA)
            self.W(PE, sA)
            self.I(tt.matmul(psF[0:64, 5, 0:128], self.latGVT[:, lh, :],
                             self.latscrT[:, w, :], start=True, stop=True,
                             skip_group_check=True), sP)
            self.W(A, sP)
            self.I(a.copy(self.rem[lr, ds(j, 1), q0c:q0c + 128],
                          psF[0:64, 5, 0:128]), sA)

    def emit_halo_proj(self, Lv):
        tt, v, g = self.t, self.v, self.g
        psF = self.psF
        sP, sA, sV, sG = self.sP, self.sA, self.sV, self.sG
        for fc in range(NCH):
            par = fc & 1
            self.W(PL, sP)
            src = self.hap(self.stg[Lv], LOFF["Wo"] + fc * 128,
                           [[D, 128], [128 * D, NCH], [1, 128]])
            self.I(g.dma_start(self.wsl[:, par, :, :], src), sG, 16)
            self.W(PE, sG)
            self.W(PE, sV)
            self.W(PE, sA)
            last = None
            for kc in range(NCH):
                last = tt.matmul(psF[:, par, 0:1], self.wsl[:, par, kc, :],
                                 self.ao[:, kc, 383:384],
                                 start=(kc == 0), stop=(kc == NCH - 1))
            self.I(last, sP)
            self.W(A, sP)
            self.W(A, sV)
            self.I(self.a.copy(self.pooled[:, fc, 0:1], psF[:, par, 0:1]), sA)

    def emit_halo_add(self):
        self.W(DVE, self.sA)
        self.W(DVE, self.sV)
        self.I(self.v.tensor_tensor(self.y[:, :, PADW - 1:PADW],
                                    self.y[:, :, PADW - 1:PADW],
                                    self.pooled[:, :, 0:1], ALU.add), self.sV)

    def emit_out_proj(self, t, Lv, outer):
        tt, v, a, g = self.t, self.v, self.a, self.g
        psF = self.psF
        sP, sA, sV, sG = self.sP, self.sA, self.sV, self.sG
        Wd = SEGW
        c0 = self.ycols(t)
        if outer:
            for fc in range(NCH):
                par = fc & 1
                self.W(PL, sP)
                self.I(g.dma_start(
                    self.wsl[:, par, 0:4, :],
                    self.hap(self.d_wall, WOFF["r_out_w"] + fc * 128,
                             [[D, 128], [128 * D, 4], [1, 128]])), sG, 16)
                self.W(PE, sG)
                self.W(PE, sV)
                self.W(PE, sA)
                last = None
                for kc in range(4):
                    last = tt.matmul(psF[:, 2 + par, 0:Wd], self.wsl[:, par, kc, :],
                                     self.rem[:, kc, 0:Wd],
                                     start=(kc == 0), stop=(kc == 3))
                self.I(last, sP)
                self.I(tt.matmul(psF[:, 4 + par, 0:Wd], self.e16[:, fc, :],
                                 self.grbf[:, 0:Wd], start=True, stop=True,
                                 skip_group_check=True), sP)
                self.W(A, sP)
                self.I(a.copy(self.grB_sb[:, 0:Wd], psF[:, 4 + par, 0:Wd]), sA)
                self.W(DVE, sA)
                self.I(v.tensor_tensor(self.t1mix[:, 0:Wd], psF[:, 2 + par, 0:Wd],
                                       self.grB_sb[:, 0:Wd], ALU.mult), sV)
                self.W(DVE, sV)
                self.I(v.tensor_tensor(self.ao[:, fc, 0:Wd], self.t1mix[:, 0:Wd],
                                       self.ao[:, fc, 0:Wd], ALU.add), sV)
        for fc in range(NCH):
            par = fc & 1
            self.W(PL, sP)
            if outer:
                src = self.hap(self.d_wall, WOFF["w_out"] + fc * 128,
                               [[D, 128], [128 * D, NCH], [1, 128]])
            else:
                src = self.hap(self.stg[Lv], LOFF["Wo"] + fc * 128,
                               [[D, 128], [128 * D, NCH], [1, 128]])
            self.I(g.dma_start(self.wsl[:, par, :, :], src), sG, 16)
            self.W(PE, sG)
            self.W(PE, sV)
            self.W(PE, sA)
            last = None
            for kc in range(NCH):
                last = tt.matmul(psF[:, par, 0:Wd], self.wsl[:, par, kc, :],
                                 self.ao[:, kc, 0:Wd],
                                 start=(kc == 0), stop=(kc == NCH - 1))
            self.I(last, sP)
            self.W(DVE, sP)
            self.I(v.tensor_tensor(self.y[:, fc, c0:c0 + Wd],
                                   self.y[:, fc, c0:c0 + Wd],
                                   psF[:, par, 0:Wd], ALU.add), sV)

    def emit_gates(self, t):
        tt, v, a = self.t, self.v, self.a
        psF, psB = self.psF, self.psB
        sP, sA, sV = self.sP, self.sA, self.sV
        Wd = SEGW
        for fc in range(NCH):
            self.W(PE, sA)
            self.I(tt.matmul(psF[0:48, 5, 0:Wd], self.gateB[:, fc, :],
                             self.qsb[:, fc, 0:Wd], start=(fc == 0),
                             stop=(fc == NCH - 1)), sP)
        self.W(A, sP)
        self.I(a.activation(self.gates[:, 0:Wd], psF[0:48, 5, 0:Wd], AF.Exp,
                            scale=-1.0), sA)
        self.W(DVE, sA)
        self.I(v.tensor_scalar_add(self.gates[:, 0:Wd], self.gates[:, 0:Wd], 1.0), sV)
        self.W(DVE, sV)
        self.I(v.reciprocal(self.gates[:, 0:Wd], self.gates[:, 0:Wd]), sV)
        self.W(A, sV)
        self.I(a.copy(self.grbf[:, 0:Wd], self.gates[32:48, 0:Wd]), sA)
        self.W(A, sA)
        self.I(a.copy(self.glbf[:, 0:Wd], self.gates[0:16, 0:Wd]), sA)
        for sl in range(4):
            self.W(PE, sA)
            self.I(tt.transpose(psB[0:128, 0, 512:528],
                                self.glbf[0:16, ts(sl, 128)], self.ident[0:16, 0:16]), sP)
            self.W(A, sP)
            self.I(a.copy(self.glT[:, sl, :], psB[0:128, 0, 512:528]), sA)

    def emit_latent_prep(self):
        nc = self.nc
        tt, v, a, g, s = self.t, self.v, self.a, self.g, self.s
        psF, psB = self.psF, self.psB
        sP, sA, sV, sL, sG, sS, sC = (self.sP, self.sA, self.sV, self.sL,
                                      self.sG, self.sS, self.sC)
        for t in range(1, 5):
            self.emit_ln_seg(t, "outer_attn")
            for kc in range(NCH):
                self.W(DVE, sV)
                self.I(v.reduce_sum(
                    self.pooled[:, kc, ts(t - 1, 4)],
                    self.xop[:, kc, 0:SEGW].rearrange("p (l w) -> p l w", w=128),
                    axis=AX), sV)
        self.W(DVE, sV)
        self.I(v.tensor_scalar_mul(self.pooledB[:], self.pooled[:], 1.0 / 128.0), sV)
        self.W(SP, sV)
        self.I(s.dma_start(self.pl_in.ap().rearrange("(c p) l -> p c l", p=128),
                           self.pooledB[:]), self.sE, 16)
        self.W(PL, self.sE)
        self.I(g.collective_compute(
            "AllGather", ALU.bypass, replica_groups=[list(range(NCORES))],
            ins=[self.pl_in.ap().opt()], outs=[self.pl_out.ap().opt()]), self.sC)
        self.W(PL, sC)
        with g.register("pid") as pid, g.register("off") as off, \
                g.register("off2") as off2:
            g.reg_load(pid, nc.partition_id_tensor.ap())
            g.reg_div(off, pid, 4)
            g.reg_mul(off, off, 4 * D * 16)
            for r in range(4):
                g.reg_add(off2, off, r * D * 16)
                src = bass.AP(self.pl_out, off2,
                              [[16, 128], [128 * 16, NCH], [1, 16]])
                dst = bass.AP(self.latPF, r * 16, [[512, 128], [64, NCH], [1, 16]])
                self.I(g.dma_start(dst, src), self.sG, 16)
        for wnm, dstt in (("wgk", self.latGK), ("wgv", self.latGV)):
            for fc in range(4):
                par = fc & 1
                self.W(PL, sP)
                self.I(g.dma_start(
                    self.wsl[:, par, :, :],
                    self.hap(self.d_wall, WOFF[wnm] + fc * 128,
                             [[512, 128], [128 * 512, NCH], [1, 128]])),
                    sG, 16)
                self.W(PE, sG)
                self.W(PE, sA)
                last = None
                for kc in range(NCH):
                    last = tt.matmul(psF[:, 4, 0:NLAT], self.wsl[:, par, kc, :],
                                     self.latPF[:, kc, :],
                                     start=(kc == 0), stop=(kc == NCH - 1))
                self.I(last, sP)
                self.W(A, sP)
                self.I(a.copy(dstt[:, fc, :], psF[:, 4, 0:NLAT]), sA)
        for lh in range(8):
            lfc, lhalf = divmod(lh, 2)
            self.W(PE, sA)
            self.I(tt.transpose(psB[0:64, 0, 512:576],
                                self.latGV[64 * lhalf:64 * lhalf + 64, lfc, :],
                                self.ident[64 * lhalf:64 * lhalf + 64,
                                           64 * lhalf:64 * lhalf + 64]), sP)
            self.W(A, sP)
            self.I(a.copy(self.latGVT[:, lh, :], psB[0:64, 0, 512:576]), sA)

    def emit_gq(self, t):
        tt, g, a = self.t, self.g, self.a
        psF = self.psF
        sP, sA, sG = self.sP, self.sA, self.sG
        Wd = SEGW
        for fc in range(4):
            par = fc & 1
            self.W(PL, sP)
            self.I(g.dma_start(
                self.wsl[:, par, :, :],
                self.hap(self.d_wall, WOFF["wgq"] + fc * 128,
                         [[512, 128], [128 * 512, NCH], [1, 128]])),
                sG, 16)
            self.W(PE, sG)
            self.W(PE, sA)
            last = None
            for kc in range(NCH):
                last = tt.matmul(psF[:, 2 + par, 0:Wd], self.wsl[:, par, kc, :],
                                 self.xop[:, kc, 0:Wd],
                                 start=(kc == 0), stop=(kc == NCH - 1))
            self.I(last, sP)
            self.W(A, sP)
            self.I(a.copy(self.gq[:, fc, 0:Wd], psF[:, 2 + par, 0:Wd]), sA)

    def emit_bridge(self, t):
        if t < 2:
            return
        self.W(PL, self.sP)
        self.I(self.g.tensor_copy(self.kwin[:, :, 0:HALO],
                                  self.kwin[:, :, SEGW:KWINW]), self.sL)

    def emit_outer(self):
        self.emit_latent_prep()
        for t in range(5):
            self.emit_ln_seg(t, "outer_attn")
            self.emit_bridge(t)
            self.emit_qk_outer(t, "k")
            self.emit_v(t)
            if t > 0:
                self.emit_qk_outer(t, "q")
                self.emit_gq(t)
                self.emit_gates(t)
                for sl in range(4):
                    self.emit_attn_subtile(t, sl, True, f"oat{t}_{sl}")
                self.emit_out_proj(t, None, True)

    def emit_layer_body(self, L):
        nc, mk = self.nc, self.mk
        tt, v, a, g, s = self.t, self.v, self.a, self.g, self.s
        sP, sA, sV, sL, sG, sS, sC = (self.sP, self.sA, self.sV, self.sL,
                                      self.sG, self.sS, self.sC)
        Lv = L % 2
        stg = self.stg[Lv]
        self.W(PL, sC)
        self.W(PL, sP)
        self.W(PL, sA)
        with g.If_eq(self.r_im4, 0):
            g.memset(self.xop[:, :, 0:HALO + 1], 0.0)
            g.memset(self.y[:, :, PADW - 1:PADW], 0.0)
        with g.Else():
            g.memset(self.xop[:, 4:8, 0:1], 0.0)
            g.dma_start(self.xop[:, 0:4, 0:HALO],
                        self.impAP0).then_inc(self.sI, 16)
            g.dma_start(self.xop[:, 4:8, 1:HALO + 1],
                        self.impAP1).then_inc(self.sI, 16)
            g.wait_ge(self.sI, 32 * (L + 1))
            g.tensor_copy(self.y[:, 0:4, PADW - 1:PADW],
                          self.xop[:, 0:4, HALO - 1:HALO])
            g.tensor_copy(self.y[:, 4:8, PADW - 1:PADW],
                          self.xop[:, 4:8, HALO:HALO + 1])
        self.I(g.memset(self.attst[0:1, 0, 7:8], 0.0), sL)
        self.W(PL, self.sX)
        for third, dstw in ((0, self.Wq_sl), (1, self.Wk_sl), (2, self.Wv_sl)):
            src = self.hap(stg, LOFF["Wqkv"] + third * D,
                           [[3 * D, 128], [128 * 3 * D, NCH], [1, D]])
            self.I(g.dma_start(dstw[:], src), sG, 16)
        self.W(PL, self.sG)
        self.I(g.dma_start(self.stg[1 - Lv].ap(),
                           self.hap(self.d_wall, OB0 + (L + 1) * LBS,
                                    [[1, 1], [1, LBS]])),
               self.sX, 16)
        self.W(SP, sV)
        self.W(SP, sA)
        self.I(s.dma_start(self.vecs[:],
                           self.hap(self.d_vecsL, L * 128 * 70,
                                    [[70, 128], [1, 70]])),
               sS, 16)
        for t in range(5):
            Wd = SEGW if t > 0 else HALO
            if t > 0:
                c0 = self.ycols(t)
                self.W(A, sP)
                self.W(A, sV)
                self.W(A, sL)
                self.I(a.copy(self.xop[:, 0:4, 0:Wd], self.y[:, 0:4, c0:c0 + Wd]), sA)
                self.I(a.copy(self.xop[:, 4:8, 0:Wd],
                              self.y[:, 4:8, c0 - 1:c0 - 1 + Wd]), sA)
            tcol = 0 if t == 0 else HALO + SEGW * (t - 1)
            self.W(SP, sP)
            self.W(SP, sV)
            self.W(SP, sL)
            self.I(s.dma_start(self.tabs[:, :, 0:Wd],
                               self.rope.ap().rearrange("s p t -> p s t")[:, :, tcol:tcol + Wd]),
                   sS, 16)
            self.emit_bridge(t)
            self.emit_qk_side(t, "k")
            self.emit_v(t)
            self.emit_qk_side(t, "q")
            subs = [2] if t == 0 else [0, 1, 2, 3]
            for sl in subs:
                self.emit_attn_subtile(t, sl, False, f"iat{t}_{sl}")
            if t == 0:
                self.emit_halo_proj(Lv)
            else:
                self.emit_out_proj(t, Lv, False)
        self.emit_halo_add()
        for t in range(1, 5):
            self.emit_ln_seg(t, "ffn")
        self.emit_ffn(Lv)
        if DEBUG_SNAP:
            self.W(PL, sV)
            self.W(PL, sP)
            self.I(g.dma_start(
                bass.AP(self.d_dbg, L * NCH * 128 * OWN,
                        [[NCH * OWN, 128], [OWN, NCH], [1, OWN]]),
                self.y[:, :, PADW:YW]), sG, 16)
        self.W(SP, sV)
        self.W(SP, sC)
        self.I(s.dma_start(self.cc_in.ap().rearrange("(c p) t -> p c t", p=128),
                           self.y[:, :, YW - HALO:YW]), self.sE, 16)
        self.W(PL, self.sE)
        self.I(g.collective_compute(
            "AllGather", ALU.bypass, replica_groups=[list(range(NCORES))],
            ins=[self.cc_in.ap().opt()], outs=[self.cc_out.ap().opt()]), sC)

    def emit_ffn(self, Lv):
        mk = self.mk
        tt, v, a, g = self.t, self.v, self.a, self.g
        psF = self.psF
        sP, sA, sV, sG = self.sP, self.sA, self.sV, self.sG
        sV_f, sA_f, sS_f = sV.c, sA.c, self.sS.c
        bank_rV, bank_rA, v_sqw = {}, {}, {}
        for gc in range(NGC):
            for ab in range(2):
                wo = LOFF["W1a"] if ab == 0 else LOFF["W1g"]
                self.W(PL, sP)
                self.I(g.dma_start(
                    self.wsl[:, ab, :, :],
                    self.hap(self.stg[Lv], wo + gc * 128,
                             [[DIP, 128], [128 * DIP, NCH], [1, 128]])), sG, 16)
            g_ld = sG.c
            for t in range(1, 5):
                p_ab = {}
                for ab in range(2):
                    self.W(PE, sG, g_ld)
                    if ab:
                        self.W(PE, sA, bank_rA.get(t & 1, 0))
                    else:
                        self.W(PE, sV, max(sV_f, bank_rV.get(t & 1, 0)))
                    last = None
                    for kc in range(NCH):
                        last = tt.matmul(psF[:, ab * 2 + (t & 1), 0:SEGW],
                                         self.wsl[:, ab, kc, :],
                                         self.xn[:, kc, ts(t - 1, SEGW)],
                                         start=(kc == 0), stop=(kc == NCH - 1),
                                         skip_group_check=True)
                    self.I(last, sP)
                    p_ab[ab] = sP.c
                self.W(A, sP, p_ab[1])
                self.W(A, sV, v_sqw.get(t & 1, sV_f))
                self.W(A, self.sS, sS_f)
                self.I(a.activation(self.sq[:, t & 1, :], psF[:, 2 + (t & 1), 0:SEGW],
                                    AF.Gelu, bias=self.vecs[:, 46 + gc:47 + gc]), sA)
                bank_rA[t & 1] = sA.c
                self.W(DVE, sP, p_ab[0])
                self.W(DVE, sA)
                self.W(DVE, self.sS, sS_f)
                self.I(v.scalar_tensor_tensor(self.gt[gc][:, t - 1, :],
                                              psF[:, (t & 1), 0:SEGW],
                                              self.vecs[:, 24 + gc:25 + gc],
                                              self.sq[:, t & 1, :],
                                              ALU.add, ALU.mult), sV)
                bank_rV[t & 1] = sV.c
                v_sqw[t & 1] = sV.c
        for fc in range(NCH):
            for grp in range(3):
                kcs = 8 if grp < 2 else 6
                self.W(PL, sP)
                self.I(g.dma_start(
                    self.wsl[:, grp & 1, 0:kcs, :],
                    self.hap(self.stg[Lv],
                             LOFF["W2p"] + grp * 8 * 128 * D + fc * 128,
                             [[D, 128], [128 * D, kcs], [1, 128]])), sG, 16)
                for t in range(1, 5):
                    self.W(PE, sG)
                    if grp == 0:
                        self.W(PE, sV)
                    last = None
                    for kc in range(kcs):
                        last = tt.matmul(psF[:, 2 + (t - 1), 0:SEGW],
                                         self.wsl[:, grp & 1, kc, :],
                                         self.gt[grp * 8 + kc][:, t - 1, :],
                                         start=(grp == 0 and kc == 0),
                                         stop=(grp == 2 and kc == kcs - 1),
                                         skip_group_check=True)
                    self.I(last, sP)
            for t in range(1, 5):
                c0 = self.ycols(t)
                yap = self.y[:, fc, c0:c0 + SEGW]
                self.W(DVE, sP)
                self.W(DVE, self.sS)
                self.I(v.scalar_tensor_tensor(yap, psF[:, 2 + (t - 1), 0:SEGW],
                                              self.vecs[:, 16 + fc:17 + fc],
                                              yap, ALU.add, ALU.add), sV)

    def emit_final(self):
        v, s = self.v, self.s
        self.W(DVE, self.sE)
        if DEBUG_SNAP:
            self.W(DVE, self.sG)
        for t in range(1, 5):
            c0 = self.ycols(t)
            self.W(SP, self.sV)
            self.W(SP, self.sP)
            self.I(s.dma_start(
                self.x1t[:, :, 0:SEGW],
                self.x1_d.ap().rearrange("c p t -> p c t")[:, :, ts(t - 1, SEGW)]),
                self.sS, 16)
            self.W(DVE, self.sS)
            for kc in range(NCH):
                self.I(v.tensor_tensor(self.y[:, kc, c0:c0 + SEGW],
                                       self.y[:, kc, c0:c0 + SEGW],
                                       self.x1t[:, kc, 0:SEGW], ALU.add), self.sV)

    def emit_output(self):
        for sem in (self.sV, self.sP, self.sA, self.sL):
            self.W(SP, sem)
        self.I(self.s.dma_start(self.y_out.ap().rearrange("c p t -> p c t"),
                                self.y[:, :, PADW:YW]), self.sS, 16)
        self.W(SP, self.sS)

    def emit(self):
        mk = self.mk
        self.init_loads()
        if self.stage == "m1":
            self.emit_output()
            self.nc.all_engine_barrier()
            return
        self.emit_outer()
        if self.stage == "outer":
            self.emit_output()
            self.nc.all_engine_barrier()
            return
        self.W(SP, self.sV)
        self.W(SP, self.sP)
        self.I(self.s.dma_start(self.x1_d.ap().rearrange("c p t -> p c t"),
                                self.y[:, :, PADW:YW]), self.sS, 16)
        for t in range(1, 5):
            self.emit_ln_seg(t, "ln0")
        self.W(SP, self.sV)
        self.W(SP, self.sP)
        self.I(self.s.dma_start(self.cc_in.ap().rearrange("(c p) t -> p c t", p=128),
                                self.y[:, :, YW - HALO:YW]), self.sE, 16)
        self.W(PL, self.sE)
        self.I(self.g.collective_compute(
            "AllGather", ALU.bypass, replica_groups=[list(range(NCORES))],
            ins=[self.cc_in.ap().opt()], outs=[self.cc_out.ap().opt()]), self.sC)
        self.W(SP, self.sA)
        self.I(self.s.dma_start(self.masks[:],
                                self.mask_in.ap().rearrange("s p n -> p s n")),
               self.sS, 16)
        if self.stage == "ln0":
            self.emit_output()
            self.nc.all_engine_barrier()
            return
        self.I(self.g.dma_start(self.stg[0].ap(),
                                self.hap(self.d_wall, OB0, [[1, 1], [1, LBS]])),
               self.sX, 16)
        for L in range(self.n_layers):
            self.emit_layer_body(L)
        if self.stage == "full":
            self.emit_final()
        self.emit_output()
        self.nc.all_engine_barrier()


def build_nc(stage=STAGE, plan=None):
    nc = bass.Bass("TRN2", target_bir_lowering=False)
    mk = MK(nc, plan)
    b = B(nc, mk, stage)
    b.emit()
    return nc, mk


def _in_maps(inputs):
    shared = _prep_shared(inputs)
    skeys = ("wall", "vecsL", "gateB", "ident", "perm",
             "sstat", "e16", "ones2", "onesR")
    maps = []
    for c in range(NCORES):
        d = _prep_core_inputs(inputs, c)
        m = {k: d[k] for k in ("x_own", "x_halo", "rope", "mask_in",
                               "mask_out", "mask_lat")}
        for k in skeys:
            m[k] = shared[k]
        maps.append(m)
    return maps


def kernel(**inputs):
    maps = _in_maps(inputs)
    nc, mk = build_nc(STAGE, None)
    res = run_bass_kernel_spmd(nc, maps, core_ids=list(range(NCORES)))
    out = np.zeros((2, 8192, D), np.float32)
    for c in range(NCORES):
        seq, sl = divmod(c, 4)
        r = np.asarray(res.results[c]["y_out"])
        out[seq, OWN * sl:OWN * (sl + 1)] = r.reshape(D, OWN).T
    return out


# version 04 (final)
# speedup vs baseline: 1.3274x; 1.0294x over previous
import math
import numpy as np
import ml_dtypes
import concourse.bass as bass
import concourse.mybir as mybir
from concourse.bass_utils import run_bass_kernel_spmd
from concourse.ordered_engine_set import OrderedEngineSet

F32 = mybir.dt.float32
BF16 = mybir.dt.bfloat16
AF = mybir.ActivationFunctionType
ALU = mybir.AluOpType
ET = mybir.EngineType
ts = bass.ts
ds = bass.ds
BF16NP = ml_dtypes.bfloat16
AX = mybir.AxisListType.X

NCORES = 8
D = 1024
NCH = 8
OWN = 2048
HALO = 384
SEGW = 512
PADW = 8
YW = PADW + OWN
NLAYER = 6
DI = 2730
DIP = 2816
NGC = 22
WINB = 384
KWINW = 896
NLAT = 64
NVSLOT = 6
EPS = 1e-5

STAGE = "full"
DEBUG_SNAP = False
ALL_ENG = [ET.PE, ET.DVE, ET.Activation, ET.Pool, ET.SP]

_OUTER_SPEC = [("wq", D * D), ("wk", D * D), ("wv", D * D),
               ("wgq", D * 512), ("wgk", D * 512), ("wgv", D * 512),
               ("r_out_w", 512 * D), ("w_out", D * D)]
_BLOCK_SPEC = [("Wqkv", D * 3 * D), ("Wo", D * D), ("W1a", D * DIP),
               ("W1g", D * DIP), ("W2p", DIP * D)]
WOFF = {}
_o = 0
for _nm, _sz in _OUTER_SPEC:
    WOFF[_nm] = _o
    _o += _sz
OB0 = _o
LOFF = {}
_b = 0
for _nm, _sz in _BLOCK_SPEC:
    LOFF[_nm] = _b
    _b += _sz
LBS = _b
WALLN = OB0 + (NLAYER + 1) * LBS
PE, DVE, A, PL, SP = ET.PE, ET.DVE, ET.Activation, ET.Pool, ET.SP


def _fm(a):
    return np.ascontiguousarray(a.T.reshape(NCH, 128, -1))


def _rope_tables(base):
    pos = np.clip(np.arange(base - HALO, base + OWN), 0, 8191).astype(np.float64)
    half = np.arange(0, 64, 2, dtype=np.float64)
    inv_freq = 1.0 / 10000.0 ** (half / 64.0)
    freqs = pos[:, None] * inv_freq[None, :]
    freqs = np.concatenate([freqs, freqs], axis=1)
    base_scale = (half + 0.4 * 64.0) / (1.4 * 64.0)
    power = (pos - 4096.0) / 128.0
    scale = base_scale[None, :] ** power[:, None]
    scale = np.concatenate([scale, scale], axis=1)
    r8 = math.sqrt(8.0)
    tabs = np.stack([np.cos(freqs) * scale * r8, np.sin(freqs) * scale * r8,
                     np.cos(freqs) / scale * r8, np.sin(freqs) / scale * r8])
    tabs = np.tile(tabs.transpose(0, 2, 1), (1, 2, 1))
    return np.ascontiguousarray(tabs.astype(np.float32))


def _win_masks(outer, first_core):
    r = np.arange(128)[:, None]
    w = np.arange(WINB)[None, :]
    lo = r + 1 if outer else r
    ok = (w >= lo) & (w <= r + 256)
    m = np.zeros((3, 128, WINB), np.float32)
    NEG = np.float32(-3e38)
    m[0] = np.where(ok, 0.0, NEG)
    if first_core:
        m[1] = np.where(ok & (w >= 256), 0.0, NEG)
        m[2] = np.where(ok & (w >= 128), 0.0, NEG)
    else:
        m[1] = m[0]
        m[2] = m[0]
    return m


def _lat_masks(base):
    m = np.zeros((16, 128, NLAT), np.float32)
    NEG = np.float32(-3e38)
    n = np.arange(NLAT)[None, :]
    for s in range(16):
        m[s] = np.where(n <= (base + 128 * s) // 128, 0.0, NEG)
    return m


def _rot_perm():
    P = np.zeros((128, 128), np.float32)
    for g in range(2):
        o = 64 * g
        for dd in range(32):
            P[o + dd + 32, o + dd] = -1.0
            P[o + dd, o + dd + 32] = 1.0
    return P


def _vec_t(v, cols):
    out = np.zeros(128 * cols, np.float64)
    out[:v.shape[0]] = v
    return np.ascontiguousarray(out.reshape(cols, 128).T.astype(np.float32))


def _prep_core_inputs(inputs, c):
    seq, sl = divmod(c, 4)
    base = OWN * sl
    x = np.asarray(inputs["L0_attn_x"][seq], np.float32)
    d = {}
    d["x_own"] = _fm(x[base:base + OWN])
    xh = np.zeros((HALO, D), np.float32)
    if base > 0:
        xh[:] = x[base - HALO:base]
    d["x_halo"] = _fm(xh)
    d["rope"] = _rope_tables(base).astype(BF16NP)
    d["mask_in"] = _win_masks(False, sl == 0).astype(BF16NP)
    d["mask_out"] = _win_masks(True, sl == 0).astype(BF16NP)
    d["mask_lat"] = _lat_masks(base).astype(BF16NP)
    return d


def _prep_shared(inputs):
    f32 = np.float32
    d = {}
    wd = {}
    for k in ("wq", "wk", "wv", "wgq", "wgk", "wgv", "r_out_w", "w_out"):
        wd[k] = np.asarray(inputs["L0_attn_" + k], f32)
    wd["Wqkv"] = np.asarray(inputs["L0_ffn_Wqkv"], f32)
    wd["Wo"] = np.asarray(inputs["L0_ffn_Wo"], f32)
    W1 = np.asarray(inputs["L0_ffn_W1"], f32)
    W1a = np.zeros((NLAYER, D, DIP), f32); W1a[:, :, :DI] = W1[:, :, :DI]
    W1g = np.zeros((NLAYER, D, DIP), f32); W1g[:, :, :DI] = W1[:, :, DI:]
    wd["W1a"], wd["W1g"] = W1a, W1g
    W2 = np.asarray(inputs["L0_ffn_W2"], f32)
    W2p = np.zeros((NLAYER, DIP, D), f32); W2p[:, :DI] = W2
    wd["W2p"] = W2p
    wall = np.zeros((1, WALLN), f32)
    for nm, sz in _OUTER_SPEC:
        wall[0, WOFF[nm]:WOFF[nm] + sz] = wd[nm].reshape(-1)
    for L in range(NLAYER):
        b0 = OB0 + L * LBS
        for nm, sz in _BLOCK_SPEC:
            wall[0, b0 + LOFF[nm]:b0 + LOFF[nm] + sz] = wd[nm][L].reshape(-1)
    d["wall"] = wall
    b1 = np.asarray(inputs["L0_ffn_b1"], f32)
    b1a = np.zeros((NLAYER, DIP), f32); b1a[:, :DI] = b1[:, :DI]
    b1g = np.zeros((NLAYER, DIP), f32); b1g[:, :DI] = b1[:, DI:]
    vecs = np.zeros((NLAYER, 128, 70), f32)
    for i in range(NLAYER):
        vecs[i, :, 0:8] = _vec_t(np.asarray(inputs["L0_ffn_ln_g"][i], f32), 8)
        vecs[i, :, 8:16] = _vec_t(np.asarray(inputs["L0_ffn_ln_b"][i], f32), 8)
        vecs[i, :, 16:24] = _vec_t(np.asarray(inputs["L0_ffn_b2"][i], f32), 8)
        vecs[i, :, 24:46] = _vec_t(b1a[i], NGC)
        vecs[i, :, 46:68] = _vec_t(b1g[i], NGC)
        vecs[i, :, 68] = np.tile(np.asarray(inputs["L0_ffn_q_scale"][i], f32), 2)
        vecs[i, :, 69] = np.tile(np.asarray(inputs["L0_ffn_k_scale"][i], f32), 2)
    d["vecsL"] = np.ascontiguousarray(vecs)
    glw = np.asarray(inputs["L0_attn_gate_local_w"], f32)
    grw = np.asarray(inputs["L0_attn_gate_remote_w"], f32)
    gB = np.zeros((128, NCH, 48), f32)
    for h in range(16):
        fc, half = divmod(h, 2)
        gB[64 * half:64 * half + 64, fc, h] = glw[h]
        gB[64 * half:64 * half + 64, fc, 32 + h] = grw[h]
    d["gateB"] = np.ascontiguousarray(gB.astype(BF16NP))
    d["ident"] = np.eye(128, dtype=f32).astype(BF16NP)
    d["perm"] = _rot_perm().astype(BF16NP)
    sstat = np.zeros((128, NCH, 16), f32)
    for fc in range(NCH):
        sstat[0:64, fc, 2 * fc] = 1.0
        sstat[64:128, fc, 2 * fc + 1] = 1.0
    d["sstat"] = np.ascontiguousarray(sstat.astype(BF16NP))
    e16 = np.zeros((16, NCH, 128), f32)
    for fc in range(NCH):
        e16[2 * fc, fc, 0:64] = 1.0
        e16[2 * fc + 1, fc, 64:128] = 1.0
    d["e16"] = np.ascontiguousarray(e16.astype(BF16NP))
    d["ones2"] = np.ones((128, 2), f32).astype(BF16NP)
    d["onesR"] = np.ones((1, 128), f32).astype(BF16NP)
    return d


class Sem:
    __slots__ = ("name", "h", "c")

    def __init__(self, name, h):
        self.name, self.h, self.c = name, h, 0


class MK:
    def __init__(self, nc, plan=None):
        self.nc = nc
        self.pass2 = plan is not None
        self.plan = plan if plan is not None else {"deltas": {}, "marks": {}}
        self.sems = {}
        self.loopstack = []
        self.scratch = {}
        self.lastwait = {}

    def sem(self, name):
        s = Sem(name, self.nc.alloc_semaphore(name=name))
        self.sems[name] = s
        return s

    def eng(self, e):
        nc = self.nc
        return {PE: nc.tensor, DVE: nc.vector, A: nc.scalar,
                PL: nc.gpsimd, SP: nc.sync}[e]

    def wreg(self, e, idx):
        key = (e, idx)
        if key not in self.scratch:
            self.scratch[key] = self.eng(e).alloc_register(f"wr{idx}")
        return self.scratch[key]

    def inc(self, inst, sem, n=1):
        inst.then_inc(sem.h, n)
        sem.c += n
        return inst

    def wait(self, e, sem, val=None):
        if val is None:
            val = sem.c
        key = (e, sem.name)
        ctx = tuple(lid for lid, _ in self.loopstack)
        prev = self.lastwait.get(key)
        if prev is not None and prev[0] == ctx and prev[1] >= val:
            return
        self.lastwait[key] = (ctx, val)
        eng = self.eng(e)
        terms = []
        for (lid, lv) in self.loopstack:
            dl = self.plan["deltas"].get((lid, sem.name), 0) if self.pass2 else 0
            if dl:
                terms.append((lv, dl))
        if not terms:
            eng.wait_ge(sem.h, max(val, 0))
            return
        r = self.wreg(e, 0)
        eng.reg_mul(r, terms[0][0], terms[0][1])
        for lv, dl in terms[1:]:
            r2 = self.wreg(e, 1)
            eng.reg_mul(r2, lv, dl)
            eng.reg_alu(r, r, r2, ALU.add)
        eng.reg_add(r, r, val)
        if val < 0:
            eng.reg_alu(r, r, 0, ALU.max)
        eng.wait_ge(sem.h, r)

    def loop(self, n, engines, tag):
        return _Loop(self, n, engines, tag)


class _Loop:
    def __init__(self, mk, n, engines, tag):
        self.mk, self.n, self.lid = mk, n, tag
        self.engines = OrderedEngineSet(engines)

    def __enter__(self):
        mk = self.mk
        self._starts = {nm: s.c for nm, s in mk.sems.items()}
        self._fori = mk.nc.Fori(0, self.n, engines=self.engines)
        self.var = self._fori.__enter__()
        mk.loopstack.append((self.lid, self.var))
        return self.var

    def __exit__(self, a_, b_, c_):
        mk = self.mk
        mk.loopstack.pop()
        mk.lastwait.clear()
        self._fori.__exit__(a_, b_, c_)
        for nm, s in mk.sems.items():
            dl = s.c - self._starts[nm]
            key = (self.lid, nm)
            if not mk.pass2:
                mk.plan["deltas"][key] = dl
            else:
                assert mk.plan["deltas"].get(key, 0) == dl, (self.lid, nm)
            s.c = self._starts[nm] + self.n * dl
        return False


class Mem:
    def __init__(self, nc):
        self.nc = nc
        self.base = (nc.sbuf_base + 31) // 32 * 32
        self.top = self.base
        self.offs = {}

    def alloc(self, name, shape, dtype, offset=None):
        esz = mybir.dt.size(dtype)
        freesz = int(np.prod(shape[1:])) * esz
        if offset is None:
            offset = self.top
            self.top = offset + ((freesz + 31) // 32) * 32
        h = self.nc.alloc_sbuf_tensor_at(name, list(shape), dtype, offset=offset)
        self.offs[name] = offset
        return h

    def finish(self):
        self.nc.sbuf_base = self.top

    def at(self, name):
        return self.offs[name]


class B:
    def __init__(self, nc, mk, stage):
        self.nc, self.mk, self.stage = nc, mk, stage
        self.n_layers = {"l2": 2, "l4": 4, "full": NLAYER}.get(stage, NLAYER)
        self._declare_dram()
        self._declare_sbuf()
        s = mk.sem
        self.sG, self.sS, self.sP, self.sA, self.sV, self.sL, self.sC = (
            s("sG"), s("sS"), s("sP"), s("sA"), s("sV"), s("sL"), s("sC"))
        self.sX = s("sX")
        self.sE = s("sE")
        self.sI = nc.alloc_semaphore(name="sI")
        self.t, self.v, self.a, self.g, self.s = (nc.tensor, nc.vector,
                                                  nc.scalar, nc.gpsimd, nc.sync)
        self.r_ipid = self.g.alloc_register("ipid")
        self.r_im4 = self.g.alloc_register("im4")
        self.r_ioff = self.g.alloc_register("ioff")
        self.r_ioff2 = self.g.alloc_register("ioff2")
        self.psF = nc.alloc_psum_tensor("psF", [128, 6, 512], F32)
        self.psB = nc.alloc_psum_tensor("psB", [128, 2, 1024], BF16)

    def _declare_dram(self):
        f = self.nc.dram_tensor
        self.x_own = f("x_own", [NCH, 128, OWN], F32, kind="ExternalInput")
        self.x_halo = f("x_halo", [NCH, 128, HALO], F32, kind="ExternalInput")
        self.rope = f("rope", [4, 128, HALO + OWN], BF16, kind="ExternalInput")
        self.mask_in = f("mask_in", [3, 128, WINB], BF16, kind="ExternalInput")
        self.mask_out = f("mask_out", [3, 128, WINB], BF16, kind="ExternalInput")
        self.mask_lat = f("mask_lat", [16, 128, NLAT], BF16, kind="ExternalInput")
        self.d_wall = f("wall", [1, WALLN], F32, kind="ExternalInput")
        self.d_vecsL = f("vecsL", [NLAYER, 128, 70], F32, kind="ExternalInput")
        for k, shp in (("gateB", [128, NCH, 48]), ("ident", [128, 128]),
                       ("perm", [128, 128]), ("sstat", [128, NCH, 16]),
                       ("e16", [16, NCH, 128]), ("ones2", [128, 2]),
                       ("onesR", [1, 128])):
            setattr(self, "d_" + k, f(k, shp, BF16, kind="ExternalInput"))
        self.y_out = f("y_out", [NCH, 128, OWN], F32, kind="ExternalOutput")
        self.x1_d = f("x1_spill", [NCH, 128, OWN], F32, kind="Internal")
        self.cc_in = f("cc_in", [D, HALO], F32, kind="Internal")
        self.cc_out = f("cc_out", [NCORES * D, HALO], F32, kind="Internal")
        self.pl_in = f("pl_in", [D, 16], BF16, kind="Internal")
        self.pl_out = f("pl_out", [NCORES * D, 16], BF16, kind="Internal")
        self.stg = [f("stg0", [1, LBS], F32, kind="Internal"),
                    f("stg1", [1, LBS], F32, kind="Internal")]
        if DEBUG_SNAP:
            self.d_dbg = f("dbg_y", [NLAYER * D, OWN], F32, kind="Internal")

    def _declare_sbuf(self):
        mem = Mem(self.nc)
        self.mem = mem
        self.y = mem.alloc("y", [128, NCH, YW], F32)
        self.kwin = mem.alloc("kwin", [128, NCH, KWINW], BF16)
        self.vt = mem.alloc("vt", [128, NVSLOT, 1024], BF16)
        self.tabs = mem.alloc("tabs", [128, 4, SEGW], BF16)
        self.qsb = mem.alloc("qsb", [128, NCH, SEGW], BF16)
        self.ao = mem.alloc("ao", [128, NCH, SEGW], BF16)
        self.xop = mem.alloc("xop", [128, NCH, SEGW], BF16)
        self.wsl = mem.alloc("wsl", [128, 2, NCH, 128], BF16)
        self.Wq_sl = mem.alloc("Wq_sl", [128, NCH, D], BF16)
        self.Wk_sl = mem.alloc("Wk_sl", [128, NCH, D], BF16)
        self.Wv_sl = mem.alloc("Wv_sl", [128, NCH, D], BF16)
        self.masks = mem.alloc("masks", [128, 3, WINB], BF16)
        self.masklat = mem.alloc("masklat", [128, 16, NLAT], BF16)
        self.ident = mem.alloc("ident", [128, 128], BF16)
        self.perm = mem.alloc("perm", [128, 128], BF16)
        self.sstat = mem.alloc("sstat", [128, NCH, 16], BF16)
        self.e16 = mem.alloc("e16", [16, NCH, 128], BF16)
        self.ones2 = mem.alloc("ones2", [128, 2], BF16)
        self.onesR = mem.alloc("onesR", [1, 128], BF16)
        self.vecs = mem.alloc("vecs", [128, 70], F32)
        self.csts = mem.alloc("csts", [128, 2], F32)
        self.sq = mem.alloc("sq", [128, 2, SEGW], BF16)
        self.attst = mem.alloc("attst", [128, 2, 8], F32)
        self.pooled = mem.alloc("pooled", [128, NCH, 16], F32)
        self.pooledB = mem.alloc("pooledB", [128, NCH, 16], BF16)
        self.latGK = mem.alloc("latGK", [128, 4, NLAT], BF16)
        self.latGV = mem.alloc("latGV", [128, 4, NLAT], BF16)
        self.latGVT = mem.alloc("latGVT", [64, 8, NLAT], BF16)
        self.T1 = mem.alloc("T1", [128, 2, WINB], BF16)
        self.PT = mem.alloc("PT", [128, 2, WINB], BF16)
        self.latscr = mem.alloc("latscr", [128, 2, NLAT], BF16)
        self.latscrT = mem.alloc("latscrT", [64, 2, 128], BF16)
        self.gateB = mem.alloc("gateB", [128, NCH, 48], BF16)
        self.lrows = mem.alloc("lrows", [1, 2560], F32)
        self.rstdF = mem.alloc("rstdF", [16, SEGW], F32)
        self.rstdH = mem.alloc("rstdH", [16, 2, SEGW], BF16)
        self.rem = mem.alloc("rem", [128, 4, SEGW], BF16)
        lro = mem.at("lrows")
        self.lrhl = mem.alloc("lrhl", [1, 2048], BF16, offset=lro + 6144)
        self.gq = mem.alloc("gq", [128, 4, SEGW], BF16, offset=mem.at("tabs"))
        self.qnF = mem.alloc("qnF", [128, SEGW], F32, offset=mem.at("T1"))
        self.xn = mem.alloc("xn", [128, NCH, OWN], BF16, offset=mem.at("Wq_sl"))
        assert mem.at("Wk_sl") == mem.at("Wq_sl") + 16384
        self.gates = mem.alloc("gates", [48, SEGW], F32, offset=lro)
        self.grbf = mem.alloc("grbf", [16, SEGW], BF16, offset=lro + 2048)
        self.glT = mem.alloc("glT", [128, 4, 16], F32, offset=lro + 3072)
        self.grB_sb = mem.alloc("grB_sb", [128, SEGW], BF16, offset=lro + 3584)
        self.glbf = mem.alloc("glbf", [16, SEGW], BF16, offset=lro + 4608)
        self.t1mix = mem.alloc("t1mix", [128, SEGW], BF16, offset=mem.at("sq"))
        self.latPF = mem.alloc("latPF", [128, NCH, NLAT], BF16, offset=mem.at("rem"))
        self.x1t = mem.alloc("x1t", [128, NCH, SEGW], F32, offset=mem.at("Wv_sl"))
        gsrc = [("Wv_sl", 0), ("Wv_sl", 4096), ("Wv_sl", 8192), ("Wv_sl", 12288),
                ("kwin", 0), ("kwin", 4096), ("kwin", 8192),
                ("vt", 0), ("vt", 4096), ("vt", 8192),
                ("qsb", 0), ("qsb", 4096), ("ao", 0), ("ao", 4096),
                ("xop", 0), ("xop", 4096), ("tabs", 0), ("rem", 0),
                ("T1", 0), ("rstdF", 0), ("lrows", 0), ("lrows", 4096)]
        assert mem.at("PT") == mem.at("T1") + 1536
        assert mem.at("gateB") == mem.at("T1") + 3840
        assert mem.at("rstdH") == mem.at("rstdF") + 2048
        self.gt = [mem.alloc(f"g{i}", [128, 4, SEGW], BF16,
                             offset=mem.at(nm) + off)
                   for i, (nm, off) in enumerate(gsrc)]
        mem.finish()
        assert mem.top <= self.nc.sbuf_top - 1024, (
            f"SBUF over budget: {mem.top} vs {self.nc.sbuf_top}")

    def W(self, e, sem, val=None):
        self.mk.wait(e, sem, val)

    def I(self, inst, sem, n=1):
        return self.mk.inc(inst, sem, n)

    @staticmethod
    def ycols(t):
        return PADW + SEGW * (t - 1)

    @staticmethod
    def hap(handle, off, dims):
        return bass.AP(handle, off, dims)

    def init_loads(self):
        s, g = self.s, self.g
        self.I(s.dma_start(self.y[:, :, PADW:YW],
                           self.x_own.ap().rearrange("c p t -> p c t")), self.sS, 16)
        for src, dst in ((self.d_ident, self.ident), (self.d_perm, self.perm),
                         (self.d_sstat, self.sstat), (self.d_e16, self.e16),
                         (self.d_ones2, self.ones2), (self.d_onesR, self.onesR),
                         (self.d_gateB, self.gateB)):
            self.I(s.dma_start(dst[:], src.ap()), self.sS, 16)
        self.I(s.dma_start(self.masklat[:],
                           self.mask_lat.ap().rearrange("s p n -> p s n")), self.sS, 16)
        self.I(s.dma_start(self.masks[:],
                           self.mask_out.ap().rearrange("s p n -> p s n")), self.sS, 16)
        for w, dst in (("wq", self.Wq_sl), ("wk", self.Wk_sl),
                       ("wv", self.Wv_sl)):
            self.I(g.dma_start(dst[:],
                               self.hap(self.d_wall, WOFF[w],
                                        [[D, 128], [128 * D, NCH], [1, D]])),
                   self.sG, 16)
        self.I(g.memset(self.xop[:], 0.0), self.sL)
        self.I(g.memset(self.y[:, :, 0:PADW], 0.0), self.sL)
        self.I(g.memset(self.csts[:, 0:1], EPS), self.sL)
        self.I(g.memset(self.csts[:, 1:2], 1e-24), self.sL)
        g.reg_load(self.r_ipid, self.nc.partition_id_tensor.ap())
        g.reg_mod(self.r_im4, self.r_ipid, 4)
        g.reg_add(self.r_ioff, self.r_ipid, NCORES - 1)
        g.reg_mod(self.r_ioff, self.r_ioff, NCORES)
        g.reg_mul(self.r_ioff, self.r_ioff, D * HALO)
        g.reg_add(self.r_ioff2, self.r_ioff, 4 * 128 * HALO)
        self.impAP0 = bass.AP(self.cc_out, self.r_ioff,
                              [[HALO, 128], [128 * HALO, 4], [1, HALO]])
        self.impAP1 = bass.AP(self.cc_out, self.r_ioff2,
                              [[HALO, 128], [128 * HALO, 4], [1, HALO]])

    def emit_ln_seg(self, t, mode):
        tt, v, a, g = self.t, self.v, self.a, self.g
        psF = self.psF
        sP, sA, sV, sL, sG, sS = self.sP, self.sA, self.sV, self.sL, self.sG, self.sS
        Wd = SEGW if t > 0 else HALO
        halo = (t == 0)
        c0 = self.ycols(t) if t > 0 else None
        if mode == "ffn":
            rawap = lambda kc: self.xn[:, kc, SEGW * (t - 1):SEGW * (t - 1) + Wd]
        else:
            rawap = lambda kc: self.xop[:, kc, 0:Wd]
        if halo:
            self.W(PL, self.sP)
            self.I(g.dma_start(self.xop[:, :, 0:HALO],
                               self.x_halo.ap().rearrange("c p t -> p c t")),
                   self.sG, 16)
            self.W(PE, sG)
            self.W(PL, sG)
        else:
            self.W(A, sP)
            self.W(A, sV)
            self.W(A, sS)
            self.W(A, sL)
            if mode == "ffn":
                self.I(a.copy(self.xn[:, 0:4, SEGW * (t - 1):SEGW * t],
                              self.y[:, 0:4, c0:c0 + Wd]), sA)
                self.I(a.copy(self.xn[:, 4:8, SEGW * (t - 1):SEGW * t],
                              self.y[:, 4:8, c0 - 1:c0 - 1 + Wd]), sA)
            else:
                self.I(a.copy(self.xop[:, :, 0:Wd], self.y[:, :, c0:c0 + Wd]), sA)
            self.W(PE, sA)
            self.W(PL, sA)
        for kc in range(NCH):
            par = kc & 1
            self.W(PL, sP)
            self.I(g.tensor_tensor(self.sq[:, par, 0:Wd], rawap(kc), rawap(kc),
                                   ALU.mult), sL)
            self.I(tt.matmul(psF[0:1, 0, 0:Wd], self.ones2[:, 0:1], rawap(kc),
                             start=(kc == 0), stop=(kc == NCH - 1)), self.sP)
            self.W(PE, sL)
            self.I(tt.matmul(psF[0:1, 1, 0:Wd], self.ones2[:, 1:2],
                             self.sq[:, par, 0:Wd],
                             start=(kc == 0), stop=(kc == NCH - 1)), self.sP)
        lr = self.lrows
        s0, s1, s2 = lr[0:1, 0:Wd], lr[0:1, 512:512 + Wd], lr[0:1, 1024:1024 + Wd]
        self.W(DVE, sP)
        self.I(v.tensor_scalar_mul(s0, psF[0:1, 0, 0:Wd], 1.0 / D), sV)
        self.W(A, sV)
        self.I(a.activation(s1, s0, AF.Square), sA)
        self.W(DVE, sA)
        self.I(v.scalar_tensor_tensor(s2, psF[0:1, 1, 0:Wd], 1.0 / D, s1,
                                      ALU.mult, ALU.subtract), sV)
        self.W(A, sV)
        self.W(A, self.sL)
        self.I(a.activation(s2, s2, AF.Ln, bias=self.csts[0:1, 0:1]), sA)
        self.W(A, sA)
        self.I(a.activation(s2, s2, AF.Exp, scale=-0.5), sA)
        self.W(DVE, sA)
        self.I(v.scalar_tensor_tensor(s1, s0, -1.0, s2, ALU.mult, ALU.mult), sV)
        hb = self.lrhl
        self.W(A, sV)
        self.I(a.copy(hb[0:1, 0:Wd], s2), sA)
        self.W(DVE, sA)
        self.I(v.tensor_sub(hb[0:1, 512:512 + Wd], s2, hb[0:1, 0:Wd]), sV)
        self.W(A, sV)
        self.I(a.copy(hb[0:1, 1024:1024 + Wd], s1), sA)
        self.W(DVE, sA)
        self.I(v.tensor_sub(hb[0:1, 1536:1536 + Wd], s1, hb[0:1, 1024:1024 + Wd]), sV)
        self.W(PE, sV)
        self.I(tt.matmul(psF[:, 2, 0:Wd], self.onesR[0:1, :], hb[0:1, 0:Wd],
                         start=True, stop=False), self.sP)
        self.I(tt.matmul(psF[:, 2, 0:Wd], self.onesR[0:1, :], hb[0:1, 512:512 + Wd],
                         start=False, stop=True), self.sP)
        self.I(tt.matmul(psF[:, 3, 0:Wd], self.onesR[0:1, :], hb[0:1, 1024:1024 + Wd],
                         start=True, stop=False), self.sP)
        self.I(tt.matmul(psF[:, 3, 0:Wd], self.onesR[0:1, :], hb[0:1, 1536:1536 + Wd],
                         start=False, stop=True), self.sP)
        for kc in range(NCH):
            if mode == "ln0":
                src = self.y[:, kc, c0:c0 + Wd]
                self.W(DVE, sP)
                self.W(DVE, sV)
                self.I(v.scalar_tensor_tensor(self.qnF[:, 0:Wd], src, 1.0,
                                              psF[:, 3, 0:Wd], ALU.mult, ALU.add), sV)
                self.W(DVE, sV)
                self.I(v.tensor_tensor(src, self.qnF[:, 0:Wd], psF[:, 2, 0:Wd],
                                       ALU.mult), sV)
            elif mode == "outer_attn":
                src = rawap(kc) if halo else self.y[:, kc, c0:c0 + Wd]
                self.W(DVE, sP)
                self.W(DVE, sV)
                self.I(v.scalar_tensor_tensor(self.qnF[:, 0:Wd], src, 1.0,
                                              psF[:, 3, 0:Wd], ALU.mult, ALU.add), sV)
                self.W(DVE, sV)
                self.I(v.tensor_tensor(rawap(kc), self.qnF[:, 0:Wd],
                                       psF[:, 2, 0:Wd], ALU.mult), sV)
            else:
                ysrc = (self.y[:, kc, c0:c0 + Wd] if kc < 4
                        else self.y[:, kc, c0 - 1:c0 - 1 + Wd])
                self.W(DVE, sP)
                self.W(DVE, sV)
                self.I(v.scalar_tensor_tensor(self.qnF[:, 0:Wd], ysrc, 1.0,
                                              psF[:, 3, 0:Wd], ALU.mult, ALU.add), sV)
                self.W(DVE, sV)
                self.W(DVE, sS)
                self.I(v.scalar_tensor_tensor(self.qnF[:, 0:Wd], self.qnF[:, 0:Wd],
                                              self.vecs[:, kc:kc + 1],
                                              psF[:, 2, 0:Wd], ALU.mult, ALU.mult), sV)
                self.W(DVE, sV)
                self.I(v.tensor_scalar_add(rawap(kc), self.qnF[:, 0:Wd],
                                           self.vecs[:, 8 + kc:9 + kc]), sV)

    def emit_qk_side(self, t, side):
        tt, v, a, g = self.t, self.v, self.a, self.g
        psF = self.psF
        sP, sA, sV, sL, sG, sS = self.sP, self.sA, self.sV, self.sL, self.sG, self.sS
        Wd = SEGW if t > 0 else HALO
        Wsl = self.Wk_sl if side == "k" else self.Wq_sl
        if side == "k":
            cur0 = 0 if t == 0 else HALO
            dst = lambda fc: self.kwin[:, fc, cur0:cur0 + Wd]
        else:
            dst = lambda fc: self.qsb[:, fc, 0:Wd]
        sA_e, sV_e, sL_e, sP_e, sG_e = sA.c, sV.c, sL.c, sP.c, sG.c
        p_mm, a_cp, l_sq, p_st = {}, {}, {}, {}

        def emit_mm8(fc):
            par = fc & 1
            self.W(PE, sA, max(sA_e, a_cp.get(fc - 2, 0)))
            self.W(PE, sV, sV_e)
            self.W(PE, sG, sG_e)
            self.W(PE, sL, sL_e)
            last = None
            for kc in range(NCH):
                last = tt.matmul(psF[:, par, 0:Wd], Wsl[:, kc, ts(fc, 128)],
                                 self.xop[:, kc, 0:Wd],
                                 start=(kc == 0), stop=(kc == NCH - 1))
            self.I(last, sP)
            p_mm[fc] = sP.c

        emit_mm8(0)
        for fc in range(NCH):
            if fc + 1 < NCH:
                emit_mm8(fc + 1)
            par = fc & 1
            self.W(A, sP, p_mm[fc])
            self.W(A, sL, sL_e)
            self.I(a.copy(dst(fc), psF[:, par, 0:Wd]), sA)
            a_cp[fc] = sA.c
            self.W(PL, sA, a_cp[fc])
            self.W(PL, sP, p_st.get(fc - 2, sP_e))
            self.I(g.tensor_tensor(self.sq[:, par, 0:Wd], dst(fc), dst(fc),
                                   ALU.mult), sL)
            l_sq[fc] = sL.c
            self.W(PE, sL, l_sq[fc])
            self.I(tt.matmul(psF[0:16, 2, 0:Wd], self.sstat[:, fc, :],
                             self.sq[:, par, 0:Wd], start=(fc == 0),
                             stop=(fc == NCH - 1)), sP)
            p_st[fc] = sP.c
        self.W(A, sP)
        self.W(A, self.sL)
        self.I(a.activation(self.rstdF[:, 0:Wd], psF[0:16, 2, 0:Wd], AF.Ln,
                            bias=self.csts[0:16, 1:2]), sA)
        self.W(A, sA)
        self.I(a.activation(self.rstdF[:, 0:Wd], self.rstdF[:, 0:Wd], AF.Exp,
                            scale=-0.5), sA)
        self.W(A, sA)
        self.I(a.copy(self.rstdH[:, 0, 0:Wd], self.rstdF[:, 0:Wd]), sA)
        self.W(DVE, sA)
        self.I(v.tensor_sub(self.rstdH[:, 1, 0:Wd], self.rstdF[:, 0:Wd],
                            self.rstdH[:, 0, 0:Wd]), sV)
        scol = 68 if side == "q" else 69
        tb = 0 if side == "q" else 2
        a_r2, v_r2, l_e2, s_e2 = sA.c, sV.c, sL.c, self.sS.c
        p_bc, v_stt2, v_sq0, l_sq1, p_rot, a_cp2 = {}, {}, {}, {}, {}, {}

        def emit_bc(fc):
            bb = 2 + 3 * (fc & 1)
            self.W(PE, sA, max(a_r2, a_cp2.get(fc - 2, 0)))
            self.W(PE, sV, max(v_r2, v_stt2.get(fc - 2, 0)))
            self.I(tt.matmul(psF[:, bb, 0:Wd], self.e16[:, fc, :],
                             self.rstdH[:, 0, 0:Wd], start=True, stop=False), sP)
            self.I(tt.matmul(psF[:, bb, 0:Wd], self.e16[:, fc, :],
                             self.rstdH[:, 1, 0:Wd], start=False, stop=True), sP)
            p_bc[fc] = sP.c

        emit_bc(0)
        for fc in range(NCH):
            if fc + 1 < NCH:
                emit_bc(fc + 1)
            par = fc & 1
            bb = 2 + 3 * (fc & 1)
            self.W(DVE, sP, max(p_bc[fc], p_rot.get(fc - 1, sP_e)))
            self.W(DVE, sL, l_sq1.get(fc - 1, l_e2))
            self.W(DVE, sS, s_e2)
            self.W(DVE, sA, a_cp[fc])
            self.I(v.scalar_tensor_tensor(self.qnF[:, 0:Wd], dst(fc),
                                          self.vecs[:, scol:scol + 1],
                                          psF[:, bb, 0:Wd], ALU.mult, ALU.mult), sV)
            v_stt2[fc] = sV.c
            self.W(DVE, sV)
            self.I(v.tensor_tensor(self.sq[:, 0, 0:Wd], self.qnF[:, 0:Wd],
                                   self.tabs[:, tb + 1, 0:Wd], ALU.mult), sV)
            v_sq0[fc] = sV.c
            self.W(PL, sV, v_stt2[fc])
            self.W(PL, sP, p_rot.get(fc - 1, sP_e))
            self.I(g.tensor_tensor(self.sq[:, 1, 0:Wd], self.qnF[:, 0:Wd],
                                   self.tabs[:, tb, 0:Wd], ALU.mult), sL)
            l_sq1[fc] = sL.c
            self.W(PE, sV, v_sq0[fc])
            self.W(PE, sL, l_sq1[fc])
            self.W(PE, sA, a_cp2.get(fc - 2, a_r2))
            self.I(tt.matmul(psF[:, 3 + par, 0:Wd], self.perm[:],
                             self.sq[:, 0, 0:Wd], start=True, stop=False), sP)
            self.I(tt.matmul(psF[:, 3 + par, 0:Wd], self.ident[:],
                             self.sq[:, 1, 0:Wd], start=False, stop=True), sP)
            p_rot[fc] = sP.c
            self.W(A, sP, p_rot[fc])
            self.I(a.copy(dst(fc), psF[:, 3 + par, 0:Wd]), sA)
            a_cp2[fc] = sA.c

    def emit_qk_outer(self, t, side):
        tt, a = self.t, self.a
        psF = self.psF
        sP, sA, sG, sV = self.sP, self.sA, self.sG, self.sV
        Wd = SEGW if t > 0 else HALO
        Wsl = self.Wk_sl if side == "k" else self.Wq_sl
        if side == "k":
            cur0 = 0 if t == 0 else HALO
            dst = lambda fc: self.kwin[:, fc, cur0:cur0 + Wd]
        else:
            dst = lambda fc: self.qsb[:, fc, 0:Wd]
        for fc in range(NCH):
            par = fc & 1
            self.W(PE, sA)
            self.W(PE, sV)
            self.W(PE, sG)
            self.W(PE, self.sL)
            last = None
            for kc in range(NCH):
                last = tt.matmul(psF[:, par, 0:Wd], Wsl[:, kc, ts(fc, 128)],
                                 self.xop[:, kc, 0:Wd],
                                 start=(kc == 0), stop=(kc == NCH - 1))
            self.I(last, sP)
            self.W(A, sP)
            self.I(a.activation(dst(fc), psF[:, par, 0:Wd], AF.Copy,
                                scale=(0.125 if side == "k" else 1.0)), sA)

    def emit_v(self, t):
        tt, a = self.t, self.a
        psF = self.psF
        sP, sA = self.sP, self.sA
        Wd = SEGW if t > 0 else HALO
        ntc = Wd // 128
        g0 = 0 if t == 0 else 3 + 4 * (t - 1)
        sA_ev = sA.c
        a_ev, p_v = {}, {}
        for tc in range(ntc):
            slot = (g0 + tc) % NVSLOT
            for half in range(2):
                i = tc * 2 + half
                par = i & 1
                self.W(PE, sA, max(sA_ev, a_ev.get(i - 2, 0)))
                last = None
                for kc in range(NCH):
                    last = tt.matmul(psF[:, 3 + par, 0:512],
                                     self.xop[:, kc, ts(tc, 128)],
                                     self.Wv_sl[:, kc, ts(half, 512)],
                                     start=(kc == 0), stop=(kc == NCH - 1))
                self.I(last, sP)
                p_v[i] = sP.c
                self.W(A, sP, p_v[i])
                self.I(a.copy(self.vt[:, slot, ts(half, 512)],
                              psF[:, 3 + par, 0:512]), sA)
                a_ev[i] = sA.c

    def emit_attn_subtile(self, t, sl, outer, tag):
        mk = self.mk
        tt, v, a, g = self.t, self.v, self.a, self.g
        psF, psB = self.psF, self.psB
        sP, sA, sV, sL = self.sP, self.sA, self.sV, self.sL
        if t == 0:
            q0c, w0, gsub, msl = 256, 0, None, 0
        else:
            q0c = 128 * sl
            w0 = HALO + 128 * sl - 256
            gsub = 4 * (t - 1) + sl
            msl = 1 if (t == 1 and sl == 0) else (2 if (t == 1 and sl == 1) else 0)
        s_glob = (3 + 4 * (t - 1) + sl) if t > 0 else 2
        cslots = [((s_glob - 2 + i) % NVSLOT) for i in range(3)]
        sA0, sV0, sP0 = sA.c, sV.c, sP.c
        v_stt, a_exp, a_pt, p_S, p_tr = {}, {}, {}, {}, {}

        SB = (0, 1, 4, 5)

        def emit_S(hv):
            fcv, half = divmod(hv, 2)
            qap = self.qsb[64 * half:64 * half + 64, fcv, q0c:q0c + 128]
            kap = self.kwin[64 * half:64 * half + 64, fcv, w0:w0 + WINB]
            self.W(PE, sV, v_stt.get(hv - 4, sV0))
            self.W(PE, sA, sA0)
            self.I(tt.matmul(psF[:, SB[hv & 3], 0:WINB], qap, kap, start=True,
                             stop=True, skip_group_check=True), sP)
            p_S[hv] = sP.c

        emit_S(0)
        emit_S(1)
        for hv in range(16):
            if hv + 2 < 16:
                emit_S(hv + 2)
            fcv, half = divmod(hv, 2)
            par = hv & 1
            self.W(DVE, sP, max(p_S[hv], p_tr.get(hv - 2, sP0)))
            self.W(DVE, sA, a_exp.get(hv - 2, sA0))
            self.I(v.scalar_tensor_tensor(self.T1[:, par, :],
                                          psF[:, SB[hv & 3], 0:WINB],
                                          1.0, self.masks[:, msl, :],
                                          ALU.mult, ALU.add), sV)
            v_stt[hv] = sV.c
            self.W(DVE, sV)
            self.I(v.reduce_max(self.attst[:, par, 0:1], self.T1[:, par, :],
                                axis=AX, negate=True), sV)
            self.W(A, sV)
            self.I(a.activation(self.T1[:, par, :], self.T1[:, par, :], AF.Exp,
                                bias=self.attst[:, par, 0:1],
                                accum_out=self.attst[:, par, 1:2]), sA)
            a_exp[hv] = sA.c
            self.W(DVE, sA)
            self.I(v.reciprocal(self.attst[:, par, 2:3],
                                self.attst[:, par, 1:2]), sV)
            self.W(DVE, sV)
            if outer:
                self.I(v.tensor_scalar(self.T1[:, par, :], self.T1[:, par, :],
                                       self.attst[:, par, 2:3],
                                       self.glT[:, sl, hv:hv + 1],
                                       ALU.mult, ALU.mult), sV)
            else:
                self.I(v.tensor_scalar_mul(self.T1[:, par, :],
                                           self.T1[:, par, :],
                                           self.attst[:, par, 2:3]), sV)
            v_scale = sV.c
            self.W(PE, sV, v_scale)
            self.W(PE, sA, a_pt.get(hv - 2, sA0))
            for c in range(3):
                self.I(tt.transpose(psB[:, par, ts(c, 128)],
                                    self.T1[:, par, ts(c, 128)],
                                    self.ident[:]), sP)
            p_tr[hv] = sP.c
            self.W(A, sP, p_tr[hv])
            self.I(a.copy(self.PT[:, par, :], psB[:, par, 0:WINB]), sA)
            a_pt[hv] = sA.c
            self.W(PE, sA, a_pt[hv])
            last = None
            for c in range(3):
                vap = self.vt[:, cslots[c], ts(hv, 64)]
                last = tt.matmul(psF[0:64, 2 + par, 0:128], vap,
                                 self.PT[:, par, ts(c, 128)],
                                 start=(c == 0), stop=(c == 2),
                                 skip_group_check=True)
            self.I(last, sP)
            self.W(A, sP)
            self.I(a.copy(self.ao[64 * half:64 * half + 64, fcv,
                                  q0c:q0c + 128], psF[0:64, 2 + par, 0:128]), sA)
        if outer and t > 0:
            self.emit_latent_pair(t, sl, gsub)

    def emit_latent_pair(self, t, sl, gsub):
        tt, v, a, g = self.t, self.v, self.a, self.g
        psF, psB = self.psF, self.psB
        sP, sA, sV, sL = self.sP, self.sA, self.sV, self.sL
        q0c = 128 * sl
        for lh in range(8):
            j, w = divmod(lh, 2)
            lr = slice(64 * w, 64 * w + 64)
            self.W(PE, sA)
            self.W(PE, sV)
            self.I(tt.matmul(psF[:, 4, 0:NLAT],
                             self.gq[lr, j, q0c:q0c + 128],
                             self.latGK[lr, j, :],
                             start=True, stop=True, skip_group_check=True), sP)
            self.W(DVE, sP)
            self.W(DVE, sA)
            self.I(v.scalar_tensor_tensor(self.latscr[:, w, :], psF[:, 4, 0:NLAT],
                                          1.0, self.masklat[:, gsub, :],
                                          ALU.mult, ALU.add), sV)
            self.W(DVE, sV)
            self.I(v.reduce_max(self.attst[:, w, 4:5], self.latscr[:, w, :],
                                axis=AX, negate=True), sV)
            self.W(A, sV)
            self.I(a.activation(self.latscr[:, w, :], self.latscr[:, w, :], AF.Exp,
                                bias=self.attst[:, w, 4:5],
                                accum_out=self.attst[:, w, 5:6]), sA)
            self.W(DVE, sA)
            self.I(v.reciprocal(self.attst[:, w, 6:7], self.attst[:, w, 5:6]), sV)
            self.W(DVE, sV)
            self.I(v.tensor_scalar_mul(self.latscr[:, w, :], self.latscr[:, w, :],
                                       self.attst[:, w, 6:7]), sV)
            self.W(PE, sV)
            self.W(PE, sA)
            self.I(tt.transpose(psB[0:64, w, 384:512], self.latscr[:, w, :],
                                self.ident[:]), sP)
            self.W(A, sP)
            self.I(a.copy(self.latscrT[:, w, :], psB[0:64, w, 384:512]), sA)
            self.W(PE, sA)
            self.I(tt.matmul(psF[0:64, 5, 0:128], self.latGVT[:, lh, :],
                             self.latscrT[:, w, :], start=True, stop=True,
                             skip_group_check=True), sP)
            self.W(A, sP)
            self.I(a.copy(self.rem[lr, ds(j, 1), q0c:q0c + 128],
                          psF[0:64, 5, 0:128]), sA)

    def emit_halo_proj(self, Lv):
        tt, v, g = self.t, self.v, self.g
        psF = self.psF
        sP, sA, sV, sG = self.sP, self.sA, self.sV, self.sG
        for fc in range(NCH):
            par = fc & 1
            self.W(PL, sP)
            src = self.hap(self.stg[Lv], LOFF["Wo"] + fc * 128,
                           [[D, 128], [128 * D, NCH], [1, 128]])
            self.I(g.dma_start(self.wsl[:, par, :, :], src), sG, 16)
            self.W(PE, sG)
            self.W(PE, sV)
            self.W(PE, sA)
            last = None
            for kc in range(NCH):
                last = tt.matmul(psF[:, par, 0:1], self.wsl[:, par, kc, :],
                                 self.ao[:, kc, 383:384],
                                 start=(kc == 0), stop=(kc == NCH - 1))
            self.I(last, sP)
            self.W(A, sP)
            self.W(A, sV)
            self.I(self.a.copy(self.pooled[:, fc, 0:1], psF[:, par, 0:1]), sA)

    def emit_halo_add(self):
        self.W(DVE, self.sA)
        self.W(DVE, self.sV)
        self.I(self.v.tensor_tensor(self.y[:, :, PADW - 1:PADW],
                                    self.y[:, :, PADW - 1:PADW],
                                    self.pooled[:, :, 0:1], ALU.add), self.sV)

    def emit_out_proj(self, t, Lv, outer):
        tt, v, a, g = self.t, self.v, self.a, self.g
        psF = self.psF
        sP, sA, sV, sG = self.sP, self.sA, self.sV, self.sG
        Wd = SEGW
        c0 = self.ycols(t)
        if outer:
            for fc in range(NCH):
                par = fc & 1
                self.W(PL, sP)
                self.I(g.dma_start(
                    self.wsl[:, par, 0:4, :],
                    self.hap(self.d_wall, WOFF["r_out_w"] + fc * 128,
                             [[D, 128], [128 * D, 4], [1, 128]])), sG, 16)
                self.W(PE, sG)
                self.W(PE, sV)
                self.W(PE, sA)
                last = None
                for kc in range(4):
                    last = tt.matmul(psF[:, 2 + par, 0:Wd], self.wsl[:, par, kc, :],
                                     self.rem[:, kc, 0:Wd],
                                     start=(kc == 0), stop=(kc == 3))
                self.I(last, sP)
                self.I(tt.matmul(psF[:, 4 + par, 0:Wd], self.e16[:, fc, :],
                                 self.grbf[:, 0:Wd], start=True, stop=True,
                                 skip_group_check=True), sP)
                self.W(A, sP)
                self.I(a.copy(self.grB_sb[:, 0:Wd], psF[:, 4 + par, 0:Wd]), sA)
                self.W(DVE, sA)
                self.I(v.tensor_tensor(self.t1mix[:, 0:Wd], psF[:, 2 + par, 0:Wd],
                                       self.grB_sb[:, 0:Wd], ALU.mult), sV)
                self.W(DVE, sV)
                self.I(v.tensor_tensor(self.ao[:, fc, 0:Wd], self.t1mix[:, 0:Wd],
                                       self.ao[:, fc, 0:Wd], ALU.add), sV)
        sP_o, sV_o, sA_o = sP.c, sV.c, sA.c
        p_o, v_res, g_o = {}, {}, {}
        for fc in range(NCH):
            par = fc & 1
            self.W(PL, sP, p_o.get(fc - 2, sP_o))
            if fc:
                self.W(PL, sG, g_o[fc - 1])
            else:
                self.W(PL, sG)
            if outer:
                src = self.hap(self.d_wall, WOFF["w_out"] + fc * 128,
                               [[D, 128], [128 * D, NCH], [1, 128]])
            else:
                src = self.hap(self.stg[Lv], LOFF["Wo"] + fc * 128,
                               [[D, 128], [128 * D, NCH], [1, 128]])
            self.I(g.dma_start(self.wsl[:, par, :, :], src), sG, 16)
            g_o[fc] = sG.c
            self.W(PE, sG, g_o[fc])
            self.W(PE, sV, max(sV_o, v_res.get(fc - 2, 0)))
            self.W(PE, sA, sA_o)
            last = None
            for kc in range(NCH):
                last = tt.matmul(psF[:, par, 0:Wd], self.wsl[:, par, kc, :],
                                 self.ao[:, kc, 0:Wd],
                                 start=(kc == 0), stop=(kc == NCH - 1))
            self.I(last, sP)
            p_o[fc] = sP.c
            self.W(DVE, sP, p_o[fc])
            self.I(v.tensor_tensor(self.y[:, fc, c0:c0 + Wd],
                                   self.y[:, fc, c0:c0 + Wd],
                                   psF[:, fc & 1, 0:Wd], ALU.add), sV)
            v_res[fc] = sV.c

    def emit_gates(self, t):
        tt, v, a = self.t, self.v, self.a
        psF, psB = self.psF, self.psB
        sP, sA, sV = self.sP, self.sA, self.sV
        Wd = SEGW
        for fc in range(NCH):
            self.W(PE, sA)
            self.I(tt.matmul(psF[0:48, 5, 0:Wd], self.gateB[:, fc, :],
                             self.qsb[:, fc, 0:Wd], start=(fc == 0),
                             stop=(fc == NCH - 1)), sP)
        self.W(A, sP)
        self.I(a.activation(self.gates[:, 0:Wd], psF[0:48, 5, 0:Wd], AF.Exp,
                            scale=-1.0), sA)
        self.W(DVE, sA)
        self.I(v.tensor_scalar_add(self.gates[:, 0:Wd], self.gates[:, 0:Wd], 1.0), sV)
        self.W(DVE, sV)
        self.I(v.reciprocal(self.gates[:, 0:Wd], self.gates[:, 0:Wd]), sV)
        self.W(A, sV)
        self.I(a.copy(self.grbf[:, 0:Wd], self.gates[32:48, 0:Wd]), sA)
        self.W(A, sA)
        self.I(a.copy(self.glbf[:, 0:Wd], self.gates[0:16, 0:Wd]), sA)
        for sl in range(4):
            self.W(PE, sA)
            self.I(tt.transpose(psB[0:128, 0, 512:528],
                                self.glbf[0:16, ts(sl, 128)], self.ident[0:16, 0:16]), sP)
            self.W(A, sP)
            self.I(a.copy(self.glT[:, sl, :], psB[0:128, 0, 512:528]), sA)

    def emit_latent_prep(self):
        nc = self.nc
        tt, v, a, g, s = self.t, self.v, self.a, self.g, self.s
        psF, psB = self.psF, self.psB
        sP, sA, sV, sL, sG, sS, sC = (self.sP, self.sA, self.sV, self.sL,
                                      self.sG, self.sS, self.sC)
        for t in range(1, 5):
            self.emit_ln_seg(t, "outer_attn")
            for kc in range(NCH):
                self.W(DVE, sV)
                self.I(v.reduce_sum(
                    self.pooled[:, kc, ts(t - 1, 4)],
                    self.xop[:, kc, 0:SEGW].rearrange("p (l w) -> p l w", w=128),
                    axis=AX), sV)
        self.W(DVE, sV)
        self.I(v.tensor_scalar_mul(self.pooledB[:], self.pooled[:], 1.0 / 128.0), sV)
        self.W(SP, sV)
        self.I(s.dma_start(self.pl_in.ap().rearrange("(c p) l -> p c l", p=128),
                           self.pooledB[:]), self.sE, 16)
        self.W(PL, self.sE)
        self.I(g.collective_compute(
            "AllGather", ALU.bypass, replica_groups=[list(range(NCORES))],
            ins=[self.pl_in.ap().opt()], outs=[self.pl_out.ap().opt()]), self.sC)
        self.W(PL, sC)
        with g.register("pid") as pid, g.register("off") as off, \
                g.register("off2") as off2:
            g.reg_load(pid, nc.partition_id_tensor.ap())
            g.reg_div(off, pid, 4)
            g.reg_mul(off, off, 4 * D * 16)
            for r in range(4):
                g.reg_add(off2, off, r * D * 16)
                src = bass.AP(self.pl_out, off2,
                              [[16, 128], [128 * 16, NCH], [1, 16]])
                dst = bass.AP(self.latPF, r * 16, [[512, 128], [64, NCH], [1, 16]])
                self.I(g.dma_start(dst, src), self.sG, 16)
        for wnm, dstt in (("wgk", self.latGK), ("wgv", self.latGV)):
            for fc in range(4):
                par = fc & 1
                self.W(PL, sP)
                self.I(g.dma_start(
                    self.wsl[:, par, :, :],
                    self.hap(self.d_wall, WOFF[wnm] + fc * 128,
                             [[512, 128], [128 * 512, NCH], [1, 128]])),
                    sG, 16)
                self.W(PE, sG)
                self.W(PE, sA)
                last = None
                for kc in range(NCH):
                    last = tt.matmul(psF[:, 4, 0:NLAT], self.wsl[:, par, kc, :],
                                     self.latPF[:, kc, :],
                                     start=(kc == 0), stop=(kc == NCH - 1))
                self.I(last, sP)
                self.W(A, sP)
                self.I(a.copy(dstt[:, fc, :], psF[:, 4, 0:NLAT]), sA)
        for lh in range(8):
            lfc, lhalf = divmod(lh, 2)
            self.W(PE, sA)
            self.I(tt.transpose(psB[0:64, 0, 512:576],
                                self.latGV[64 * lhalf:64 * lhalf + 64, lfc, :],
                                self.ident[64 * lhalf:64 * lhalf + 64,
                                           64 * lhalf:64 * lhalf + 64]), sP)
            self.W(A, sP)
            self.I(a.copy(self.latGVT[:, lh, :], psB[0:64, 0, 512:576]), sA)

    def emit_gq(self, t):
        tt, g, a = self.t, self.g, self.a
        psF = self.psF
        sP, sA, sG = self.sP, self.sA, self.sG
        Wd = SEGW
        for fc in range(4):
            par = fc & 1
            self.W(PL, sP)
            self.I(g.dma_start(
                self.wsl[:, par, :, :],
                self.hap(self.d_wall, WOFF["wgq"] + fc * 128,
                         [[512, 128], [128 * 512, NCH], [1, 128]])),
                sG, 16)
            self.W(PE, sG)
            self.W(PE, sA)
            last = None
            for kc in range(NCH):
                last = tt.matmul(psF[:, 2 + par, 0:Wd], self.wsl[:, par, kc, :],
                                 self.xop[:, kc, 0:Wd],
                                 start=(kc == 0), stop=(kc == NCH - 1))
            self.I(last, sP)
            self.W(A, sP)
            self.I(a.copy(self.gq[:, fc, 0:Wd], psF[:, 2 + par, 0:Wd]), sA)

    def emit_bridge(self, t):
        if t < 2:
            return
        self.W(PL, self.sP)
        self.I(self.g.tensor_copy(self.kwin[:, :, 0:HALO],
                                  self.kwin[:, :, SEGW:KWINW]), self.sL)

    def emit_outer(self):
        self.emit_latent_prep()
        for t in range(5):
            self.emit_ln_seg(t, "outer_attn")
            self.emit_bridge(t)
            self.emit_qk_outer(t, "k")
            self.emit_v(t)
            if t > 0:
                self.emit_qk_outer(t, "q")
                self.emit_gq(t)
                self.emit_gates(t)
                for sl in range(4):
                    self.emit_attn_subtile(t, sl, True, f"oat{t}_{sl}")
                self.emit_out_proj(t, None, True)

    def emit_layer_body(self, L):
        nc, mk = self.nc, self.mk
        tt, v, a, g, s = self.t, self.v, self.a, self.g, self.s
        sP, sA, sV, sL, sG, sS, sC = (self.sP, self.sA, self.sV, self.sL,
                                      self.sG, self.sS, self.sC)
        Lv = L % 2
        stg = self.stg[Lv]
        self.W(PL, sC)
        self.W(PL, sP)
        self.W(PL, sA)
        with g.If_eq(self.r_im4, 0):
            g.memset(self.xop[:, :, 0:HALO + 1], 0.0)
            g.memset(self.y[:, :, PADW - 1:PADW], 0.0)
        with g.Else():
            g.memset(self.xop[:, 4:8, 0:1], 0.0)
            g.dma_start(self.xop[:, 0:4, 0:HALO],
                        self.impAP0).then_inc(self.sI, 16)
            g.dma_start(self.xop[:, 4:8, 1:HALO + 1],
                        self.impAP1).then_inc(self.sI, 16)
            g.wait_ge(self.sI, 32 * (L + 1))
            g.tensor_copy(self.y[:, 0:4, PADW - 1:PADW],
                          self.xop[:, 0:4, HALO - 1:HALO])
            g.tensor_copy(self.y[:, 4:8, PADW - 1:PADW],
                          self.xop[:, 4:8, HALO:HALO + 1])
        self.I(g.memset(self.attst[0:1, 0, 7:8], 0.0), sL)
        self.W(PL, self.sX)
        for third, dstw in ((0, self.Wq_sl), (1, self.Wk_sl), (2, self.Wv_sl)):
            src = self.hap(stg, LOFF["Wqkv"] + third * D,
                           [[3 * D, 128], [128 * 3 * D, NCH], [1, D]])
            self.I(g.dma_start(dstw[:], src), sG, 16)
        self.W(PL, self.sG)
        self.I(g.dma_start(self.stg[1 - Lv].ap(),
                           self.hap(self.d_wall, OB0 + (L + 1) * LBS,
                                    [[1, 1], [1, LBS]])),
               self.sX, 16)
        self.W(SP, sV)
        self.W(SP, sA)
        self.I(s.dma_start(self.vecs[:],
                           self.hap(self.d_vecsL, L * 128 * 70,
                                    [[70, 128], [1, 70]])),
               sS, 16)
        for t in range(5):
            Wd = SEGW if t > 0 else HALO
            if t > 0:
                c0 = self.ycols(t)
                self.W(A, sP)
                self.W(A, sV)
                self.W(A, sL)
                self.I(a.copy(self.xop[:, 0:4, 0:Wd], self.y[:, 0:4, c0:c0 + Wd]), sA)
                self.I(a.copy(self.xop[:, 4:8, 0:Wd],
                              self.y[:, 4:8, c0 - 1:c0 - 1 + Wd]), sA)
            tcol = 0 if t == 0 else HALO + SEGW * (t - 1)
            self.W(SP, sP)
            self.W(SP, sV)
            self.W(SP, sL)
            self.I(s.dma_start(self.tabs[:, :, 0:Wd],
                               self.rope.ap().rearrange("s p t -> p s t")[:, :, tcol:tcol + Wd]),
                   sS, 16)
            self.emit_bridge(t)
            self.emit_qk_side(t, "k")
            self.emit_v(t)
            self.emit_qk_side(t, "q")
            subs = [2] if t == 0 else [0, 1, 2, 3]
            for sl in subs:
                self.emit_attn_subtile(t, sl, False, f"iat{t}_{sl}")
            if t == 0:
                self.emit_halo_proj(Lv)
            else:
                self.emit_out_proj(t, Lv, False)
        self.emit_halo_add()
        for t in range(1, 5):
            self.emit_ln_seg(t, "ffn")
        self.emit_ffn(Lv)
        if DEBUG_SNAP:
            self.W(PL, sV)
            self.W(PL, sP)
            self.I(g.dma_start(
                bass.AP(self.d_dbg, L * NCH * 128 * OWN,
                        [[NCH * OWN, 128], [OWN, NCH], [1, OWN]]),
                self.y[:, :, PADW:YW]), sG, 16)
        self.W(SP, sV)
        self.W(SP, sC)
        self.I(s.dma_start(self.cc_in.ap().rearrange("(c p) t -> p c t", p=128),
                           self.y[:, :, YW - HALO:YW]), self.sE, 16)
        self.W(PL, self.sE)
        self.I(g.collective_compute(
            "AllGather", ALU.bypass, replica_groups=[list(range(NCORES))],
            ins=[self.cc_in.ap().opt()], outs=[self.cc_out.ap().opt()]), sC)

    def emit_ffn(self, Lv):
        mk = self.mk
        tt, v, a, g = self.t, self.v, self.a, self.g
        psF = self.psF
        sP, sA, sV, sG = self.sP, self.sA, self.sV, self.sG
        sV_f, sA_f, sS_f = sV.c, sA.c, self.sS.c
        bank_rV, bank_rA, v_sqw = {}, {}, {}
        for gc in range(NGC):
            for ab in range(2):
                wo = LOFF["W1a"] if ab == 0 else LOFF["W1g"]
                self.W(PL, sP)
                self.I(g.dma_start(
                    self.wsl[:, ab, :, :],
                    self.hap(self.stg[Lv], wo + gc * 128,
                             [[DIP, 128], [128 * DIP, NCH], [1, 128]])), sG, 16)
            g_ld = sG.c
            for t in range(1, 5):
                p_ab = {}
                for ab in range(2):
                    self.W(PE, sG, g_ld)
                    if ab:
                        self.W(PE, sA, bank_rA.get(t & 1, 0))
                    else:
                        self.W(PE, sV, max(sV_f, bank_rV.get(t & 1, 0)))
                    last = None
                    for kc in range(NCH):
                        last = tt.matmul(psF[:, ab * 2 + (t & 1), 0:SEGW],
                                         self.wsl[:, ab, kc, :],
                                         self.xn[:, kc, ts(t - 1, SEGW)],
                                         start=(kc == 0), stop=(kc == NCH - 1),
                                         skip_group_check=True)
                    self.I(last, sP)
                    p_ab[ab] = sP.c
                self.W(A, sP, p_ab[1])
                self.W(A, sV, v_sqw.get(t & 1, sV_f))
                self.W(A, self.sS, sS_f)
                self.I(a.activation(self.sq[:, t & 1, :], psF[:, 2 + (t & 1), 0:SEGW],
                                    AF.Gelu, bias=self.vecs[:, 46 + gc:47 + gc]), sA)
                bank_rA[t & 1] = sA.c
                self.W(DVE, sP, p_ab[0])
                self.W(DVE, sA)
                self.W(DVE, self.sS, sS_f)
                self.I(v.scalar_tensor_tensor(self.gt[gc][:, t - 1, :],
                                              psF[:, (t & 1), 0:SEGW],
                                              self.vecs[:, 24 + gc:25 + gc],
                                              self.sq[:, t & 1, :],
                                              ALU.add, ALU.mult), sV)
                bank_rV[t & 1] = sV.c
                v_sqw[t & 1] = sV.c
        for fc in range(NCH):
            for grp in range(3):
                kcs = 8 if grp < 2 else 6
                self.W(PL, sP)
                self.I(g.dma_start(
                    self.wsl[:, grp & 1, 0:kcs, :],
                    self.hap(self.stg[Lv],
                             LOFF["W2p"] + grp * 8 * 128 * D + fc * 128,
                             [[D, 128], [128 * D, kcs], [1, 128]])), sG, 16)
                for t in range(1, 5):
                    self.W(PE, sG)
                    if grp == 0:
                        self.W(PE, sV)
                    last = None
                    for kc in range(kcs):
                        last = tt.matmul(psF[:, 2 + (t - 1), 0:SEGW],
                                         self.wsl[:, grp & 1, kc, :],
                                         self.gt[grp * 8 + kc][:, t - 1, :],
                                         start=(grp == 0 and kc == 0),
                                         stop=(grp == 2 and kc == kcs - 1),
                                         skip_group_check=True)
                    self.I(last, sP)
            for t in range(1, 5):
                c0 = self.ycols(t)
                yap = self.y[:, fc, c0:c0 + SEGW]
                self.W(DVE, sP)
                self.W(DVE, self.sS)
                self.I(v.scalar_tensor_tensor(yap, psF[:, 2 + (t - 1), 0:SEGW],
                                              self.vecs[:, 16 + fc:17 + fc],
                                              yap, ALU.add, ALU.add), sV)

    def emit_final(self):
        v, s = self.v, self.s
        self.W(DVE, self.sE)
        if DEBUG_SNAP:
            self.W(DVE, self.sG)
        for t in range(1, 5):
            c0 = self.ycols(t)
            self.W(SP, self.sV)
            self.W(SP, self.sP)
            self.I(s.dma_start(
                self.x1t[:, :, 0:SEGW],
                self.x1_d.ap().rearrange("c p t -> p c t")[:, :, ts(t - 1, SEGW)]),
                self.sS, 16)
            self.W(DVE, self.sS)
            for kc in range(NCH):
                self.I(v.tensor_tensor(self.y[:, kc, c0:c0 + SEGW],
                                       self.y[:, kc, c0:c0 + SEGW],
                                       self.x1t[:, kc, 0:SEGW], ALU.add), self.sV)

    def emit_output(self):
        for sem in (self.sV, self.sP, self.sA, self.sL):
            self.W(SP, sem)
        self.I(self.s.dma_start(self.y_out.ap().rearrange("c p t -> p c t"),
                                self.y[:, :, PADW:YW]), self.sS, 16)
        self.W(SP, self.sS)

    def emit(self):
        mk = self.mk
        self.init_loads()
        if self.stage == "m1":
            self.emit_output()
            self.nc.all_engine_barrier()
            return
        self.emit_outer()
        if self.stage == "outer":
            self.emit_output()
            self.nc.all_engine_barrier()
            return
        self.W(SP, self.sV)
        self.W(SP, self.sP)
        self.I(self.s.dma_start(self.x1_d.ap().rearrange("c p t -> p c t"),
                                self.y[:, :, PADW:YW]), self.sS, 16)
        for t in range(1, 5):
            self.emit_ln_seg(t, "ln0")
        self.W(SP, self.sV)
        self.W(SP, self.sP)
        self.I(self.s.dma_start(self.cc_in.ap().rearrange("(c p) t -> p c t", p=128),
                                self.y[:, :, YW - HALO:YW]), self.sE, 16)
        self.W(PL, self.sE)
        self.I(self.g.collective_compute(
            "AllGather", ALU.bypass, replica_groups=[list(range(NCORES))],
            ins=[self.cc_in.ap().opt()], outs=[self.cc_out.ap().opt()]), self.sC)
        self.W(SP, self.sA)
        self.I(self.s.dma_start(self.masks[:],
                                self.mask_in.ap().rearrange("s p n -> p s n")),
               self.sS, 16)
        if self.stage == "ln0":
            self.emit_output()
            self.nc.all_engine_barrier()
            return
        self.I(self.g.dma_start(self.stg[0].ap(),
                                self.hap(self.d_wall, OB0, [[1, 1], [1, LBS]])),
               self.sX, 16)
        for L in range(self.n_layers):
            self.emit_layer_body(L)
        if self.stage == "full":
            self.emit_final()
        self.emit_output()
        self.nc.all_engine_barrier()


def build_nc(stage=STAGE, plan=None):
    nc = bass.Bass("TRN2", target_bir_lowering=False)
    mk = MK(nc, plan)
    b = B(nc, mk, stage)
    b.emit()
    return nc, mk


def _in_maps(inputs):
    shared = _prep_shared(inputs)
    skeys = ("wall", "vecsL", "gateB", "ident", "perm",
             "sstat", "e16", "ones2", "onesR")
    maps = []
    for c in range(NCORES):
        d = _prep_core_inputs(inputs, c)
        m = {k: d[k] for k in ("x_own", "x_halo", "rope", "mask_in",
                               "mask_out", "mask_lat")}
        for k in skeys:
            m[k] = shared[k]
        maps.append(m)
    return maps


def kernel(**inputs):
    maps = _in_maps(inputs)
    nc, mk = build_nc(STAGE, None)
    res = run_bass_kernel_spmd(nc, maps, core_ids=list(range(NCORES)))
    out = np.zeros((2, 8192, D), np.float32)
    for c in range(NCORES):
        seq, sl = divmod(c, 4)
        r = np.asarray(res.results[c]["y_out"])
        out[seq, OWN * sl:OWN * (sl + 1)] = r.reshape(D, OWN).T
    return out
